# version 01
# baseline (speedup 1.0000x reference)
import bisect

ENGINES = ("tensor", "scalar", "vector", "gpsimd", "sync")


class Op:
    __slots__ = ("eng", "fn", "deps", "kind", "need_tok", "val", "name",
                 "n_incs", "pos", "semkey", "idx")

    def __init__(self, eng, fn, deps, kind, name, n_incs=1, semkey=None):
        self.eng = eng
        self.fn = fn
        self.deps = list(deps)
        self.kind = kind
        self.need_tok = False
        self.val = None
        self.name = name
        self.n_incs = n_incs
        self.pos = None
        self.semkey = semkey
        self.idx = None


class Res:
    __slots__ = ("writers", "readers", "name")

    def __init__(self, name=""):
        self.writers = []
        self.readers = []
        self.name = name


class Plan:
    def __init__(self):
        self.ops = []

    def add(self, eng, fn, deps=(), kind="compute", name="",
            reads=(), writes=(), n_incs=1, semkey=None):
        deps = list(deps)
        for r in reads:
            deps += r.writers
        for r in writes:
            deps += r.readers
        seen = set()
        dd = []
        for d in deps:
            if d is None or id(d) in seen:
                continue
            seen.add(id(d))
            dd.append(d)
        if semkey is None and kind in ("dma", "cc"):
            semkey = writes[0].name if writes else name
        op = Op(eng, fn, dd, kind, name, n_incs, semkey)
        op.idx = len(self.ops)
        self.ops.append(op)
        for r in writes:
            r.readers = []
            r.writers.append(op)
        for r in reads:
            r.readers.append(op)
        for d in dd:
            if d.eng != eng or d.kind != "compute":
                d.need_tok = True
        return op

    def resolve(self):
        cnt = {}
        pos = {}
        self.sem_hist = {}
        for op in self.ops:
            pos[op.eng] = pos.get(op.eng, 0) + 1
            op.pos = pos[op.eng]
            if op.kind in ("dma", "cc"):
                op.need_tok = True
            if op.need_tok:
                key = (op.eng, op.kind, op.semkey)
                step = 16 if op.kind == "dma" else 1
                cnt[key] = cnt.get(key, 0) + step * op.n_incs
                op.val = cnt[key]
                h = self.sem_hist.setdefault(key, ([], []))
                h[0].append(op.idx)
                h[1].append(cnt[key])
        self.totals = cnt

    def wait_val(self, dep, consumer_idx):
        if dep.kind == "compute":
            return dep.val
        key = (dep.eng, dep.kind, dep.semkey)
        idxs, vals = self.sem_hist[key]
        i = bisect.bisect_left(idxs, consumer_idx) - 1
        if i < 0:
            return dep.val
        return max(dep.val, vals[i])

    def emit(self, nc, block, es):
        self.resolve()
        sems = {}
        for n, key in enumerate(self.totals):
            sems[key] = es.enter_context(nc.semaphore(f"s{n}_{key[2] or key[0]}"))
        by_eng = {e: [op for op in self.ops if op.eng == e] for e in ENGINES}

        def mk_body(eng_name, ops):
            def body(e):
                hi = {}
                last_drain = 0
                for op in ops:
                    need_drain = False
                    for d in op.deps:
                        if d.eng == eng_name and d.kind == "compute":
                            if d.pos > last_drain:
                                need_drain = True
                            continue
                        assert d.val is not None, (op.name, d.name)
                        key = (d.eng, d.kind, d.semkey)
                        v = self.wait_val(d, op.idx)
                        if hi.get(key, -1) >= v:
                            continue
                        hi[key] = v
                        e.wait_ge(sems[key], v)
                    if need_drain:
                        e.drain()
                        last_drain = op.pos - 1
                    last = op.fn(e)
                    if op.need_tok:
                        assert last is not None, f"{op.name}: token needed"
                        insts = last if isinstance(last, (list, tuple)) else [last]
                        assert len(insts) == op.n_incs, (op.name, len(insts),
                                                         op.n_incs)
                        step = 16 if op.kind == "dma" else 1
                        for ins_ in insts:
                            ins_.then_inc(sems[(op.eng, op.kind, op.semkey)],
                                          step)
            return body

        for e in ENGINES:
            if by_eng[e]:
                getattr(block, e)(mk_body(e, by_eng[e]))
        return sems


"""Megakernel for mk_macaron_d1 (d_model 1024, B=4, S=4096) on 8 NeuronCores.

Core 2b+r owns batch b, role r = even/odd 128-row tiles (2048 tokens).
Residual x feature-major (FM) fp32 in SBUF across all 3 macaron stages.
Pair K/V exchanges via 2-rank AllGather (HBM bounce). Matmuls bf16
(stats/broadcast matmuls f32r), PSUM fp32, elementwise fp32.
"""
import contextlib
import numpy as np
import ml_dtypes
import concourse.bass as bass
import concourse.mybir as mybir
from concourse.bass_utils import run_bass_kernel_spmd

F32 = mybir.dt.float32
F32R = mybir.dt.float32r
BF16 = mybir.dt.bfloat16
AF = mybir.ActivationFunctionType
ALU = mybir.AluOpType
AXX = mybir.AxisListType.X

C = 1024
PC = 8
T = 2048
NT = 16
TOKB = 512
NB = 4
NP = 8
NE = 8
FF = 2048
NF = 16
EPS = 1e-5
EXPB = -12.0

CONFIG = {"stages": "ABC", "taps": []}
BF = ml_dtypes.bfloat16
RG2 = [[0, 1], [2, 3], [4, 5], [6, 7]]


def shard_inputs(ins):
    x = np.ascontiguousarray(np.asarray(ins["L0_attn_x"], np.float32))
    tril = np.tril(np.ones((128, 128), np.float32))
    masks_by_role = []
    for r in (0, 1):
        mm = np.zeros((8, 128, 512), np.float32)
        for d in range(8):
            for a in range(4):
                own = 2 * a + r
                if own > d:
                    mm[d, :, a * 128:(a + 1) * 128] = 1.0
                elif own == d:
                    mm[d, :, a * 128:(a + 1) * 128] = tril.T
        masks_by_role.append(np.ascontiguousarray(mm.astype(BF)))
    jmean = np.full((128, 128), 1.0 / C, np.float32)
    jone = np.ones((128, 128), np.float32)
    onesb = np.ones((1, 128), np.float32).astype(BF)
    ident = np.eye(128, dtype=np.float32)
    sel2 = np.zeros((1, 256), np.float32)
    sel2[0, 0:64] = 1.0
    sel2[0, 192:256] = 1.0
    oh8 = np.zeros((8, 8 * 128), np.float32)
    for e_ in range(8):
        oh8[e_, e_ * 128:(e_ + 1) * 128] = 1.0
    oh8 = oh8.astype(BF)
    ln1w = np.ascontiguousarray(np.asarray(ins["L0_ffn_ln1_w"], np.float32).reshape(PC, 128).T)
    ln2w = np.ascontiguousarray(np.asarray(ins["L0_ffn_ln2_w"], np.float32).reshape(PC, 128).T)
    mnorm = np.ascontiguousarray(np.asarray(ins["L0_attn_memory_norm"], np.float32).reshape(4, 128).T)
    maps = []
    for core in range(8):
        b, r = core // 2, core % 2
        xb = x[b].reshape(32, 128, C)[r::2].reshape(T, C)
        maps.append({
            "x_in": np.ascontiguousarray(xb.T),
            "wattn": np.asarray(ins["L0_ffn_c_attn_w"], np.float32),
            "wproj": np.asarray(ins["L0_ffn_c_proj_w"], np.float32),
            "wrout": np.asarray(ins["L0_ffn_router_w"], np.float32),
            "w1": np.asarray(ins["L0_ffn_w1"], np.float32),
            "w2": np.asarray(ins["L0_ffn_w2"], np.float32),
            "wq": np.asarray(ins["L0_attn_Wq"], np.float32),
            "wk": np.asarray(ins["L0_attn_Wk"], np.float32),
            "wv": np.asarray(ins["L0_attn_Wv"], np.float32),
            "mem": np.asarray(ins["L0_attn_memory"], np.float32),
            "mnorm": mnorm,
            "memw": np.asarray(ins["L0_attn_memory_weight"], np.float32).reshape(1, 1),
            "ln1w": ln1w, "ln2w": ln2w,
            "masks": masks_by_role[r],
            "jmean": jmean, "jone": jone, "onesb": onesb, "ident": ident,
            "sel2": sel2, "oh8": oh8,
        })
    return maps


def gather_output(results):
    out = np.empty((4, 4096, C), np.float32)
    for core in range(8):
        b, r = core // 2, core % 2
        tok = np.ascontiguousarray(np.asarray(results[core]["out_x"]).T)
        out[b].reshape(32, 128, C)[r::2] = tok.reshape(NT, 128, C)
    return out


class Arena:
    def __init__(self, tens, cols):
        self.t = tens
        self.cols = cols
        self.off = 0

    def al(self, ncols_bf16, dt=BF16):
        ncols_bf16 = (ncols_bf16 + 1) // 2 * 2
        assert self.off + ncols_bf16 <= self.cols, (self.off, ncols_bf16)
        v = self.t[:, self.off:self.off + ncols_bf16]
        self.off += ncols_bf16
        return v.bitcast(F32) if dt == F32 else v

    def reset(self):
        self.off = 0


def build(cfg=None):
    cfg = dict(CONFIG, **(cfg or {}))
    stages = cfg["stages"]
    taps = set(cfg["taps"])

    nc = bass.Bass("TRN2", target_bir_lowering=False)
    dr = {}

    def din(name, shape, dt=F32):
        dr[name] = nc.dram_tensor(name, shape, dt, kind="ExternalInput").ap()

    def dout(name, shape, dt=F32):
        dr[name] = nc.dram_tensor(name, shape, dt, kind="ExternalOutput").ap()

    def dint(name, shape, dt=BF16):
        dr[name] = nc.dram_tensor(name, shape, dt).ap()

    din("x_in", [C, T])
    din("wattn", [C, 3 * C]); din("wproj", [C, C]); din("wrout", [C, 8])
    din("w1", [C, NE * FF]); din("w2", [NE * FF, C])
    din("wq", [C, C]); din("wk", [C, 512]); din("wv", [C, 512])
    din("mem", [512, 512]); din("mnorm", [128, 4]); din("memw", [1, 1])
    din("ln1w", [128, PC]); din("ln2w", [128, PC])
    din("masks", [8, 128, 512], BF16)
    din("jmean", [128, 128]); din("jone", [128, 128])
    din("onesb", [1, 128], BF16); din("ident", [128, 128])
    din("sel2", [1, 256]); din("oh8", [8, 8 * 128], BF16)
    dout("out_x", [C, T], F32)

    dint("w1b", [C, NE * FF]); dint("w2b", [NE * FF, C])
    dint("qs", [C, T]); dint("attns", [C, T]); dint("z2s", [C, T])
    for st in [s for s in stages if s in "AC"]:
        dint(f"{st}k_in", [C, T]); dint(f"{st}k_out", [2 * C, T])
        dint(f"{st}v_in", [NP * T, 128]); dint(f"{st}v_out", [2 * NP * T, 128])
    if "B" in stages:
        dint("Bk_in", [512, T]); dint("Bk_out", [1024, T])
        dint("Bv_in", [NP * T, 64]); dint("Bv_out", [2 * NP * T, 64])

    for tn in taps:
        dout("tap_" + tn, [C, T], F32)

    es = contextlib.ExitStack()

    def sb(name, shape, dt):
        return es.enter_context(nc.sbuf_tensor(name, shape, dt))

    with es:
        x = sb("x", [128, PC * T], F32)
        accb = sb("accb", [128, PC * T], F32)
        NSTG = 3
        stgt = sb("stgt", [128, NSTG * 512], BF16)
        tmpv = sb("tmpv", [128, TOKB], F32)
        tmpv2 = sb("tmpv2", [128, TOKB], F32)
        rdbs = sb("rdbs", [128, TOKB], F32)
        dnst = sb("dnst", [1, 1024], F32)
        aux_m = sb("aux_m", [128, TOKB], F32)
        aux_v = sb("aux_v", [128, TOKB], F32)
        aux_r = sb("aux_r", [128, TOKB], F32)
        xsqb = sb("xsqb", [128, 2 * TOKB], BF16)
        xcpb = sb("xcpb", [128, 2 * TOKB], BF16)
        jm = sb("jm", [128, 128], BF16)
        jo = sb("jo", [128, 128], BF16)
        onb = sb("onb", [1, 128], BF16)
        idn = sb("idn", [128, 128], F32)
        sel = sb("sel", [1, 256], BF16)
        selw = sb("selw", [1, 256], BF16)
        onw = sb("onw", [1, 128], BF16)
        oneb = sb("oneb", [128, 1], F32)
        dnsb = sb("dnsb", [1, 4 * 512], BF16)
        l1w = sb("l1w", [128, PC], F32)
        l2w = sb("l2w", [128, PC], F32)
        mnormt = sb("mnormt", [128, 4], F32)
        mnormb = sb("mnormb", [128, 4], BF16)
        epsb = sb("epsb", [128, 1], F32)
        negC = sb("negC", [128, 1], F32)
        wblend = sb("wblend", [1, 1], F32)
        scal = sb("scal", [2, 8], F32)
        r8 = sb("r8", [128, 8], F32)
        r8b = sb("r8b", [128, 8], F32)
        r1a = sb("r1a", [128, 1], F32)
        r1b = sb("r1b", [128, 1], F32)
        rwt = sb("rwt", [128, 8], F32)
        wbs = sb("wbs", [128, TOKB], BF16)
        rwT = sb("rwT", [8, T], BF16)

        ARN = (nc.sbuf_bytes_remaining - 64) // 2 // 2 * 2
        arn_t = sb("arn", [128, ARN], BF16)
        arena = Arena(arn_t, ARN)

        PPs = [es.enter_context(nc.psum_tensor(f"PP{i}", [128, 1024], F32))
               for i in range(4)]
        PP0, PP1, PP2, PP3 = PPs

        block = es.enter_context(nc.Block())
        P = Plan()

        R_x = [Res(f"x{c}") for c in range(PC)]
        R_acc = [Res(f"acc{c}") for c in range(PC)]
        R_ps = [Res(f"ps{i}") for i in range(8)]
        R_stg = [Res(f"stg{i}") for i in range(NSTG)]
        R_aux = Res("aux")
        R_xsq = [Res("xsq0"), Res("xsq1")]
        R_dns = Res("dns")
        R_rdbs = Res("rdbs")
        R_tmp = Res("tmp")
        R_tmp2 = Res("tmp2")
        R_qs = Res("qs")
        R_z2s = Res("z2s")
        R_attns = Res("attns")
        R_w1b = Res("w1b")
        R_w2b = Res("w2b")
        R_wbs = Res("wbs")
        R_rconst = Res("consts")
        R_scal = Res("scal")
        R_r8 = Res("r8")
        R_rwT = Res("rwT")
        R_r1 = Res("r1")
        R_rwt = Res("rwt")

        def pslot(i):
            return PPs[i // 2], (i % 2) * 512, R_ps[i]

        def bar_deps():
            last = {}
            for op in P.ops:
                last[op.eng] = op
            return list(last.values())

        def new_res(name, bar):
            r = Res(name)
            r.readers = list(bar)
            return r

        def gran_w(w_ap, o128, cchunks=PC):
            return w_ap.rearrange("(c p) o -> p c o", p=128)[
                :, :, o128 * 128:(o128 + 1) * 128]

        def c3(sb_ap, o=128):
            return sb_ap.rearrange("p (c o) -> p c o", o=o)

        def f(e):
            return [
                e.dma_start(out=jm[:, :], in_=dr["jmean"][:, :]),
                e.dma_start(out=jo[:, :], in_=dr["jone"][:, :]),
                e.dma_start(out=onb[:, :], in_=dr["onesb"][:, :]),
                e.dma_start(out=idn[:, :], in_=dr["ident"][:, :]),
                e.dma_start(out=sel[:, :], in_=dr["sel2"][:, :]),
                e.dma_start(out=l1w[:, :], in_=dr["ln1w"][:, :]),
                e.dma_start(out=l2w[:, :], in_=dr["ln2w"][:, :]),
                e.dma_start(out=mnormt[:, :], in_=dr["mnorm"][:, :]),
                e.dma_start(out=wblend[:, :], in_=dr["memw"][:, :]),
            ]
        op_consts = P.add("gpsimd", f, kind="dma", name="consts",
                          writes=(R_rconst,), n_incs=9)

        def f(e):
            insts = []
            for c in range(PC):
                insts.append(e.dma_start(
                    out=x[:, c * T:(c + 1) * T],
                    in_=dr["x_in"][c * 128:(c + 1) * 128, :]))
            return insts
        P.add("sync", f, kind="dma", name="xload", writes=tuple(R_x), n_incs=PC)

        def f(e):
            e.memset(epsb[:, :], EPS)
            e.memset(oneb[:, :], 1.0)
            e.memset(negC[:, :], EXPB)
            e.tensor_copy(mnormb[:, :], mnormt[:, :])
            return e.tensor_scalar_mul(scal[0:1, 0:1], wblend[0:1, 0:1], -1.0)
        op_i1 = P.add("vector", f, deps=[op_consts], name="init_v1",
                      writes=(R_scal,))
        R_rconst.writers.append(op_i1)

        def f(e):
            return e.activation(scal[0:1, 1:2], scal[0:1, 0:1], AF.Exp)
        op_i2 = P.add("scalar", f, deps=[op_i1], name="init_s1",
                      reads=(R_scal,))
        R_scal.writers.append(op_i2)

        def f(e):
            e.tensor_scalar_add(scal[0:1, 1:2], scal[0:1, 1:2], 1.0)
            e.drain()
            e.reciprocal(scal[0:1, 3:4], scal[0:1, 1:2])
            e.drain()
            e.tensor_scalar_mul(scal[0:1, 2:3], scal[0:1, 3:4], -1.0)
            e.drain()
            return e.tensor_scalar_add(scal[0:1, 2:3], scal[0:1, 2:3], 1.0)
        op_i3 = P.add("vector", f, deps=[op_i2], name="init_v2",
                      reads=(R_scal,), writes=(R_scal,))

        def f(e):
            return e.dma_start(out=scal[1:2, 2:4], in_=scal[0:1, 2:4])
        op_i4 = P.add("gpsimd", f, deps=[op_i3], kind="dma", name="init_rep",
                      reads=(R_scal,))
        R_scal.writers.append(op_i4)

        def f(e):
            e.tensor_scalar(selw[:, :], sel[:, :], scal[0:1, 2:3], None, ALU.mult)
            return e.tensor_scalar(onw[:, :], jo[0:1, :], scal[0:1, 3:4],
                                   None, ALU.mult)
        op_selw = P.add("vector", f, deps=[op_i4], name="init_v3",
                        reads=(R_scal, R_rconst), writes=(R_scal,))

        def emit_ln(src_t, R_src, gain_col, out32_t, R_o32, outbf_t, R_obf, tag):
            for bq in range(NB):
                ppm, offm, Rm = pslot(0)
                _, offs, Rs = pslot(1)
                ps_m = ppm[:, offm:offm + 512]
                ps_ss = ppm[:, 512:1024]
                last_stat = None
                for c in range(PC):
                    sl = slice(c * T + bq * TOKB, c * T + (bq + 1) * TOKB)
                    xq = xsqb[:, (c % 2) * 512:(c % 2) * 512 + 512]
                    xc = xcpb[:, (c % 2) * 512:(c % 2) * 512 + 512]

                    def f(e, sl=sl, xq=xq, xc=xc):
                        e.activation(xc, src_t[:, sl], AF.Copy)
                        return e.activation(xq, src_t[:, sl], AF.Square)
                    sqop = P.add("scalar", f, name=f"{tag}sq{bq}_{c}",
                                 reads=(R_src[c],), writes=(R_xsq[c % 2],))

                    def f(e, xq=xq, xc=xc, c=c):
                        e.matmul(ps_m, jm[:, :], xc,
                                 start=(c == 0), stop=(c == PC - 1))
                        return e.matmul(ps_ss, jo[:, :], xq,
                                        start=(c == 0), stop=(c == PC - 1))
                    stat = P.add("tensor", f, deps=[sqop, op_consts],
                                 name=f"{tag}st{bq}_{c}",
                                 reads=(R_xsq[c % 2], R_src[c]),
                                 writes=(Rm, Rs) if c == 0 else ())
                    if c > 0:
                        Rm.writers.append(stat)
                        Rs.writers.append(stat)
                    last_stat = stat

                def f(e):
                    e.tensor_copy(aux_m[:, :], ps_m)
                    e.tensor_scalar_mul(aux_v[:, :], ps_ss, 1.0 / C)
                    e.drain()
                    e.tensor_tensor(tmpv[:, :], aux_m[:, :], aux_m[:, :], ALU.mult)
                    e.drain()
                    return e.tensor_tensor(aux_v[:, :], aux_v[:, :], tmpv[:, :],
                                           ALU.subtract)
                vop = P.add("vector", f, deps=[last_stat], name=f"{tag}var{bq}",
                            reads=(Rm, Rs), writes=(R_aux, R_tmp))

                def f(e):
                    e.activation(aux_v[:, :], aux_v[:, :], AF.Ln, bias=epsb[:, 0:1])
                    e.drain()
                    return e.activation(aux_r[:, :], aux_v[:, :], AF.Exp, scale=-0.5)
                rop = P.add("scalar", f, deps=[vop], name=f"{tag}r{bq}",
                            reads=(R_aux, R_rconst))
                R_aux.writers.append(rop)

                def f(e, bq=bq):
                    last = None
                    for c0 in range(0, PC, 2):
                        for c in (c0, c0 + 1):
                            sl = slice(c * T + bq * TOKB, c * T + (bq + 1) * TOKB)
                            tv = tmpv if c == c0 else tmpv2
                            e.tensor_tensor(tv[:, :], src_t[:, sl], aux_m[:, :],
                                            ALU.subtract)
                        e.drain()
                        for c in (c0, c0 + 1):
                            sl = slice(c * T + bq * TOKB, c * T + (bq + 1) * TOKB)
                            tv = tmpv if c == c0 else tmpv2
                            if out32_t is not None:
                                last = e.tensor_tensor(out32_t[:, sl], tv[:, :],
                                                       aux_r[:, :], ALU.mult)
                            if outbf_t is not None:
                                if gain_col is not None:
                                    last = e.scalar_tensor_tensor(
                                        outbf_t[:, sl], tv[:, :],
                                        gain_col[:, c:c + 1], aux_r[:, :],
                                        ALU.mult, ALU.mult)
                                else:
                                    last = e.tensor_tensor(outbf_t[:, sl],
                                                           tv[:, :],
                                                           aux_r[:, :], ALU.mult)
                        if c0 + 2 < PC:
                            e.drain()
                    return last
                wr = tuple(R_o32[c] for c in range(PC)) if R_o32 else ()
                wr += tuple(R_obf[c] for c in range(PC)) if R_obf else ()
                P.add("vector", f, deps=[rop], name=f"{tag}nrm{bq}",
                      reads=tuple(R_src) + (R_aux, R_rconst),
                      writes=wr + (R_tmp, R_tmp2))

        def emit_tap(name, src_t, R_list):
            if name not in taps:
                return

            def f(e):
                insts = []
                for c in range(PC):
                    insts.append(e.dma_start(
                        out=dr["tap_" + name][c * 128:(c + 1) * 128, :],
                        in_=src_t[:, c * T:(c + 1) * T]))
                return insts
            P.add("sync", f, kind="dma", name="tap_" + name,
                  reads=tuple(R_list), n_incs=PC, semkey="tap")

        def emit_qkv(st, z_t, R_z, wring, R_wring, wvw_t, R_wvw):
            kin = dr[f"{st}k_in"]
            vin = dr[f"{st}v_in"]
            NSL = len(R_wring)
            k_dma, v_dma, q_dma = [], [], []
            if st in "AC":
                grans = [("wattn", o, o) for o in list(range(8, 16)) + list(range(8))]
                vsrc = ("wattn", 2048)
                nvh = 2
            else:
                grans = [("wk", o, 8 + o) for o in range(4)] + \
                        [("wq", o, o) for o in range(8)]
                vsrc = ("wv", 0)
                nvh = 1
            ip = 0
            for gi, (wn, o, dsto) in enumerate(grans):
                slot = gi % NSL

                def f(e, wn=wn, o=o, slot=slot):
                    return e.dma_start(
                        out=c3(wring[:, slot * 1024:(slot + 1) * 1024]),
                        in_=gran_w(dr[wn], o))
                gop = P.add("gpsimd", f, kind="dma", name=f"{st}qW{gi}",
                            writes=(R_wring[slot],))
                for bq in range(NB):
                    pp, off, Rp = pslot(ip % 8)

                    def f(e, slot=slot, bq=bq, pp=pp, off=off):
                        last = None
                        for c in range(PC):
                            last = e.matmul(
                                pp[:, off:off + 512],
                                wring[:, slot * 1024 + c * 128: slot * 1024 + (c + 1) * 128],
                                z_t[:, c * T + bq * TOKB: c * T + (bq + 1) * TOKB],
                                start=(c == 0), stop=(c == PC - 1))
                        return last
                    P.add("tensor", f, deps=[gop], name=f"{st}qM{gi}_{bq}",
                          reads=(R_wring[slot],) + tuple(R_z), writes=(Rp,))

                    def f(e, ip=ip, pp=pp, off=off):
                        return e.activation(stgt[:, (ip % NSTG) * 512:(ip % NSTG) * 512 + 512],
                                            pp[:, off:off + 512], AF.Copy)
                    P.add("scalar", f, name=f"{st}qE{gi}_{bq}",
                          reads=(Rp,), writes=(R_stg[ip % NSTG],))

                    if dsto >= 8:
                        dst = kin[(dsto - 8) * 128:(dsto - 7) * 128,
                                  bq * TOKB:(bq + 1) * TOKB]
                        wres = ()
                    else:
                        dst = dr["qs"][dsto * 128:(dsto + 1) * 128,
                                       bq * TOKB:(bq + 1) * TOKB]
                        wres = (R_qs,)

                    def f(e, dst=dst, ip=ip):
                        return e.dma_start(out=dst,
                                           in_=stgt[:, (ip % NSTG) * 512:(ip % NSTG) * 512 + 512])
                    dmop = P.add("sync", f, kind="dma", name=f"{st}qD{gi}_{bq}",
                                 reads=(R_stg[ip % NSTG],), writes=wres,
                                 semkey=("qs" if wres else "kin"))
                    (k_dma if dsto >= 8 else q_dma).append(dmop)
                    ip += 1

            def f(e):
                return e.collective_compute(
                    "AllGather", ALU.bypass, replica_groups=RG2,
                    ins=[kin.opt()], outs=[dr[f"{st}k_out"].opt()])
            cc_k = P.add("gpsimd", f, deps=k_dma, kind="cc", name=f"{st}cck", semkey="cck")

            for vh in range(nvh):
                def f(e, vh=vh):
                    wn, base = vsrc
                    src = dr[wn].rearrange("(c p) o -> p c o", p=128)[
                        :, :, base + vh * 512: base + (vh + 1) * 512]
                    return e.dma_start(out=c3(wvw_t[:, :], 512), in_=src)
                wvop = P.add("gpsimd", f, kind="dma", name=f"{st}wv{vh}",
                             writes=(R_wvw,))
                for tc in range(NT):
                    pp, off, Rp = pslot(ip % 8)

                    def f(e, tc=tc, pp=pp, off=off):
                        last = None
                        for c in range(PC):
                            last = e.matmul(
                                pp[:, off:off + 512],
                                z_t[:, c * T + tc * 128: c * T + (tc + 1) * 128],
                                wvw_t[:, c * 512:(c + 1) * 512],
                                start=(c == 0), stop=(c == PC - 1))
                        return last
                    P.add("tensor", f, deps=[wvop], name=f"{st}vM{vh}_{tc}",
                          reads=(R_wvw,) + tuple(R_z), writes=(Rp,))

                    def f(e, ip=ip, pp=pp, off=off):
                        return e.activation(stgt[:, (ip % NSTG) * 512:(ip % NSTG) * 512 + 512],
                                            pp[:, off:off + 512], AF.Copy)
                    P.add("scalar", f, name=f"{st}vE{vh}_{tc}",
                          reads=(Rp,), writes=(R_stg[ip % NSTG],))

                    if st in "AC":
                        dst = vin.rearrange("(p t) d -> t p d", p=NP)[
                            tc * 128:(tc + 1) * 128, 4 * vh:4 * vh + 4, :]
                        npd = 4
                    else:
                        dst = vin.rearrange("(p t) d -> t p d", p=NP)[
                            tc * 128:(tc + 1) * 128, :, :]
                        npd = 8

                    def f(e, dst=dst, ip=ip, npd=npd):
                        src = stgt[:, (ip % NSTG) * 512:(ip % NSTG) * 512 + 512
                                   ].rearrange("t (p d) -> t p d", p=npd)
                        return e.dma_start(out=dst, in_=src)
                    v_dma.append(P.add("sync", f, kind="dma",
                                       name=f"{st}vD{vh}_{tc}",
                                       reads=(R_stg[ip % NSTG],),
                                       semkey="vin"))
                    ip += 1

            def f(e):
                return e.collective_compute(
                    "AllGather", ALU.bypass, replica_groups=RG2,
                    ins=[vin.opt()], outs=[dr[f"{st}v_out"].opt()])
            cc_v = P.add("gpsimd", f, deps=v_dma, kind="cc", name=f"{st}ccv", semkey="ccv")
            return cc_k, cc_v, q_dma

        def chunks_for(j, causal):
            out = []
            for g in range(32):
                s, tt = divmod(g, 16)
                seq = 2 * tt + s
                if seq < 8 * j:
                    out.append((g, None))
                elif seq < 8 * j + 8:
                    out.append((g, seq - 8 * j))
                elif not causal:
                    out.append((g, None))
            return out

        def emit_mha(st, cc_k, cc_v, q_ops, kvk, R_kvk, vt, R_vt, qp, R_qp,
                     prb, R_prb, mskb, R_mskb):
            kout = dr[f"{st}k_out"]
            vout = dr[f"{st}v_out"]
            attn_dma = []

            def f(e):
                return e.dma_start(out=c3(mskb[:, :], 512),
                                   in_=dr["masks"].rearrange("d k q -> k d q"))
            mop = P.add("sync", f, kind="dma", name=f"{st}mld", writes=(R_mskb,))

            def f(e):
                return e.memset(vt[:, :], 1.0)
            vtones = P.add("vector", f, name=f"{st}vt1", writes=(R_vt,))

            Rav = (R_ps[4], R_ps[5])
            for p in range(NP):
                def f(e, p=p):
                    return [e.dma_start(
                        out=kvk[:, s * 2048:(s + 1) * 2048],
                        in_=kout[s * C + p * 128: s * C + (p + 1) * 128, :])
                        for s in range(2)]
                kld = P.add("sync", f, deps=[cc_k], kind="dma",
                            name=f"{st}kld{p}", writes=(R_kvk,), n_incs=2)

                def f(e, p=p):
                    insts = []
                    for s in range(2):
                        for h in range(2):
                            base = s * 16 * 130 + h * 65
                            dst = vt[:, base:base + 15 * 130 + 65].rearrange(
                                "k (g w) -> k g w", w=65)[:, 0::2, :][:, :, 0:64]
                            src = vout[s * NP * T + p * T: s * NP * T + (p + 1) * T,
                                       h * 64:(h + 1) * 64
                                       ].rearrange("(g k) d -> k g d", k=128)
                            insts.append(e.dma_start(out=dst, in_=src))
                    return insts
                vld = P.add("sync", f, deps=[cc_v, vtones], kind="dma",
                            name=f"{st}vld{p}", writes=(R_vt,), n_incs=4)

                def f(e, p=p):
                    return e.dma_start(out=qp[:, :],
                                       in_=dr["qs"][p * 128:(p + 1) * 128, :])
                qld = P.add("sync", f, deps=q_ops, kind="dma",
                            name=f"{st}qld{p}", reads=(R_qs,), writes=(R_qp,))

                for j in range(NB):
                    sched = chunks_for(j, True)
                    nch = len(sched)
                    for ci, (g, d) in enumerate(sched):
                        pp = PPs[ci % 2]
                        Rpa, Rpb = R_ps[(ci % 2) * 2], R_ps[(ci % 2) * 2 + 1]

                        def f(e, g=g, j=j, pp=pp):
                            e.matmul(pp[:, 0:512], kvk[0:64, g * 128:(g + 1) * 128],
                                     qp[0:64, j * TOKB:(j + 1) * TOKB],
                                     start=True, stop=True, tile_position=(0, 0))
                            return e.matmul(
                                pp[:, 512:1024], kvk[64:128, g * 128:(g + 1) * 128],
                                qp[64:128, j * TOKB:(j + 1) * TOKB],
                                start=True, stop=True, tile_position=(64, 0))
                        scop = P.add("tensor", f, deps=[kld, qld],
                                     name=f"{st}sc{p}_{j}_{ci}",
                                     reads=(R_kvk, R_qp), writes=(Rpa, Rpb))

                        prs = ci % 2
                        R_pr = R_prb[prs]

                        def f(e, pp=pp, prs=prs):
                            return e.activation(
                                prb[:, prs * 1024:(prs + 1) * 1024],
                                pp[:, 0:1024], AF.Exp, bias=negC[:, 0:1],
                                scale=0.125)
                        exop = P.add("scalar", f, deps=[scop],
                                     name=f"{st}ex{p}_{j}_{ci}",
                                     reads=(Rpa, Rpb, R_rconst), writes=(R_pr,))

                        if d is not None:
                            def f(e, prs=prs, d=d):
                                pr = prb[:, prs * 1024:(prs + 1) * 1024]
                                mk = mskb[:, d * 512:(d + 1) * 512]
                                e.tensor_tensor(pr[:, 0:512], pr[:, 0:512], mk,
                                                ALU.mult)
                                return e.tensor_tensor(pr[:, 512:1024],
                                                       pr[:, 512:1024], mk,
                                                       ALU.mult)
                            P.add("vector", f, deps=[exop, mop],
                                  name=f"{st}mk{p}_{j}_{ci}",
                                  reads=(R_pr, R_mskb), writes=(R_pr,))

                        gb = (g // 16) * 16 * 130 + (g % 16) * 130

                        def f(e, gb=gb, prs=prs, first=(ci == 0),
                              last=(ci == nch - 1)):
                            pr = prb[:, prs * 1024:(prs + 1) * 1024]
                            e.matmul(PP2[0:65, 0:512], vt[:, gb:gb + 65],
                                     pr[:, 0:512], start=first, stop=last)
                            return e.matmul(PP2[0:65, 512:1024],
                                            vt[:, gb + 65:gb + 130],
                                            pr[:, 512:1024], start=first,
                                            stop=last)
                        avop = P.add("tensor", f, deps=[vld],
                                     name=f"{st}av{p}_{j}_{ci}",
                                     reads=(R_pr, R_vt),
                                     writes=Rav if ci == 0 else ())
                        if ci > 0:
                            for rr in Rav:
                                rr.writers.append(avop)
                        last_av = avop

                    def f(e):
                        e.tensor_copy(dnst[0:1, 0:512], PP2[64:65, 0:512])
                        e.tensor_copy(dnst[0:1, 512:1024], PP2[64:65, 512:1024])
                        e.drain()
                        e.reciprocal(dnst[:, :], dnst[:, :])
                        e.drain()
                        e.tensor_copy(dnsb[:, 0:1024], dnst[:, :])
                        e.drain()
                        return e.tensor_tensor(dnsb[:, 1024:2048], dnst[:, :],
                                               dnsb[:, 0:1024], ALU.subtract)
                    dnop = P.add("vector", f, deps=[last_av],
                                 name=f"{st}dn{p}_{j}", reads=Rav,
                                 writes=(R_dns,))

                    pR, oR, RR = pslot(6)

                    def f(e, pR=pR, oR=oR):
                        e.matmul(pR[:, oR:oR + 512], sel[0:1, 0:128],
                                 dnsb[:, 0:512], start=True, stop=False)
                        e.matmul(pR[:, oR:oR + 512], sel[0:1, 128:256],
                                 dnsb[:, 512:1024], start=False, stop=False)
                        e.matmul(pR[:, oR:oR + 512], sel[0:1, 0:128],
                                 dnsb[:, 1024:1536], start=False, stop=False)
                        return e.matmul(pR[:, oR:oR + 512], sel[0:1, 128:256],
                                        dnsb[:, 1536:2048], start=False,
                                        stop=True)
                    rdop = P.add("tensor", f, deps=[dnop, op_consts],
                                 name=f"{st}rd{p}_{j}", reads=(R_dns,),
                                 writes=(RR,))

                    def f(e, pR=pR, oR=oR):
                        return e.activation(rdbs[:, :], pR[:, oR:oR + 512],
                                            AF.Copy)
                    rcop = P.add("scalar", f, name=f"{st}rc{p}_{j}",
                                 reads=(RR,), writes=(R_rdbs,))

                    sslot = (p * NB + j) % NSTG

                    def f(e, sslot=sslot):
                        sv = stgt[:, sslot * 512:(sslot + 1) * 512]
                        e.tensor_tensor(sv[0:64, :], PP2[0:64, 0:512],
                                        rdbs[0:64, :], ALU.mult)
                        return e.tensor_tensor(sv[64:128, :],
                                               PP2[0:64, 512:1024],
                                               rdbs[64:128, :], ALU.mult)
                    oop = P.add("vector", f, deps=[rcop], name=f"{st}ao{p}_{j}",
                                reads=Rav + (R_rdbs,), writes=(R_stg[sslot],))

                    def f(e, p=p, j=j, sslot=sslot):
                        return e.dma_start(
                            out=dr["attns"][p * 128:(p + 1) * 128,
                                            j * TOKB:(j + 1) * TOKB],
                            in_=stgt[:, sslot * 512:(sslot + 1) * 512])
                    attn_dma.append(P.add("sync", f, kind="dma",
                                          name=f"{st}aod{p}_{j}",
                                          reads=(R_stg[sslot],),
                                          writes=(R_attns,)))
            return attn_dma

        def emit_proj(st, attst, R_attst, pjw, R_pjw):
            def f(e):
                return [e.dma_start(out=c3(pjw[:, o * 1024:(o + 1) * 1024]),
                                    in_=gran_w(dr["wproj"], o))
                        for o in range(PC)]
            wop = P.add("gpsimd", f, kind="dma", name=f"{st}pjw",
                        writes=(R_pjw,), n_incs=PC)
            for bq in range(NB):
                def f(e, bq=bq):
                    return [e.dma_start(
                        out=attst[:, c * 512:(c + 1) * 512],
                        in_=dr["attns"][c * 128:(c + 1) * 128,
                                        bq * TOKB:(bq + 1) * TOKB])
                        for c in range(PC)]
                lop = P.add("sync", f, kind="dma", name=f"{st}pjl{bq}",
                            reads=(R_attns,), writes=(R_attst,), n_incs=PC)
                for o in range(PC):
                    pp, off, Rp = pslot((bq * PC + o) % 8)

                    def f(e, o=o, pp=pp, off=off):
                        last = None
                        for c in range(PC):
                            last = e.matmul(
                                pp[:, off:off + 512],
                                pjw[:, o * 1024 + c * 128: o * 1024 + (c + 1) * 128],
                                attst[:, c * 512:(c + 1) * 512],
                                start=(c == 0), stop=(c == PC - 1))
                        return last
                    mmop = P.add("tensor", f, deps=[wop, lop],
                                 name=f"{st}pjM{bq}_{o}",
                                 reads=(R_pjw, R_attst), writes=(Rp,))

                    def f(e, o=o, bq=bq, pp=pp, off=off):
                        sl = slice(o * T + bq * TOKB, o * T + (bq + 1) * TOKB)
                        return e.tensor_tensor(accb[:, sl], accb[:, sl],
                                               pp[:, off:off + 512], ALU.add)
                    P.add("vector", f, name=f"{st}pjA{bq}_{o}",
                          reads=(Rp, R_acc[o]), writes=(R_acc[o],))

        def emit_fold(dst_t, R_dst, src_t, R_src, scale, tag):
            for c in range(PC):
                def f(e, c=c):
                    last = None
                    for bq in range(NB):
                        sl = slice(c * T + bq * TOKB, c * T + (bq + 1) * TOKB)
                        last = e.scalar_tensor_tensor(dst_t[:, sl], src_t[:, sl],
                                                      scale, dst_t[:, sl],
                                                      ALU.mult, ALU.add)
                    return last
                P.add("vector", f, name=f"{tag}{c}",
                      reads=(R_src[c], R_dst[c]), writes=(R_dst[c],))

        def emit_router(st, z_t, R_z, rww, R_rww, rwwf, hilo, R_hilo):
            def f(e):
                return e.dma_start(
                    out=c3(rwwf[:, :], 8),
                    in_=dr["wrout"].rearrange("(c p) o -> p c o", p=128))
            wop = P.add("gpsimd", f, kind="dma", name=f"{st}rww",
                        writes=(R_rww,))

            def f(e):
                last = None
                for c in range(PC):
                    last = e.tensor_scalar(rww[:, c * 8:(c + 1) * 8],
                                           rwwf[:, c * 8:(c + 1) * 8],
                                           l2w[:, c:c + 1], None, ALU.mult)
                return last
            wop = P.add("vector", f, deps=[wop, op_consts], name=f"{st}rwf",
                        reads=(R_rww, R_rconst), writes=(R_rww,))
            for tc in range(NT):
                pp, off, Rp = pslot(tc % 4)

                def f(e, tc=tc):
                    last = None
                    for c in range(PC):
                        sl = slice(c * T + tc * 128, c * T + (tc + 1) * 128)
                        e.activation(xcpb[:, 0:128], z_t[:, sl], AF.Copy)
                        last = e.tensor_tensor(xcpb[:, 128:256], z_t[:, sl],
                                               xcpb[:, 0:128], ALU.subtract)
                    return last
                def fmm(e, tc=tc, pp=pp, off=off):
                    last = None
                    for c in range(PC):
                        last = e.matmul(
                            pp[:, off:off + 8],
                            hilo[:, c * 256:c * 256 + 128],
                            rww[:, c * 8:(c + 1) * 8],
                            start=(c == 0), stop=False)
                        last = e.matmul(
                            pp[:, off:off + 8],
                            hilo[:, c * 256 + 128:c * 256 + 256],
                            rww[:, c * 8:(c + 1) * 8],
                            start=False, stop=(c == PC - 1))
                    return last

                def fhl(e, tc=tc):
                    last = None
                    for c in range(PC):
                        sl = slice(c * T + tc * 128, c * T + (tc + 1) * 128)
                        e.activation(hilo[:, c * 256:c * 256 + 128], z_t[:, sl],
                                     AF.Copy)
                        last = e.activation(hilo[:, c * 256 + 128:c * 256 + 256],
                                            z_t[:, sl], AF.Copy, scale=-1.0,
                                            bias=None) if False else last
                    return last
                def fhi(e, tc=tc):
                    last = None
                    for c in range(PC):
                        sl = slice(c * T + tc * 128, c * T + (tc + 1) * 128)
                        last = e.activation(hilo[:, c * 256:c * 256 + 128],
                                            z_t[:, sl], AF.Copy)
                    return last
                hiop = P.add("scalar", fhi, name=f"{st}rHi{tc}",
                             reads=tuple(R_z), writes=(R_hilo,))

                def flo(e, tc=tc):
                    last = None
                    for c in range(PC):
                        sl = slice(c * T + tc * 128, c * T + (tc + 1) * 128)
                        last = e.tensor_tensor(
                            hilo[:, c * 256 + 128:c * 256 + 256], z_t[:, sl],
                            hilo[:, c * 256:c * 256 + 128], ALU.subtract)
                    return last
                loop_ = P.add("vector", flo, deps=[hiop], name=f"{st}rLo{tc}",
                              reads=tuple(R_z) + (R_hilo,), writes=(R_hilo,))

                mmop = P.add("tensor", fmm, deps=[wop, loop_],
                             name=f"{st}rM{tc}",
                             reads=(R_rww, R_hilo), writes=(Rp,))

                def f(e, pp=pp, off=off):
                    e.tensor_copy(r8[:, :], pp[:, off:off + 8])
                    e.drain()
                    return e.reduce_max(r1a[:, :], r8[:, :], AXX, negate=True)
                v1 = P.add("vector", f, deps=[mmop], name=f"{st}rV1{tc}",
                           reads=(Rp,), writes=(R_r8, R_r1))

                def f(e):
                    return e.activation(r8[:, :], r8[:, :], AF.Exp,
                                        bias=r1a[:, 0:1])
                s1 = P.add("scalar", f, deps=[v1], name=f"{st}rS{tc}",
                           reads=(R_r8, R_r1))
                R_r8.writers.append(s1)

                def f(e):
                    e.reduce_max(r1a[:, :], r8[:, :], AXX)
                    e.drain()
                    e.tensor_scalar(r8b[:, :], r8[:, :], r1a[:, 0:1], None,
                                    ALU.is_lt)
                    e.drain()
                    e.tensor_tensor(r8b[:, :], r8b[:, :], r8[:, :], ALU.mult)
                    e.drain()
                    e.reduce_max(r1b[:, :], r8b[:, :], AXX)
                    e.drain()
                    e.tensor_scalar(r8b[:, :], r8[:, :], r1b[:, 0:1], None,
                                    ALU.is_ge)
                    e.drain()
                    e.tensor_tensor(r8b[:, :], r8b[:, :], r8[:, :], ALU.mult)
                    e.drain()
                    e.reduce_sum(r1b[:, :], r8b[:, :], AXX)
                    e.drain()
                    e.reciprocal(r1b[:, :], r1b[:, :])
                    e.drain()
                    return e.tensor_scalar(rwt[:, :], r8b[:, :], r1b[:, 0:1],
                                           0.5, ALU.mult, ALU.mult)
                v2 = P.add("vector", f, deps=[s1], name=f"{st}rV2{tc}",
                           reads=(R_r8,), writes=(R_r8, R_rwt))

                pT, oT, RT = pslot(5)

                def f(e, pT=pT, oT=oT):
                    return e.transpose(pT[0:8, oT:oT + 128], rwt[:, :], idn[:, :])
                top = P.add("tensor", f, deps=[v2, op_consts],
                            name=f"{st}rT{tc}", reads=(R_rconst, R_rwt),
                            writes=(RT,))

                def f(e, tc=tc, pT=pT, oT=oT):
                    return e.tensor_copy(rwT[0:8, tc * 128:(tc + 1) * 128],
                                         pT[0:8, oT:oT + 128])
                P.add("vector", f, deps=[top], name=f"{st}rC{tc}",
                      reads=(RT,), writes=(R_rwT,))

        def emit_moe(st, z2t, R_z2t, w1r, R_w1r, w2r, R_w2r, hst, R_hs,
                     oh8, R_oh8, first_cast):
            def f(e):
                return e.dma_start(out=oh8[0:8, :], in_=dr["oh8"][:, :])
            ohop = P.add("sync", f, kind="dma", name=f"{st}oh8",
                         writes=(R_oh8,))
            N1 = len(R_w1r)
            N2 = len(R_w2r)
            i1 = 0
            i2 = 0
            for bq in range(NB):
                def f(e, bq=bq):
                    return [e.dma_start(
                        out=z2t[:, c * 512:(c + 1) * 512],
                        in_=dr["z2s"][c * 128:(c + 1) * 128,
                                      bq * TOKB:(bq + 1) * TOKB])
                        for c in range(PC)]
                zld = P.add("sync", f, kind="dma", name=f"{st}mZ{bq}",
                            reads=(R_z2s,), writes=(R_z2t,), n_incs=PC)
                for ei in range(NE):
                    pW, oW, RW = pslot(6)

                    def f(e, ei=ei, bq=bq, pW=pW, oW=oW):
                        return e.matmul(pW[:, oW:oW + 512],
                                        oh8[0:8, ei * 128:(ei + 1) * 128],
                                        rwT[0:8, bq * TOKB:(bq + 1) * TOKB],
                                        start=True, stop=True)
                    wbop = P.add("tensor", f, deps=[ohop],
                                 name=f"{st}wb{bq}_{ei}",
                                 reads=(R_rwT, R_oh8), writes=(RW,))

                    def f(e, pW=pW, oW=oW):
                        return e.activation(wbs[:, :], pW[:, oW:oW + 512],
                                            AF.Copy)
                    wbsop = P.add("scalar", f, name=f"{st}wbs{bq}_{ei}",
                                  reads=(RW,), writes=(R_wbs,))

                    h_ops = []
                    for fi in range(NF):
                        slot = i1 % N1
                        cast = first_cast and bq == 0
                        srcw = dr["w1" if cast else "w1b"]
                        srcap = srcw.rearrange("(c p) o -> p c o", p=128)[
                            :, :, ei * FF + fi * 128: ei * FF + (fi + 1) * 128]

                        def f(e, srcap=srcap, slot=slot):
                            return e.dma_start(
                                out=c3(w1r[:, slot * 1024:(slot + 1) * 1024]),
                                in_=srcap)
                        w1l = P.add("gpsimd" if cast else "sync", f, kind="dma",
                                    name=f"{st}w1L{bq}_{ei}_{fi}",
                                    reads=() if cast else (R_w1b,),
                                    writes=(R_w1r[slot],))
                        if cast:
                            dstap = dr["w1b"].rearrange("(c p) o -> p c o", p=128)[
                                :, :, ei * FF + fi * 128: ei * FF + (fi + 1) * 128]

                            def f(e, dstap=dstap, slot=slot):
                                return e.dma_start(
                                    out=dstap,
                                    in_=c3(w1r[:, slot * 1024:(slot + 1) * 1024]))
                            P.add("sync", f, deps=[w1l], kind="dma",
                                  name=f"{st}w1B{ei}_{fi}",
                                  reads=(R_w1r[slot],), writes=(R_w1b,))

                        pH, oH, RH = pslot(fi % 2)

                        def f(e, slot=slot, pH=pH, oH=oH):
                            last = None
                            for c in range(PC):
                                last = e.matmul(
                                    pH[:, oH:oH + 512],
                                    w1r[:, slot * 1024 + c * 128: slot * 1024 + (c + 1) * 128],
                                    z2t[:, c * 512:(c + 1) * 512],
                                    start=(c == 0), stop=(c == PC - 1))
                            return last
                        P.add("tensor", f, deps=[w1l, zld],
                              name=f"{st}hM{bq}_{ei}_{fi}",
                              reads=(R_w1r[slot], R_z2t), writes=(RH,))

                        def f(e, fi=fi, pH=pH, oH=oH):
                            return e.activation(hst[:, fi * 512:(fi + 1) * 512],
                                                pH[:, oH:oH + 512], AF.Gelu)
                        P.add("scalar", f, name=f"{st}g{bq}_{ei}_{fi}",
                              reads=(RH,), writes=(R_hs[fi],))

                        def f(e, fi=fi):
                            sl = hst[:, fi * 512:(fi + 1) * 512]
                            return e.tensor_tensor(sl, sl, wbs[:, :], ALU.mult)
                        h_ops.append(P.add("vector", f, deps=[wbsop],
                                           name=f"{st}hs{bq}_{ei}_{fi}",
                                           reads=(R_hs[fi], R_wbs),
                                           writes=(R_hs[fi],)))
                        i1 += 1

                    for co in range(PC):
                        slot = i2 % N2
                        cast = first_cast and bq == 0
                        srcw = dr["w2" if cast else "w2b"]
                        srcap = srcw.rearrange("(f p) o -> p f o", p=128)[
                            :, ei * NF:(ei + 1) * NF, co * 128:(co + 1) * 128]

                        def f(e, srcap=srcap, slot=slot):
                            return e.dma_start(
                                out=c3(w2r[:, slot * 2048:(slot + 1) * 2048]),
                                in_=srcap)
                        w2l = P.add("gpsimd" if cast else "sync", f, kind="dma",
                                    name=f"{st}w2L{bq}_{ei}_{co}",
                                    reads=() if cast else (R_w2b,),
                                    writes=(R_w2r[slot],))
                        if cast:
                            dstap = dr["w2b"].rearrange("(f p) o -> p f o", p=128)[
                                :, ei * NF:(ei + 1) * NF, co * 128:(co + 1) * 128]

                            def f(e, dstap=dstap, slot=slot):
                                return e.dma_start(
                                    out=dstap,
                                    in_=c3(w2r[:, slot * 2048:(slot + 1) * 2048]))
                            P.add("sync", f, deps=[w2l], kind="dma",
                                  name=f"{st}w2B{ei}_{co}",
                                  reads=(R_w2r[slot],), writes=(R_w2b,))

                        pY, oY, RY = pslot(2 + co % 2)

                        def f(e, slot=slot, pY=pY, oY=oY):
                            last = None
                            for fi in range(NF):
                                last = e.matmul(
                                    pY[:, oY:oY + 512],
                                    w2r[:, slot * 2048 + fi * 128: slot * 2048 + (fi + 1) * 128],
                                    hst[:, fi * 512:(fi + 1) * 512],
                                    start=(fi == 0), stop=(fi == NF - 1))
                            return last
                        ymm = P.add("tensor", f, deps=[w2l] + h_ops,
                                    name=f"{st}yM{bq}_{ei}_{co}",
                                    reads=(R_w2r[slot],) + tuple(R_hs),
                                    writes=(RY,))

                        def f(e, co=co, bq=bq, pY=pY, oY=oY):
                            sl = slice(co * T + bq * TOKB, co * T + (bq + 1) * TOKB)
                            return e.tensor_tensor(x[:, sl], x[:, sl],
                                                   pY[:, oY:oY + 512], ALU.add)
                        P.add("vector", f, name=f"{st}yA{bq}_{ei}_{co}",
                              reads=(RY, R_x[co]), writes=(R_x[co],))
                        i2 += 1

        def emit_ffn_stage(st, first):
            arena.reset()
            z_t = arena.al(PC * T)
            wring = arena.al(2 * 1024)
            wvw_t = arena.al(PC * 512)
            bar = bar_deps()
            R_z = [new_res(f"z{c}", bar) for c in range(PC)]
            R_wring = [new_res(f"wr{i}", bar) for i in range(2)]
            R_wvw = new_res(f"wvw", bar)

            emit_ln(x, R_x, None, accb, R_acc, None, None, st + "L0")
            if st == "A":
                emit_tap("yA", accb, R_acc)
            emit_ln(accb, R_acc, l1w, None, None, z_t, R_z, st + "L1")

            if st == "A" and "z1" in taps:
                def f(e):
                    return [e.dma_start(out=dr["tap_z1"][c * 128:(c + 1) * 128, :],
                                        in_=z_t[:, c * T:(c + 1) * T])
                            for c in range(PC)]
                P.add("gpsimd", f, kind="dma", name="tap_z1",
                      reads=tuple(R_z), n_incs=PC, semkey="tap")
            cc_k, cc_v, q_ops = emit_qkv(st, z_t, R_z, wring, R_wring,
                                         wvw_t, R_wvw)

            arena.reset()
            kvk = arena.al(2 * 2048)
            vt = arena.al(2 * 16 * 130)
            qp = arena.al(2048)
            prb = arena.al(2 * 1024)
            mskb = arena.al(8 * 512)
            bar = bar_deps()
            R_kvk = new_res(f"kvk", bar)
            R_vt = new_res(f"vt", bar)
            R_qp = new_res(f"qp", bar)
            R_prb = [new_res(f"pr{i}", bar) for i in range(2)]
            R_mskb = new_res(f"mskb", bar)
            ad_ops = emit_mha(st, cc_k, cc_v, q_ops, kvk, R_kvk, vt, R_vt,
                              qp, R_qp, prb, R_prb, mskb, R_mskb)
            if st == "A" and "Bq" in taps:
                def f(e):
                    insts = [e.dma_start(out=dr["tap_Bq"][0:1024, :],
                                         in_=dr["qs"][:, :])]
                    if "Badot" in taps:
                        insts.append(e.dma_start(out=dr["tap_Badot"][0:1024, :],
                                                 in_=dr["Ak_in"][:, :]))
                    return insts
                P.add("gpsimd", f, deps=q_ops, kind="dma", name="tap_q",
                      reads=(R_qs,), n_incs=2 if "Badot" in taps else 1,
                      semkey="tap")
            if st == "A" and "attn_heads" in taps:
                def f(e):
                    return e.dma_start(out=dr["tap_attn_heads"][:, :],
                                       in_=dr["attns"][:, :])
                P.add("gpsimd", f, deps=ad_ops, kind="dma", name="tap_ah",
                      reads=(R_attns,), semkey="tap")

            arena.reset()
            attst = arena.al(PC * 512)
            pjw = arena.al(PC * 1024)
            bar = bar_deps()
            R_attst = new_res(f"attst", bar)
            R_pjw = new_res(f"pjw", bar)
            emit_proj(st, attst, R_attst, pjw, R_pjw)
            if st == "A":
                emit_tap("a", accb, R_acc)

            emit_fold(x, R_x, accb, R_acc, 0.5, st + "fold")
            arena.reset()
            z2_t = arena.al(PC * T)
            rww = arena.al(PC * 8)
            hilo = arena.al(PC * 256)
            rwwf = arena.al(PC * 8 * 2, F32)
            bar = bar_deps()
            R_z2 = [new_res(f"z2{c}", bar) for c in range(PC)]
            R_rww = new_res(f"rww", bar)
            R_hilo = new_res(f"hilo", bar)
            emit_ln(accb, R_acc, l2w, accb, R_acc, z2_t, R_z2, st + "L2")

            def f(e):
                return [e.dma_start(out=dr["z2s"][c * 128:(c + 1) * 128, :],
                                    in_=z2_t[:, c * T:(c + 1) * T])
                        for c in range(PC)]
            P.add("sync", f, kind="dma", name=f"{st}z2d",
                  reads=tuple(R_z2), writes=(R_z2s,), n_incs=PC)

            emit_router(st, accb, R_acc, rww, R_rww, rwwf, hilo, R_hilo)

            arena.reset()
            z2t = arena.al(PC * 512)
            w1r = arena.al(4 * 1024)
            w2r = arena.al(2 * 2048)
            hst = arena.al(NF * 512)
            oh8t = arena.al(8 * 128)
            bar = bar_deps()
            R_z2t = new_res(f"z2t", bar)
            R_w1r = [new_res(f"w1r{i}", bar) for i in range(4)]
            R_w2r = [new_res(f"w2r{i}", bar) for i in range(2)]
            R_hs = [new_res(f"hs{i}", bar) for i in range(NF)]
            R_oh8 = new_res(f"oh8", bar)
            emit_moe(st, z2t, R_z2t, w1r, R_w1r, w2r, R_w2r, hst, R_hs,
                     oh8t, R_oh8, first)

        def emit_stageB():
            arena.reset()
            z_t = arena.al(PC * T)
            wring = arena.al(2 * 1024)
            wvw_t = arena.al(PC * 512)
            bar = bar_deps()
            R_z = [new_res(f"z{c}", bar) for c in range(PC)]
            R_wring = [new_res(f"wr{i}", bar) for i in range(2)]
            R_wvw = new_res("Bwvw", bar)

            emit_ln(x, R_x, None, None, None, z_t, R_z, "BL0")
            cc_k, cc_v, q_ops = emit_qkv("B", z_t, R_z, wring, R_wring,
                                         wvw_t, R_wvw)

            arena.reset()
            memsb = arena.al(16 * 128)
            sig = arena.al(4 * 512)
            qtl = arena.al(2 * 512)
            bar = bar_deps()
            R_memsb = new_res("Bmem", bar)
            R_sig = [new_res(f"sig{i}", bar) for i in range(4)]
            R_qtl = [new_res(f"qt{i}", bar) for i in range(2)]

            def f(e):
                return e.dma_start(
                    out=c3(memsb[:, :], 512),
                    in_=dr["mem"].rearrange("(ci p) (co q) -> p ci (co q)",
                                            p=128, q=128))
            memop = P.add("gpsimd", f, kind="dma", name="BmemL",
                          writes=(R_memsb,))

            for gq in range(2):
                for bq in range(NB):
                    elu_ops = []
                    for c4 in range(4):
                        cc_ = 4 * gq + c4
                        qslot = c4 % 2

                        def f(e, cc_=cc_, bq=bq, qslot=qslot):
                            return e.dma_start(
                                out=qtl[:, qslot * 512:(qslot + 1) * 512],
                                in_=dr["qs"][cc_ * 128:(cc_ + 1) * 128,
                                             bq * TOKB:(bq + 1) * TOKB])
                        qldo = P.add("sync", f, deps=q_ops, kind="dma",
                                     name=f"BqL{gq}_{bq}_{c4}", reads=(R_qs,),
                                     writes=(R_qtl[qslot],))

                        def f(e, qslot=qslot):
                            qv = qtl[:, qslot * 512:(qslot + 1) * 512]
                            return e.tensor_scalar_min(tmpv[:, :], qv, 0.0)
                        v1 = P.add("vector", f, deps=[qldo],
                                   name=f"Bel1{gq}_{bq}_{c4}",
                                   reads=(R_qtl[qslot],), writes=(R_tmp,))

                        def f(e):
                            return e.activation(tmpv2[:, :], tmpv[:, :], AF.Exp)
                        s1 = P.add("scalar", f, deps=[v1],
                                   name=f"Bel2{gq}_{bq}_{c4}",
                                   reads=(R_tmp,), writes=(R_tmp2,))

                        def f(e, qslot=qslot, c4=c4):
                            qv = qtl[:, qslot * 512:(qslot + 1) * 512]
                            e.tensor_scalar_max(tmpv[:, :], qv, 0.0)
                            e.drain()
                            return e.scalar_tensor_tensor(
                                sig[:, c4 * 512:(c4 + 1) * 512], tmpv2[:, :],
                                -1.0, tmpv[:, :], ALU.add, ALU.add)
                        elu_ops.append(P.add(
                            "vector", f, deps=[s1],
                            name=f"Bel3{gq}_{bq}_{c4}",
                            reads=(R_qtl[qslot], R_tmp2),
                            writes=(R_sig[c4], R_tmp)))

                    pD, oD, RD = pslot(6)

                    def f(e, pD=pD, oD=oD):
                        last = None
                        for c4 in range(4):
                            last = e.matmul(pD[0:1, oD:oD + 512],
                                            mnormb[:, c4:c4 + 1],
                                            sig[:, c4 * 512:(c4 + 1) * 512],
                                            start=(c4 == 0), stop=(c4 == 3))
                        return last
                    dmm = P.add("tensor", f, deps=elu_ops + [op_consts],
                                name=f"Bden{gq}_{bq}",
                                reads=tuple(R_sig), writes=(RD,))

                    def f(e, pD=pD, oD=oD):
                        e.tensor_copy(dnst[0:1, 0:512], pD[0:1, oD:oD + 512])
                        e.drain()
                        e.reciprocal(dnst[0:1, 0:512], dnst[0:1, 0:512])
                        e.drain()
                        e.tensor_copy(dnsb[0:1, 0:512], dnst[0:1, 0:512])
                        e.drain()
                        return e.tensor_tensor(dnsb[0:1, 512:1024],
                                               dnst[0:1, 0:512], dnsb[0:1, 0:512],
                                               ALU.subtract)
                    dnop = P.add("vector", f, deps=[dmm], name=f"Bdn{gq}_{bq}",
                                 reads=(RD,), writes=(R_dns,))

                    pR, oR, RR = pslot(7)

                    def f(e, pR=pR, oR=oR):
                        e.matmul(pR[:, oR:oR + 512], onw[:, :],
                                 dnsb[0:1, 0:512], start=True, stop=False)
                        return e.matmul(pR[:, oR:oR + 512], onw[:, :],
                                        dnsb[0:1, 512:1024], start=False,
                                        stop=True)
                    rdop = P.add("tensor", f, deps=[dnop, op_selw],
                                 name=f"Brd{gq}_{bq}", reads=(R_dns,),
                                 writes=(RR,))

                    def f(e, pR=pR, oR=oR):
                        return e.activation(rdbs[:, :], pR[:, oR:oR + 512],
                                            AF.Copy)
                    rcop = P.add("scalar", f, name=f"Brc{gq}_{bq}",
                                 reads=(RR,), writes=(R_rdbs,))

                    for kvo in range(4):
                        pN, oN, RN = pslot(4 + kvo % 2)

                        def f(e, gq=gq, kvo=kvo, pN=pN, oN=oN):
                            last = None
                            for c4 in range(4):
                                mcol = (c4 * 4 + kvo) * 128
                                last = e.matmul(
                                    pN[:, oN:oN + 512],
                                    memsb[:, mcol:mcol + 128],
                                    sig[:, c4 * 512:(c4 + 1) * 512],
                                    start=(c4 == 0), stop=(c4 == 3))
                            return last
                        nmm = P.add("tensor", f, deps=[memop] + elu_ops,
                                    name=f"BnM{gq}_{bq}_{kvo}",
                                    reads=(R_memsb,) + tuple(R_sig),
                                    writes=(RN,))

                        def f(e, gq=gq, bq=bq, kvo=kvo, pN=pN, oN=oN):
                            co = 4 * gq + kvo
                            sl = slice(co * T + bq * TOKB,
                                       co * T + (bq + 1) * TOKB)
                            e.tensor_tensor(tmpv[:, :], pN[:, oN:oN + 512],
                                            rdbs[:, :], ALU.mult)
                            e.drain()
                            return e.tensor_tensor(x[:, sl], x[:, sl],
                                                   tmpv[:, :], ALU.add)
                        P.add("vector", f, deps=[rcop, nmm],
                              name=f"BnA{gq}_{bq}_{kvo}",
                              reads=(RN, R_rdbs, R_x[4 * gq + kvo]),
                              writes=(R_x[4 * gq + kvo], R_tmp))

            arena.reset()
            kd = arena.al(2 * 2048)
            vt = arena.al(2 * 16 * 65)
            qp = arena.al(2048)
            prb = arena.al(2 * 1024)
            bar = bar_deps()
            R_kd = new_res("Bkd", bar)
            R_vt = new_res("Bvt", bar)
            R_qp = new_res("Bqp", bar)
            R_prb = [new_res(f"pr{i}", bar) for i in range(2)]

            def f(e):
                return e.memset(vt[:, :], 1.0)
            vtones = P.add("vector", f, name="Bvt1", writes=(R_vt,))

            Rav = (R_ps[4], R_ps[5])
            for p in range(NP):
                def f(e, p=p):
                    insts = []
                    for s in range(2):
                        src = dr["Bk_out"][s * 512 + (p // 2) * 128 + (p % 2) * 64:
                                           s * 512 + (p // 2) * 128 + (p % 2) * 64 + 64, :]
                        insts.append(e.dma_start(
                            out=kd[0:64, s * 2048:(s + 1) * 2048], in_=src))
                        insts.append(e.dma_start(
                            out=kd[64:128, s * 2048:(s + 1) * 2048], in_=src))
                    return insts
                kld = P.add("sync", f, deps=[cc_k], kind="dma",
                            name=f"Bkld{p}", writes=(R_kd,), n_incs=4)

                def f(e, p=p):
                    insts = []
                    for s in range(2):
                        dst = vt[:, s * 16 * 65:(s + 1) * 16 * 65].rearrange(
                            "k (g w) -> k g w", w=65)[:, :, 0:64]
                        src = dr["Bv_out"][s * NP * T + p * T: s * NP * T + (p + 1) * T, :
                                           ].rearrange("(g k) d -> k g d", k=128)
                        insts.append(e.dma_start(out=dst, in_=src))
                    return insts
                vld = P.add("sync", f, deps=[cc_v, vtones], kind="dma",
                            name=f"Bvld{p}", writes=(R_vt,), n_incs=2)

                def f(e, p=p):
                    return e.dma_start(out=qp[:, :],
                                       in_=dr["qs"][p * 128:(p + 1) * 128, :])
                qld = P.add("sync", f, deps=q_ops, kind="dma",
                            name=f"Bqld{p}", reads=(R_qs,), writes=(R_qp,))

                for j in range(NB):
                    for ci in range(32):
                        g = ci
                        pp = PPs[ci % 2]
                        Rpa, Rpb = R_ps[(ci % 2) * 2], R_ps[(ci % 2) * 2 + 1]

                        def f(e, g=g, j=j, pp=pp):
                            e.matmul(pp[:, 0:512], kd[0:64, g * 128:(g + 1) * 128],
                                     qp[0:64, j * TOKB:(j + 1) * TOKB],
                                     start=True, stop=True, tile_position=(0, 0))
                            return e.matmul(
                                pp[:, 512:1024], kd[64:128, g * 128:(g + 1) * 128],
                                qp[64:128, j * TOKB:(j + 1) * TOKB],
                                start=True, stop=True, tile_position=(64, 0))
                        scop = P.add("tensor", f, deps=[kld, qld],
                                     name=f"Bsc{p}_{j}_{ci}",
                                     reads=(R_kd, R_qp), writes=(Rpa, Rpb))

                        prs = ci % 2
                        R_pr = R_prb[prs]

                        def f(e, pp=pp, prs=prs):
                            return e.activation(
                                prb[:, prs * 1024:(prs + 1) * 1024],
                                pp[:, 0:1024], AF.Exp, bias=negC[:, 0:1],
                                scale=0.125)
                        exop = P.add("scalar", f, deps=[scop],
                                     name=f"Bex{p}_{j}_{ci}",
                                     reads=(Rpa, Rpb, R_rconst), writes=(R_pr,))

                        gb = g * 65

                        def f(e, gb=gb, prs=prs, first=(ci == 0),
                              last=(ci == 31)):
                            pr = prb[:, prs * 1024:(prs + 1) * 1024]
                            e.matmul(PP2[0:65, 0:512], vt[:, gb:gb + 65],
                                     pr[:, 0:512], start=first, stop=last)
                            return e.matmul(PP2[0:65, 512:1024],
                                            vt[:, gb:gb + 65],
                                            pr[:, 512:1024], start=first,
                                            stop=last)
                        avop = P.add("tensor", f, deps=[vld],
                                     name=f"Bav{p}_{j}_{ci}",
                                     reads=(R_pr, R_vt),
                                     writes=Rav if ci == 0 else ())
                        if ci > 0:
                            for rr in Rav:
                                rr.writers.append(avop)
                        last_av = avop

                    def f(e):
                        e.tensor_copy(dnst[0:1, 0:512], PP2[64:65, 0:512])
                        e.tensor_copy(dnst[0:1, 512:1024], PP2[64:65, 512:1024])
                        e.drain()
                        e.reciprocal(dnst[:, :], dnst[:, :])
                        e.drain()
                        e.tensor_copy(dnsb[:, 0:1024], dnst[:, :])
                        e.drain()
                        return e.tensor_tensor(dnsb[:, 1024:2048], dnst[:, :],
                                               dnsb[:, 0:1024], ALU.subtract)
                    dnop = P.add("vector", f, deps=[last_av],
                                 name=f"Bdn2{p}_{j}", reads=Rav,
                                 writes=(R_dns,))

                    pR, oR, RR = pslot(6)

                    def f(e, pR=pR, oR=oR):
                        e.matmul(pR[:, oR:oR + 512], selw[0:1, 0:128],
                                 dnsb[:, 0:512], start=True, stop=False)
                        e.matmul(pR[:, oR:oR + 512], selw[0:1, 128:256],
                                 dnsb[:, 512:1024], start=False, stop=False)
                        e.matmul(pR[:, oR:oR + 512], selw[0:1, 0:128],
                                 dnsb[:, 1024:1536], start=False, stop=False)
                        return e.matmul(pR[:, oR:oR + 512], selw[0:1, 128:256],
                                        dnsb[:, 1536:2048], start=False,
                                        stop=True)
                    rdop = P.add("tensor", f, deps=[dnop, op_selw],
                                 name=f"Brd2{p}_{j}", reads=(R_dns,),
                                 writes=(RR,))

                    def f(e, pR=pR, oR=oR):
                        return e.activation(rdbs[:, :], pR[:, oR:oR + 512],
                                            AF.Copy)
                    rcop = P.add("scalar", f, name=f"Brc2{p}_{j}",
                                 reads=(RR,), writes=(R_rdbs,))

                    def f(e, p=p, j=j):
                        sl = slice(p * T + j * TOKB, p * T + (j + 1) * TOKB)
                        e.tensor_tensor(tmpv[0:64, :], PP2[0:64, 0:512],
                                        rdbs[0:64, :], ALU.mult)
                        e.tensor_tensor(tmpv[64:128, :], PP2[0:64, 512:1024],
                                        rdbs[64:128, :], ALU.mult)
                        e.drain()
                        e.tensor_tensor(x[0:64, sl], x[0:64, sl],
                                        tmpv[0:64, :], ALU.add)
                        return e.tensor_tensor(x[64:128, sl], x[64:128, sl],
                                               tmpv[64:128, :], ALU.add)
                    P.add("vector", f, deps=[rcop], name=f"Bao{p}_{j}",
                          reads=Rav + (R_rdbs, R_x[p]),
                          writes=(R_x[p], R_tmp))

        if stages == "L":
            emit_ln(x, R_x, None, accb, R_acc, None, None, "L0")
            emit_tap("yA", accb, R_acc)
        if stages == "L3":
            emit_ln(x, R_x, None, accb, R_acc, None, None, "L0")
            def f(e):
                e.tensor_copy(accb[:, 0:512], aux_m[:, :])
                e.tensor_copy(accb[:, 512:1024], aux_v[:, :])
                e.tensor_copy(accb[:, 1024:1536], aux_r[:, :])
                e.tensor_copy(accb[:, 1536:2048].bitcast(BF16)[:, 0:512], xsqb[:, 0:512])
                e.tensor_copy(accb[:, 2048:2560].bitcast(BF16)[:, 0:512], xcpb[:, 0:512])
                return e.engine_nop()
            P.add("vector", f, deps=bar_deps(), name="dbg_aux",
                  writes=tuple(R_acc), reads=(R_aux,))
            emit_tap("yA", accb, R_acc)
        if stages == "L2":
            def f(e):
                for c in range(PC):
                    e.memset(accb[:, c * T:(c + 1) * T], 3.14)
                return e.engine_nop()
            mop_ = P.add("vector", f, name="dbg_memset", writes=tuple(R_acc))
            emit_tap("yA", accb, R_acc)
        if "A" in stages:
            emit_ffn_stage("A", first=True)
            emit_tap("x1", x, R_x)
        if "B" in stages:
            emit_stageB()
            emit_tap("x2", x, R_x)
        if "C" in stages:
            emit_ffn_stage("C", first=("A" not in stages))
            emit_tap("x3", x, R_x)

        def f(e):
            return [e.dma_start(out=dr["out_x"][c * 128:(c + 1) * 128, :],
                                in_=x[:, c * T:(c + 1) * T])
                    for c in range(PC)]
        outop = P.add("sync", f, kind="dma", name="xout",
                      reads=tuple(R_x), n_incs=PC, semkey="out")

        def f(e):
            return e.nop()
        P.add("sync", f, deps=[outop], name="terminal")

        P.emit(nc, block, es)
        rem = nc.sbuf_bytes_remaining
        nops = len(P.ops)
    return nc, {"sbuf_remaining": rem, "n_ops": nops}


def kernel(**inputs):
    nc, meta = build({"stages": "ABC", "taps": []})
    in_maps = shard_inputs(inputs)
    res = run_bass_kernel_spmd(nc, in_maps, core_ids=list(range(8)))
    return gather_output(res.results)



# version 02
# speedup vs baseline: 1.0335x; 1.0335x over previous
import bisect

ENGINES = ("tensor", "scalar", "vector", "gpsimd", "sync")


class Op:
    __slots__ = ("eng", "fn", "deps", "kind", "need_tok", "val", "name",
                 "n_incs", "pos", "semkey", "idx")

    def __init__(self, eng, fn, deps, kind, name, n_incs=1, semkey=None):
        self.eng = eng
        self.fn = fn
        self.deps = list(deps)
        self.kind = kind
        self.need_tok = False
        self.val = None
        self.name = name
        self.n_incs = n_incs
        self.pos = None
        self.semkey = semkey
        self.idx = None


class Res:
    __slots__ = ("writers", "readers", "name")

    def __init__(self, name=""):
        self.writers = []
        self.readers = []
        self.name = name


class Plan:
    def __init__(self):
        self.ops = []

    def add(self, eng, fn, deps=(), kind="compute", name="",
            reads=(), writes=(), n_incs=1, semkey=None):
        deps = list(deps)
        for r in reads:
            deps += r.writers
        for r in writes:
            deps += r.readers
        seen = set()
        dd = []
        for d in deps:
            if d is None or id(d) in seen:
                continue
            seen.add(id(d))
            dd.append(d)
        if semkey is None and kind in ("dma", "cc"):
            semkey = writes[0].name if writes else name
        op = Op(eng, fn, dd, kind, name, n_incs, semkey)
        op.idx = len(self.ops)
        self.ops.append(op)
        for r in writes:
            r.readers = []
            r.writers.append(op)
        for r in reads:
            r.readers.append(op)
        for d in dd:
            if d.eng != eng or d.kind != "compute":
                d.need_tok = True
        return op

    def resolve(self):
        cnt = {}
        pos = {}
        self.sem_hist = {}
        for op in self.ops:
            pos[op.eng] = pos.get(op.eng, 0) + 1
            op.pos = pos[op.eng]
            if op.kind in ("dma", "cc"):
                op.need_tok = True
            if op.need_tok:
                key = (op.eng, op.kind, op.semkey)
                step = 16 if op.kind == "dma" else 1
                cnt[key] = cnt.get(key, 0) + step * op.n_incs
                op.val = cnt[key]
                h = self.sem_hist.setdefault(key, ([], []))
                h[0].append(op.idx)
                h[1].append(cnt[key])
        self.totals = cnt

    def wait_val(self, dep, consumer_idx):
        if dep.kind == "compute":
            return dep.val
        key = (dep.eng, dep.kind, dep.semkey)
        idxs, vals = self.sem_hist[key]
        i = bisect.bisect_left(idxs, consumer_idx) - 1
        if i < 0:
            return dep.val
        return max(dep.val, vals[i])

    def emit(self, nc, block, es):
        self.resolve()
        sems = {}
        for n, key in enumerate(self.totals):
            sems[key] = es.enter_context(nc.semaphore(f"s{n}_{key[2] or key[0]}"))
        by_eng = {e: [op for op in self.ops if op.eng == e] for e in ENGINES}

        def mk_body(eng_name, ops):
            def body(e):
                hi = {}
                last_drain = 0
                for op in ops:
                    need_drain = False
                    for d in op.deps:
                        if d.eng == eng_name and d.kind == "compute":
                            if d.pos > last_drain:
                                need_drain = True
                            continue
                        assert d.val is not None, (op.name, d.name)
                        key = (d.eng, d.kind, d.semkey)
                        v = self.wait_val(d, op.idx)
                        if hi.get(key, -1) >= v:
                            continue
                        hi[key] = v
                        e.wait_ge(sems[key], v)
                    if need_drain:
                        e.drain()
                        last_drain = op.pos - 1
                    last = op.fn(e)
                    if op.need_tok:
                        assert last is not None, f"{op.name}: token needed"
                        insts = last if isinstance(last, (list, tuple)) else [last]
                        assert len(insts) == op.n_incs, (op.name, len(insts),
                                                         op.n_incs)
                        step = 16 if op.kind == "dma" else 1
                        for ins_ in insts:
                            ins_.then_inc(sems[(op.eng, op.kind, op.semkey)],
                                          step)
            return body

        for e in ENGINES:
            if by_eng[e]:
                getattr(block, e)(mk_body(e, by_eng[e]))
        return sems


import contextlib
import numpy as np
import ml_dtypes
import concourse.bass as bass
import concourse.mybir as mybir
from concourse.bass_utils import run_bass_kernel_spmd

F32 = mybir.dt.float32
F32R = mybir.dt.float32r
BF16 = mybir.dt.bfloat16
AF = mybir.ActivationFunctionType
ALU = mybir.AluOpType
AXX = mybir.AxisListType.X

C = 1024
PC = 8
T = 2048
NT = 16
TOKB = 512
NB = 4
NP = 8
NE = 8
FF = 2048
NF = 16
EPS = 1e-5
EXPB = -12.0

CONFIG = {"stages": "ABC", "taps": []}
BF = ml_dtypes.bfloat16
RG2 = [[0, 1], [2, 3], [4, 5], [6, 7]]


def shard_inputs(ins):
    x = np.ascontiguousarray(np.asarray(ins["L0_attn_x"], np.float32))
    tril = np.tril(np.ones((128, 128), np.float32))
    masks_by_role = []
    for r in (0, 1):
        mm = np.zeros((8, 128, 512), np.float32)
        for d in range(8):
            for a in range(4):
                own = 2 * a + r
                if own > d:
                    mm[d, :, a * 128:(a + 1) * 128] = 1.0
                elif own == d:
                    mm[d, :, a * 128:(a + 1) * 128] = tril.T
        masks_by_role.append(np.ascontiguousarray(mm.astype(BF)))
    jmean = np.full((128, 128), 1.0 / C, np.float32)
    jone = np.ones((128, 128), np.float32)
    onesb = np.ones((1, 128), np.float32).astype(BF)
    ident = np.eye(128, dtype=np.float32)
    sel2 = np.zeros((1, 256), np.float32)
    sel2[0, 0:64] = 1.0
    sel2[0, 192:256] = 1.0
    oh8 = np.zeros((8, 8 * 128), np.float32)
    for e_ in range(8):
        oh8[e_, e_ * 128:(e_ + 1) * 128] = 1.0
    oh8 = oh8.astype(BF)
    ln1w = np.ascontiguousarray(np.asarray(ins["L0_ffn_ln1_w"], np.float32).reshape(PC, 128).T)
    ln2w = np.ascontiguousarray(np.asarray(ins["L0_ffn_ln2_w"], np.float32).reshape(PC, 128).T)
    mnorm = np.ascontiguousarray(np.asarray(ins["L0_attn_memory_norm"], np.float32).reshape(4, 128).T)
    maps = []
    for core in range(8):
        b, r = core // 2, core % 2
        xb = x[b].reshape(32, 128, C)[r::2].reshape(T, C)
        maps.append({
            "x_in": np.ascontiguousarray(xb.T),
            "wattn": np.asarray(ins["L0_ffn_c_attn_w"], np.float32),
            "wproj": np.asarray(ins["L0_ffn_c_proj_w"], np.float32),
            "wrout": np.asarray(ins["L0_ffn_router_w"], np.float32),
            "w1": np.asarray(ins["L0_ffn_w1"], np.float32),
            "w2": np.asarray(ins["L0_ffn_w2"], np.float32),
            "wq": np.asarray(ins["L0_attn_Wq"], np.float32),
            "wk": np.asarray(ins["L0_attn_Wk"], np.float32),
            "wv": np.asarray(ins["L0_attn_Wv"], np.float32),
            "mem": np.asarray(ins["L0_attn_memory"], np.float32),
            "mnorm": mnorm,
            "memw": np.asarray(ins["L0_attn_memory_weight"], np.float32).reshape(1, 1),
            "ln1w": ln1w, "ln2w": ln2w,
            "masks": masks_by_role[r],
            "jmean": jmean, "jone": jone, "onesb": onesb, "ident": ident,
            "sel2": sel2, "oh8": oh8,
        })
    return maps


def gather_output(results):
    out = np.empty((4, 4096, C), np.float32)
    for core in range(8):
        b, r = core // 2, core % 2
        tok = np.ascontiguousarray(np.asarray(results[core]["out_x"]).T)
        out[b].reshape(32, 128, C)[r::2] = tok.reshape(NT, 128, C)
    return out


class Arena:

    def __init__(self, tens, cols):
        self.t = tens
        self.cols = cols
        self.off = 0
        self.spans = []

    def al2(self, ncols_bf16, name, dt=BF16):
        ncols_bf16 = (ncols_bf16 + 1) // 2 * 2
        assert self.off + ncols_bf16 <= self.cols, (self.off, ncols_bf16)
        s, e = self.off, self.off + ncols_bf16
        v = self.t[:, s:e]
        self.off = e
        r = Res(name)
        dead = []
        for (s0, e0, r0) in self.spans:
            if s0 < e and s < e0:
                r.readers.extend(r0.readers)
                r.readers.extend(r0.writers)
        self.spans = [sp for sp in self.spans if not (sp[0] < e and s < sp[1])]
        self.spans.append((s, e, r))
        return (v.bitcast(F32) if dt == F32 else v), r

    def al_multi(self, nslots, slot_cols, name, dt=BF16):
        views, ress = [], []
        for i in range(nslots):
            v, r = self.al2(slot_cols, f"{name}{i}", dt)
            views.append(v)
            ress.append(r)
        return views, ress

    def reset(self):
        self.off = 0


def build(cfg=None):
    cfg = dict(CONFIG, **(cfg or {}))
    stages = cfg["stages"]
    taps = set(cfg["taps"])

    nc = bass.Bass("TRN2", target_bir_lowering=False)
    dr = {}

    def din(name, shape, dt=F32):
        dr[name] = nc.dram_tensor(name, shape, dt, kind="ExternalInput").ap()

    def dout(name, shape, dt=F32):
        dr[name] = nc.dram_tensor(name, shape, dt, kind="ExternalOutput").ap()

    def dint(name, shape, dt=BF16):
        dr[name] = nc.dram_tensor(name, shape, dt).ap()

    din("x_in", [C, T])
    din("wattn", [C, 3 * C]); din("wproj", [C, C]); din("wrout", [C, 8])
    din("w1", [C, NE * FF]); din("w2", [NE * FF, C])
    din("wq", [C, C]); din("wk", [C, 512]); din("wv", [C, 512])
    din("mem", [512, 512]); din("mnorm", [128, 4]); din("memw", [1, 1])
    din("ln1w", [128, PC]); din("ln2w", [128, PC])
    din("masks", [8, 128, 512], BF16)
    din("jmean", [128, 128]); din("jone", [128, 128])
    din("onesb", [1, 128], BF16); din("ident", [128, 128])
    din("sel2", [1, 256]); din("oh8", [8, 8 * 128], BF16)
    dout("out_x", [C, T], F32)

    dint("w1b", [C, NE * FF]); dint("w2b", [NE * FF, C])
    dint("qs", [C, T]); dint("attns", [C, T]); dint("z2s", [C, T])
    for st in [s for s in stages if s in "AC"]:
        dint(f"{st}k_in", [C, T]); dint(f"{st}k_out", [2 * C, T])
        dint(f"{st}v_in", [NP * T, 128]); dint(f"{st}v_out", [2 * NP * T, 128])
    if "B" in stages:
        dint("Bk_in", [512, T]); dint("Bk_out", [1024, T])
        dint("Bv_in", [NP * T, 64]); dint("Bv_out", [2 * NP * T, 64])

    for tn in taps:
        dout("tap_" + tn, [C, T], F32)

    es = contextlib.ExitStack()

    def sb(name, shape, dt):
        return es.enter_context(nc.sbuf_tensor(name, shape, dt))

    with es:
        x = sb("x", [128, PC * T], F32)
        accb = sb("accb", [128, PC * T], F32)
        NSTG = 3
        stgt = sb("stgt", [128, NSTG * 512], BF16)
        tmpv = sb("tmpv", [128, TOKB], F32)
        tmpv2 = sb("tmpv2", [128, TOKB], F32)
        rdbs = sb("rdbs", [128, TOKB], F32)
        dnst = sb("dnst", [1, 1024], F32)
        aux_m = sb("aux_m", [128, TOKB], F32)
        aux_v = sb("aux_v", [128, TOKB], F32)
        aux_r = sb("aux_r", [128, TOKB], F32)
        xsqb = sb("xsqb", [128, 2 * TOKB], BF16)
        xcpb = sb("xcpb", [128, 2 * TOKB], BF16)
        jm = sb("jm", [128, 128], BF16)
        jo = sb("jo", [128, 128], BF16)
        onb = sb("onb", [1, 128], BF16)
        idn = sb("idn", [128, 128], F32)
        sel = sb("sel", [1, 256], BF16)
        selw = sb("selw", [1, 256], BF16)
        onw = sb("onw", [1, 128], BF16)
        oneb = sb("oneb", [128, 1], F32)
        dnsb = sb("dnsb", [1, 4 * 512], BF16)
        l1w = sb("l1w", [128, PC], F32)
        l2w = sb("l2w", [128, PC], F32)
        mnormt = sb("mnormt", [128, 4], F32)
        mnormb = sb("mnormb", [128, 4], BF16)
        epsb = sb("epsb", [128, 1], F32)
        negC = sb("negC", [128, 1], F32)
        wblend = sb("wblend", [1, 1], F32)
        scal = sb("scal", [2, 8], F32)
        r8 = sb("r8", [128, 8], F32)
        r8b = sb("r8b", [128, 8], F32)
        r1a = sb("r1a", [128, 1], F32)
        r1b = sb("r1b", [128, 1], F32)
        rwt = sb("rwt", [128, 8], F32)
        wbs = sb("wbs", [128, TOKB], BF16)
        rwT = sb("rwT", [8, T], BF16)

        ARN = (nc.sbuf_bytes_remaining - 64) // 2 // 2 * 2
        arn_t = sb("arn", [128, ARN], BF16)
        arena = Arena(arn_t, ARN)

        PPs = [es.enter_context(nc.psum_tensor(f"PP{i}", [128, 1024], F32))
               for i in range(4)]
        PP0, PP1, PP2, PP3 = PPs

        block = es.enter_context(nc.Block())
        P = Plan()

        R_x = [Res(f"x{c}") for c in range(PC)]
        R_acc = [Res(f"acc{c}") for c in range(PC)]
        R_ps = [Res(f"ps{i}") for i in range(8)]
        R_stg = [Res(f"stg{i}") for i in range(NSTG)]
        R_aux = Res("aux")
        R_xsq = [Res("xsq0"), Res("xsq1")]
        R_dns = Res("dns")
        R_rdbs = Res("rdbs")
        R_tmp = Res("tmp")
        R_tmp2 = Res("tmp2")
        R_qs = Res("qs")
        R_z2s = Res("z2s")
        R_attns = Res("attns")
        R_w1b = Res("w1b")
        R_w2b = Res("w2b")
        R_wbs = Res("wbs")
        R_rconst = Res("consts")
        R_scal = Res("scal")
        R_r8 = Res("r8")
        R_rwT = Res("rwT")
        R_r1 = Res("r1")
        R_rwt = Res("rwt")

        def pslot(i):
            return PPs[i // 2], (i % 2) * 512, R_ps[i]

        def bar_deps():
            last = {}
            for op in P.ops:
                last[op.eng] = op
            return list(last.values())

        def new_res(name, bar):
            r = Res(name)
            r.readers = list(bar)
            return r

        def gran_w(w_ap, o128, cchunks=PC):
            return w_ap.rearrange("(c p) o -> p c o", p=128)[
                :, :, o128 * 128:(o128 + 1) * 128]

        def c3(sb_ap, o=128):
            return sb_ap.rearrange("p (c o) -> p c o", o=o)

        def f(e):
            return [
                e.dma_start(out=jm[:, :], in_=dr["jmean"][:, :]),
                e.dma_start(out=jo[:, :], in_=dr["jone"][:, :]),
                e.dma_start(out=onb[:, :], in_=dr["onesb"][:, :]),
                e.dma_start(out=idn[:, :], in_=dr["ident"][:, :]),
                e.dma_start(out=sel[:, :], in_=dr["sel2"][:, :]),
                e.dma_start(out=l1w[:, :], in_=dr["ln1w"][:, :]),
                e.dma_start(out=l2w[:, :], in_=dr["ln2w"][:, :]),
                e.dma_start(out=mnormt[:, :], in_=dr["mnorm"][:, :]),
                e.dma_start(out=wblend[:, :], in_=dr["memw"][:, :]),
            ]
        op_consts = P.add("gpsimd", f, kind="dma", name="consts",
                          writes=(R_rconst,), n_incs=9)

        def f(e):
            insts = []
            for c in range(PC):
                insts.append(e.dma_start(
                    out=x[:, c * T:(c + 1) * T],
                    in_=dr["x_in"][c * 128:(c + 1) * 128, :]))
            return insts
        P.add("sync", f, kind="dma", name="xload", writes=tuple(R_x), n_incs=PC)

        def f(e):
            e.memset(epsb[:, :], EPS)
            e.memset(oneb[:, :], 1.0)
            e.memset(negC[:, :], EXPB)
            e.tensor_copy(mnormb[:, :], mnormt[:, :])
            return e.tensor_scalar_mul(scal[0:1, 0:1], wblend[0:1, 0:1], -1.0)
        op_i1 = P.add("vector", f, deps=[op_consts], name="init_v1",
                      writes=(R_scal,))
        R_rconst.writers.append(op_i1)

        def f(e):
            return e.activation(scal[0:1, 1:2], scal[0:1, 0:1], AF.Exp)
        op_i2 = P.add("scalar", f, deps=[op_i1], name="init_s1",
                      reads=(R_scal,))
        R_scal.writers.append(op_i2)

        def f(e):
            e.tensor_scalar_add(scal[0:1, 1:2], scal[0:1, 1:2], 1.0)
            e.drain()
            e.reciprocal(scal[0:1, 3:4], scal[0:1, 1:2])
            e.drain()
            e.tensor_scalar_mul(scal[0:1, 2:3], scal[0:1, 3:4], -1.0)
            e.drain()
            return e.tensor_scalar_add(scal[0:1, 2:3], scal[0:1, 2:3], 1.0)
        op_i3 = P.add("vector", f, deps=[op_i2], name="init_v2",
                      reads=(R_scal,), writes=(R_scal,))

        def f(e):
            return e.dma_start(out=scal[1:2, 2:4], in_=scal[0:1, 2:4])
        op_i4 = P.add("gpsimd", f, deps=[op_i3], kind="dma", name="init_rep",
                      reads=(R_scal,))
        R_scal.writers.append(op_i4)

        def f(e):
            e.tensor_scalar(selw[:, :], sel[:, :], scal[0:1, 2:3], None, ALU.mult)
            return e.tensor_scalar(onw[:, :], jo[0:1, :], scal[0:1, 3:4],
                                   None, ALU.mult)
        op_selw = P.add("vector", f, deps=[op_i4], name="init_v3",
                        reads=(R_scal, R_rconst), writes=(R_scal,))

        def emit_ln(src_t, R_src, gain_col, out32_t, R_o32, outbf_t, R_obf, tag):
            for bq in range(NB):
                ppm, offm, Rm = pslot(0)
                _, offs, Rs = pslot(1)
                ps_m = ppm[:, offm:offm + 512]
                ps_ss = ppm[:, 512:1024]
                last_stat = None
                for c in range(PC):
                    sl = slice(c * T + bq * TOKB, c * T + (bq + 1) * TOKB)
                    xq = xsqb[:, (c % 2) * 512:(c % 2) * 512 + 512]
                    xc = xcpb[:, (c % 2) * 512:(c % 2) * 512 + 512]

                    def f(e, sl=sl, xq=xq, xc=xc):
                        e.activation(xc, src_t[:, sl], AF.Copy)
                        return e.activation(xq, src_t[:, sl], AF.Square)
                    sqop = P.add("scalar", f, name=f"{tag}sq{bq}_{c}",
                                 reads=(R_src[c],), writes=(R_xsq[c % 2],))

                    def f(e, xq=xq, xc=xc, c=c):
                        e.matmul(ps_m, jm[:, :], xc,
                                 start=(c == 0), stop=(c == PC - 1))
                        return e.matmul(ps_ss, jo[:, :], xq,
                                        start=(c == 0), stop=(c == PC - 1))
                    stat = P.add("tensor", f, deps=[sqop, op_consts],
                                 name=f"{tag}st{bq}_{c}",
                                 reads=(R_xsq[c % 2], R_src[c]),
                                 writes=(Rm, Rs) if c == 0 else ())
                    if c > 0:
                        Rm.writers.append(stat)
                        Rs.writers.append(stat)
                    last_stat = stat

                def f(e):
                    e.tensor_copy(aux_m[:, :], ps_m)
                    e.tensor_scalar_mul(aux_v[:, :], ps_ss, 1.0 / C)
                    e.drain()
                    e.tensor_tensor(tmpv[:, :], aux_m[:, :], aux_m[:, :], ALU.mult)
                    e.drain()
                    return e.tensor_tensor(aux_v[:, :], aux_v[:, :], tmpv[:, :],
                                           ALU.subtract)
                vop = P.add("vector", f, deps=[last_stat], name=f"{tag}var{bq}",
                            reads=(Rm, Rs), writes=(R_aux, R_tmp))

                def f(e):
                    e.activation(aux_v[:, :], aux_v[:, :], AF.Ln, bias=epsb[:, 0:1])
                    e.drain()
                    return e.activation(aux_r[:, :], aux_v[:, :], AF.Exp, scale=-0.5)
                rop = P.add("scalar", f, deps=[vop], name=f"{tag}r{bq}",
                            reads=(R_aux, R_rconst))
                R_aux.writers.append(rop)

                def f(e, bq=bq):
                    last = None
                    for c0 in range(0, PC, 2):
                        for c in (c0, c0 + 1):
                            sl = slice(c * T + bq * TOKB, c * T + (bq + 1) * TOKB)
                            tv = tmpv if c == c0 else tmpv2
                            e.tensor_tensor(tv[:, :], src_t[:, sl], aux_m[:, :],
                                            ALU.subtract)
                        e.drain()
                        for c in (c0, c0 + 1):
                            sl = slice(c * T + bq * TOKB, c * T + (bq + 1) * TOKB)
                            tv = tmpv if c == c0 else tmpv2
                            if out32_t is not None:
                                last = e.tensor_tensor(out32_t[:, sl], tv[:, :],
                                                       aux_r[:, :], ALU.mult)
                            if outbf_t is not None:
                                if gain_col is not None:
                                    last = e.scalar_tensor_tensor(
                                        outbf_t[:, sl], tv[:, :],
                                        gain_col[:, c:c + 1], aux_r[:, :],
                                        ALU.mult, ALU.mult)
                                else:
                                    last = e.tensor_tensor(outbf_t[:, sl],
                                                           tv[:, :],
                                                           aux_r[:, :], ALU.mult)
                        if c0 + 2 < PC:
                            e.drain()
                    return last
                wr = tuple(R_o32[c] for c in range(PC)) if R_o32 else ()
                wr += tuple(R_obf[c] for c in range(PC)) if R_obf else ()
                P.add("vector", f, deps=[rop], name=f"{tag}nrm{bq}",
                      reads=tuple(R_src) + (R_aux, R_rconst),
                      writes=wr + (R_tmp, R_tmp2))

        def emit_tap(name, src_t, R_list):
            if name not in taps:
                return

            def f(e):
                insts = []
                for c in range(PC):
                    insts.append(e.dma_start(
                        out=dr["tap_" + name][c * 128:(c + 1) * 128, :],
                        in_=src_t[:, c * T:(c + 1) * T]))
                return insts
            P.add("sync", f, kind="dma", name="tap_" + name,
                  reads=tuple(R_list), n_incs=PC, semkey="tap")

        def emit_qkv(st, z_t, R_z, wrv, R_wring, wvw_t, R_wvw):
            kin = dr[f"{st}k_in"]
            vin = dr[f"{st}v_in"]
            NSL = len(R_wring)
            k_dma, v_dma, q_dma = [], [], []
            if st in "AC":
                grans = [("wattn", o, o) for o in list(range(8, 16)) + list(range(8))]
                vsrc = ("wattn", 2048)
                nvh = 2
            else:
                grans = [("wk", o, 8 + o) for o in range(4)] + \
                        [("wq", o, o) for o in range(8)]
                vsrc = ("wv", 0)
                nvh = 1
            ip = 0
            for gi, (wn, o, dsto) in enumerate(grans):
                slot = gi % NSL

                def f(e, wn=wn, o=o, slot=slot):
                    return e.dma_start(out=c3(wrv[slot]), in_=gran_w(dr[wn], o))
                gop = P.add("gpsimd", f, kind="dma", name=f"{st}qW{gi}",
                            writes=(R_wring[slot],))
                for bq in range(NB):
                    pp, off, Rp = pslot(ip % 8)

                    def f(e, slot=slot, bq=bq, pp=pp, off=off):
                        last = None
                        for c in range(PC):
                            last = e.matmul(
                                pp[:, off:off + 512],
                                wrv[slot][:, c * 128:(c + 1) * 128],
                                z_t[:, c * T + bq * TOKB: c * T + (bq + 1) * TOKB],
                                start=(c == 0), stop=(c == PC - 1))
                        return last
                    P.add("tensor", f, deps=[gop], name=f"{st}qM{gi}_{bq}",
                          reads=(R_wring[slot],) + tuple(R_z), writes=(Rp,))

                    def f(e, ip=ip, pp=pp, off=off):
                        return e.activation(stgt[:, (ip % NSTG) * 512:(ip % NSTG) * 512 + 512],
                                            pp[:, off:off + 512], AF.Copy)
                    P.add("scalar", f, name=f"{st}qE{gi}_{bq}",
                          reads=(Rp,), writes=(R_stg[ip % NSTG],))

                    if dsto >= 8:
                        dst = kin[(dsto - 8) * 128:(dsto - 7) * 128,
                                  bq * TOKB:(bq + 1) * TOKB]
                        wres = ()
                    else:
                        dst = dr["qs"][dsto * 128:(dsto + 1) * 128,
                                       bq * TOKB:(bq + 1) * TOKB]
                        wres = (R_qs,)

                    def f(e, dst=dst, ip=ip):
                        return e.dma_start(out=dst,
                                           in_=stgt[:, (ip % NSTG) * 512:(ip % NSTG) * 512 + 512])
                    dmop = P.add("sync", f, kind="dma", name=f"{st}qD{gi}_{bq}",
                                 reads=(R_stg[ip % NSTG],), writes=wres,
                                 semkey=("qs" if wres else "kin"))
                    (k_dma if dsto >= 8 else q_dma).append(dmop)
                    ip += 1

            def f(e):
                return e.collective_compute(
                    "AllGather", ALU.bypass, replica_groups=RG2,
                    ins=[kin.opt()], outs=[dr[f"{st}k_out"].opt()])
            cc_k = P.add("gpsimd", f, deps=k_dma, kind="cc", name=f"{st}cck", semkey="cck")

            for vh in range(nvh):
                def f(e, vh=vh):
                    wn, base = vsrc
                    src = dr[wn].rearrange("(c p) o -> p c o", p=128)[
                        :, :, base + vh * 512: base + (vh + 1) * 512]
                    return e.dma_start(out=c3(wvw_t[:, :], 512), in_=src)
                wvop = P.add("gpsimd", f, kind="dma", name=f"{st}wv{vh}",
                             writes=(R_wvw,))
                for tc in range(NT):
                    pp, off, Rp = pslot(ip % 8)

                    def f(e, tc=tc, pp=pp, off=off):
                        last = None
                        for c in range(PC):
                            last = e.matmul(
                                pp[:, off:off + 512],
                                z_t[:, c * T + tc * 128: c * T + (tc + 1) * 128],
                                wvw_t[:, c * 512:(c + 1) * 512],
                                start=(c == 0), stop=(c == PC - 1))
                        return last
                    P.add("tensor", f, deps=[wvop], name=f"{st}vM{vh}_{tc}",
                          reads=(R_wvw,) + tuple(R_z), writes=(Rp,))

                    def f(e, ip=ip, pp=pp, off=off):
                        return e.activation(stgt[:, (ip % NSTG) * 512:(ip % NSTG) * 512 + 512],
                                            pp[:, off:off + 512], AF.Copy)
                    P.add("scalar", f, name=f"{st}vE{vh}_{tc}",
                          reads=(Rp,), writes=(R_stg[ip % NSTG],))

                    if st in "AC":
                        dst = vin.rearrange("(p t) d -> t p d", p=NP)[
                            tc * 128:(tc + 1) * 128, 4 * vh:4 * vh + 4, :]
                        npd = 4
                    else:
                        dst = vin.rearrange("(p t) d -> t p d", p=NP)[
                            tc * 128:(tc + 1) * 128, :, :]
                        npd = 8

                    def f(e, dst=dst, ip=ip, npd=npd):
                        src = stgt[:, (ip % NSTG) * 512:(ip % NSTG) * 512 + 512
                                   ].rearrange("t (p d) -> t p d", p=npd)
                        return e.dma_start(out=dst, in_=src)
                    v_dma.append(P.add("sync", f, kind="dma",
                                       name=f"{st}vD{vh}_{tc}",
                                       reads=(R_stg[ip % NSTG],),
                                       semkey="vin"))
                    ip += 1

            def f(e):
                return e.collective_compute(
                    "AllGather", ALU.bypass, replica_groups=RG2,
                    ins=[vin.opt()], outs=[dr[f"{st}v_out"].opt()])
            cc_v = P.add("gpsimd", f, deps=v_dma, kind="cc", name=f"{st}ccv", semkey="ccv")
            return cc_k, cc_v, q_dma

        def chunks_for(j, causal):
            out = []
            for g in range(32):
                s, tt = divmod(g, 16)
                seq = 2 * tt + s
                if seq < 8 * j:
                    out.append((g, None))
                elif seq < 8 * j + 8:
                    out.append((g, seq - 8 * j))
                elif not causal:
                    out.append((g, None))
            return out

        def emit_mha(st, cc_k, cc_v, q_ops, kvkv, R_kvkl, vt, R_vt, qpv,
                     R_qpl, prbv, R_prb, mskb, R_mskb):
            kout = dr[f"{st}k_out"]
            vout = dr[f"{st}v_out"]
            attn_dma = []

            def f(e):
                return e.dma_start(out=c3(mskb[:, :], 512),
                                   in_=dr["masks"].rearrange("d k q -> k d q"))
            mop = P.add("sync", f, kind="dma", name=f"{st}mld", writes=(R_mskb,))

            def f(e):
                return e.memset(vt[:, :], 1.0)
            vtones = P.add("vector", f, name=f"{st}vt1", writes=(R_vt,))

            Rav = (R_ps[4], R_ps[5])
            for p in range(NP):
                psl = p % 2
                kvk = kvkv[psl]
                R_kvk = R_kvkl[psl]
                qp = qpv[psl]
                R_qp = R_qpl[psl]

                def f(e, p=p, kvk=kvk):
                    return [e.dma_start(
                        out=kvk[:, s * 2048:(s + 1) * 2048],
                        in_=kout[s * C + p * 128: s * C + (p + 1) * 128, :])
                        for s in range(2)]
                kld = P.add("sync", f, deps=[cc_k], kind="dma",
                            name=f"{st}kld{p}", writes=(R_kvk,), n_incs=2)

                def f(e, p=p):
                    insts = []
                    for s in range(2):
                        for h in range(2):
                            base = s * 16 * 130 + h * 65
                            dst = vt[:, base:base + 15 * 130 + 65].rearrange(
                                "k (g w) -> k g w", w=65)[:, 0::2, :][:, :, 0:64]
                            src = vout[s * NP * T + p * T: s * NP * T + (p + 1) * T,
                                       h * 64:(h + 1) * 64
                                       ].rearrange("(g k) d -> k g d", k=128)
                            insts.append(e.dma_start(out=dst, in_=src))
                    return insts
                vld = P.add("sync", f, deps=[cc_v, vtones], kind="dma",
                            name=f"{st}vld{p}", writes=(R_vt,), n_incs=4)

                def f(e, p=p, qp=qp):
                    return e.dma_start(out=qp[:, :],
                                       in_=dr["qs"][p * 128:(p + 1) * 128, :])
                qld = P.add("sync", f, deps=q_ops, kind="dma",
                            name=f"{st}qld{p}", reads=(R_qs,), writes=(R_qp,))

                for j in range(NB):
                    sched = chunks_for(j, True)
                    nch = len(sched)
                    for ci, (g, d) in enumerate(sched):
                        pp = PPs[ci % 2]
                        Rpa, Rpb = R_ps[(ci % 2) * 2], R_ps[(ci % 2) * 2 + 1]

                        def f(e, g=g, j=j, pp=pp, kvk=kvk, qp=qp):
                            e.matmul(pp[:, 0:512], kvk[0:64, g * 128:(g + 1) * 128],
                                     qp[0:64, j * TOKB:(j + 1) * TOKB],
                                     start=True, stop=True, tile_position=(0, 0))
                            return e.matmul(
                                pp[:, 512:1024], kvk[64:128, g * 128:(g + 1) * 128],
                                qp[64:128, j * TOKB:(j + 1) * TOKB],
                                start=True, stop=True, tile_position=(64, 0))
                        scop = P.add("tensor", f, deps=[kld, qld],
                                     name=f"{st}sc{p}_{j}_{ci}",
                                     reads=(R_kvk, R_qp), writes=(Rpa, Rpb))

                        prs = ci % 2
                        R_pr = R_prb[prs]

                        def f(e, pp=pp, prs=prs):
                            return e.activation(
                                prbv[prs][:, :],
                                pp[:, 0:1024], AF.Exp, bias=negC[:, 0:1],
                                scale=0.125)
                        exop = P.add("scalar", f, deps=[scop],
                                     name=f"{st}ex{p}_{j}_{ci}",
                                     reads=(Rpa, Rpb, R_rconst), writes=(R_pr,))

                        if d is not None:
                            def f(e, prs=prs, d=d):
                                pr = prbv[prs][:, :]
                                mk = mskb[:, d * 512:(d + 1) * 512]
                                e.tensor_tensor(pr[:, 0:512], pr[:, 0:512], mk,
                                                ALU.mult)
                                return e.tensor_tensor(pr[:, 512:1024],
                                                       pr[:, 512:1024], mk,
                                                       ALU.mult)
                            P.add("vector", f, deps=[exop, mop],
                                  name=f"{st}mk{p}_{j}_{ci}",
                                  reads=(R_pr, R_mskb), writes=(R_pr,))

                        gb = (g // 16) * 16 * 130 + (g % 16) * 130

                        def f(e, gb=gb, prs=prs, first=(ci == 0),
                              last=(ci == nch - 1)):
                            pr = prbv[prs][:, :]
                            e.matmul(PP2[0:65, 0:512], vt[:, gb:gb + 65],
                                     pr[:, 0:512], start=first, stop=last)
                            return e.matmul(PP2[0:65, 512:1024],
                                            vt[:, gb + 65:gb + 130],
                                            pr[:, 512:1024], start=first,
                                            stop=last)
                        avop = P.add("tensor", f, deps=[vld],
                                     name=f"{st}av{p}_{j}_{ci}",
                                     reads=(R_pr, R_vt),
                                     writes=Rav if ci == 0 else ())
                        if ci > 0:
                            for rr in Rav:
                                rr.writers.append(avop)
                        last_av = avop

                    def f(e):
                        e.tensor_copy(dnst[0:1, 0:512], PP2[64:65, 0:512])
                        e.tensor_copy(dnst[0:1, 512:1024], PP2[64:65, 512:1024])
                        e.drain()
                        e.reciprocal(dnst[:, :], dnst[:, :])
                        e.drain()
                        e.tensor_copy(dnsb[:, 0:1024], dnst[:, :])
                        e.drain()
                        return e.tensor_tensor(dnsb[:, 1024:2048], dnst[:, :],
                                               dnsb[:, 0:1024], ALU.subtract)
                    dnop = P.add("vector", f, deps=[last_av],
                                 name=f"{st}dn{p}_{j}", reads=Rav,
                                 writes=(R_dns,))

                    pR, oR, RR = pslot(6)

                    def f(e, pR=pR, oR=oR):
                        e.matmul(pR[:, oR:oR + 512], sel[0:1, 0:128],
                                 dnsb[:, 0:512], start=True, stop=False)
                        e.matmul(pR[:, oR:oR + 512], sel[0:1, 128:256],
                                 dnsb[:, 512:1024], start=False, stop=False)
                        e.matmul(pR[:, oR:oR + 512], sel[0:1, 0:128],
                                 dnsb[:, 1024:1536], start=False, stop=False)
                        return e.matmul(pR[:, oR:oR + 512], sel[0:1, 128:256],
                                        dnsb[:, 1536:2048], start=False,
                                        stop=True)
                    rdop = P.add("tensor", f, deps=[dnop, op_consts],
                                 name=f"{st}rd{p}_{j}", reads=(R_dns,),
                                 writes=(RR,))

                    def f(e, pR=pR, oR=oR):
                        return e.activation(rdbs[:, :], pR[:, oR:oR + 512],
                                            AF.Copy)
                    rcop = P.add("scalar", f, name=f"{st}rc{p}_{j}",
                                 reads=(RR,), writes=(R_rdbs,))

                    sslot = (p * NB + j) % NSTG

                    def f(e, sslot=sslot):
                        sv = stgt[:, sslot * 512:(sslot + 1) * 512]
                        e.tensor_tensor(sv[0:64, :], PP2[0:64, 0:512],
                                        rdbs[0:64, :], ALU.mult)
                        return e.tensor_tensor(sv[64:128, :],
                                               PP2[0:64, 512:1024],
                                               rdbs[64:128, :], ALU.mult)
                    oop = P.add("vector", f, deps=[rcop], name=f"{st}ao{p}_{j}",
                                reads=Rav + (R_rdbs,), writes=(R_stg[sslot],))

                    def f(e, p=p, j=j, sslot=sslot):
                        return e.dma_start(
                            out=dr["attns"][p * 128:(p + 1) * 128,
                                            j * TOKB:(j + 1) * TOKB],
                            in_=stgt[:, sslot * 512:(sslot + 1) * 512])
                    attn_dma.append(P.add("sync", f, kind="dma",
                                          name=f"{st}aod{p}_{j}",
                                          reads=(R_stg[sslot],),
                                          writes=(R_attns,)))
            return attn_dma

        def emit_proj(st, attst, R_attst, pjw, R_pjw):
            def f(e):
                return [e.dma_start(out=c3(pjw[:, o * 1024:(o + 1) * 1024]),
                                    in_=gran_w(dr["wproj"], o))
                        for o in range(PC)]
            wop = P.add("gpsimd", f, kind="dma", name=f"{st}pjw",
                        writes=(R_pjw,), n_incs=PC)
            for bq in range(NB):
                def f(e, bq=bq):
                    return [e.dma_start(
                        out=attst[:, c * 512:(c + 1) * 512],
                        in_=dr["attns"][c * 128:(c + 1) * 128,
                                        bq * TOKB:(bq + 1) * TOKB])
                        for c in range(PC)]
                lop = P.add("sync", f, kind="dma", name=f"{st}pjl{bq}",
                            reads=(R_attns,), writes=(R_attst,), n_incs=PC)
                for o in range(PC):
                    pp, off, Rp = pslot((bq * PC + o) % 8)

                    def f(e, o=o, pp=pp, off=off):
                        last = None
                        for c in range(PC):
                            last = e.matmul(
                                pp[:, off:off + 512],
                                pjw[:, o * 1024 + c * 128: o * 1024 + (c + 1) * 128],
                                attst[:, c * 512:(c + 1) * 512],
                                start=(c == 0), stop=(c == PC - 1))
                        return last
                    mmop = P.add("tensor", f, deps=[wop, lop],
                                 name=f"{st}pjM{bq}_{o}",
                                 reads=(R_pjw, R_attst), writes=(Rp,))

                    def f(e, o=o, bq=bq, pp=pp, off=off):
                        sl = slice(o * T + bq * TOKB, o * T + (bq + 1) * TOKB)
                        return e.tensor_tensor(accb[:, sl], accb[:, sl],
                                               pp[:, off:off + 512], ALU.add)
                    P.add("vector", f, name=f"{st}pjA{bq}_{o}",
                          reads=(Rp, R_acc[o]), writes=(R_acc[o],))

        def emit_fold(dst_t, R_dst, src_t, R_src, scale, tag):
            for c in range(PC):
                def f(e, c=c):
                    last = None
                    for bq in range(NB):
                        sl = slice(c * T + bq * TOKB, c * T + (bq + 1) * TOKB)
                        last = e.scalar_tensor_tensor(dst_t[:, sl], src_t[:, sl],
                                                      scale, dst_t[:, sl],
                                                      ALU.mult, ALU.add)
                    return last
                P.add("vector", f, name=f"{tag}{c}",
                      reads=(R_src[c], R_dst[c]), writes=(R_dst[c],))

        def emit_router(st, z_t, R_z, rww, R_rww, rwwf, hilo, R_hilo):
            def f(e):
                return e.dma_start(
                    out=c3(rwwf[:, :], 8),
                    in_=dr["wrout"].rearrange("(c p) o -> p c o", p=128))
            wop = P.add("gpsimd", f, kind="dma", name=f"{st}rww",
                        writes=(R_rww,))

            def f(e):
                last = None
                for c in range(PC):
                    last = e.tensor_scalar(rww[:, c * 8:(c + 1) * 8],
                                           rwwf[:, c * 8:(c + 1) * 8],
                                           l2w[:, c:c + 1], None, ALU.mult)
                return last
            wop = P.add("vector", f, deps=[wop, op_consts], name=f"{st}rwf",
                        reads=(R_rww, R_rconst), writes=(R_rww,))
            for tc in range(NT):
                pp, off, Rp = pslot(tc % 4)

                def f(e, tc=tc):
                    last = None
                    for c in range(PC):
                        sl = slice(c * T + tc * 128, c * T + (tc + 1) * 128)
                        e.activation(xcpb[:, 0:128], z_t[:, sl], AF.Copy)
                        last = e.tensor_tensor(xcpb[:, 128:256], z_t[:, sl],
                                               xcpb[:, 0:128], ALU.subtract)
                    return last
                def fmm(e, tc=tc, pp=pp, off=off):
                    last = None
                    for c in range(PC):
                        last = e.matmul(
                            pp[:, off:off + 8],
                            hilo[:, c * 256:c * 256 + 128],
                            rww[:, c * 8:(c + 1) * 8],
                            start=(c == 0), stop=False)
                        last = e.matmul(
                            pp[:, off:off + 8],
                            hilo[:, c * 256 + 128:c * 256 + 256],
                            rww[:, c * 8:(c + 1) * 8],
                            start=False, stop=(c == PC - 1))
                    return last

                def fhl(e, tc=tc):
                    last = None
                    for c in range(PC):
                        sl = slice(c * T + tc * 128, c * T + (tc + 1) * 128)
                        e.activation(hilo[:, c * 256:c * 256 + 128], z_t[:, sl],
                                     AF.Copy)
                        last = e.activation(hilo[:, c * 256 + 128:c * 256 + 256],
                                            z_t[:, sl], AF.Copy, scale=-1.0,
                                            bias=None) if False else last
                    return last
                def fhi(e, tc=tc):
                    last = None
                    for c in range(PC):
                        sl = slice(c * T + tc * 128, c * T + (tc + 1) * 128)
                        last = e.activation(hilo[:, c * 256:c * 256 + 128],
                                            z_t[:, sl], AF.Copy)
                    return last
                hiop = P.add("scalar", fhi, name=f"{st}rHi{tc}",
                             reads=tuple(R_z), writes=(R_hilo,))

                def flo(e, tc=tc):
                    last = None
                    for c in range(PC):
                        sl = slice(c * T + tc * 128, c * T + (tc + 1) * 128)
                        last = e.tensor_tensor(
                            hilo[:, c * 256 + 128:c * 256 + 256], z_t[:, sl],
                            hilo[:, c * 256:c * 256 + 128], ALU.subtract)
                    return last
                loop_ = P.add("vector", flo, deps=[hiop], name=f"{st}rLo{tc}",
                              reads=tuple(R_z) + (R_hilo,), writes=(R_hilo,))

                mmop = P.add("tensor", fmm, deps=[wop, loop_],
                             name=f"{st}rM{tc}",
                             reads=(R_rww, R_hilo), writes=(Rp,))

                def f(e, pp=pp, off=off):
                    e.tensor_copy(r8[:, :], pp[:, off:off + 8])
                    e.drain()
                    return e.reduce_max(r1a[:, :], r8[:, :], AXX, negate=True)
                v1 = P.add("vector", f, deps=[mmop], name=f"{st}rV1{tc}",
                           reads=(Rp,), writes=(R_r8, R_r1))

                def f(e):
                    return e.activation(r8[:, :], r8[:, :], AF.Exp,
                                        bias=r1a[:, 0:1])
                s1 = P.add("scalar", f, deps=[v1], name=f"{st}rS{tc}",
                           reads=(R_r8, R_r1))
                R_r8.writers.append(s1)

                def f(e):
                    e.reduce_max(r1a[:, :], r8[:, :], AXX)
                    e.drain()
                    e.tensor_scalar(r8b[:, :], r8[:, :], r1a[:, 0:1], None,
                                    ALU.is_lt)
                    e.drain()
                    e.tensor_tensor(r8b[:, :], r8b[:, :], r8[:, :], ALU.mult)
                    e.drain()
                    e.reduce_max(r1b[:, :], r8b[:, :], AXX)
                    e.drain()
                    e.tensor_scalar(r8b[:, :], r8[:, :], r1b[:, 0:1], None,
                                    ALU.is_ge)
                    e.drain()
                    e.tensor_tensor(r8b[:, :], r8b[:, :], r8[:, :], ALU.mult)
                    e.drain()
                    e.reduce_sum(r1b[:, :], r8b[:, :], AXX)
                    e.drain()
                    e.reciprocal(r1b[:, :], r1b[:, :])
                    e.drain()
                    return e.tensor_scalar(rwt[:, :], r8b[:, :], r1b[:, 0:1],
                                           0.5, ALU.mult, ALU.mult)
                v2 = P.add("vector", f, deps=[s1], name=f"{st}rV2{tc}",
                           reads=(R_r8,), writes=(R_r8, R_rwt))

                pT, oT, RT = pslot(5)

                def f(e, pT=pT, oT=oT):
                    return e.transpose(pT[0:8, oT:oT + 128], rwt[:, :], idn[:, :])
                top = P.add("tensor", f, deps=[v2, op_consts],
                            name=f"{st}rT{tc}", reads=(R_rconst, R_rwt),
                            writes=(RT,))

                def f(e, tc=tc, pT=pT, oT=oT):
                    return e.tensor_copy(rwT[0:8, tc * 128:(tc + 1) * 128],
                                         pT[0:8, oT:oT + 128])
                P.add("vector", f, deps=[top], name=f"{st}rC{tc}",
                      reads=(RT,), writes=(R_rwT,))

        def emit_moe(st, z2t, R_z2t, w1v, R_w1r, w2v, R_w2r, hsv, R_hs,
                     oh8, R_oh8, first_cast):
            def f(e):
                return e.dma_start(out=oh8[0:8, :], in_=dr["oh8"][:, :])
            ohop = P.add("sync", f, kind="dma", name=f"{st}oh8",
                         writes=(R_oh8,))
            N1 = len(R_w1r)
            N2 = len(R_w2r)
            i1 = 0
            i2 = 0
            for bq in range(NB):
                def f(e, bq=bq):
                    return [e.dma_start(
                        out=z2t[:, c * 512:(c + 1) * 512],
                        in_=dr["z2s"][c * 128:(c + 1) * 128,
                                      bq * TOKB:(bq + 1) * TOKB])
                        for c in range(PC)]
                zld = P.add("sync", f, kind="dma", name=f"{st}mZ{bq}",
                            reads=(R_z2s,), writes=(R_z2t,), n_incs=PC)
                for ei in range(NE):
                    pW, oW, RW = pslot(6)

                    def f(e, ei=ei, bq=bq, pW=pW, oW=oW):
                        return e.matmul(pW[:, oW:oW + 512],
                                        oh8[0:8, ei * 128:(ei + 1) * 128],
                                        rwT[0:8, bq * TOKB:(bq + 1) * TOKB],
                                        start=True, stop=True)
                    wbop = P.add("tensor", f, deps=[ohop],
                                 name=f"{st}wb{bq}_{ei}",
                                 reads=(R_rwT, R_oh8), writes=(RW,))

                    def f(e, pW=pW, oW=oW):
                        return e.activation(wbs[:, :], pW[:, oW:oW + 512],
                                            AF.Copy)
                    wbsop = P.add("scalar", f, name=f"{st}wbs{bq}_{ei}",
                                  reads=(RW,), writes=(R_wbs,))

                    h_ops = []
                    for fi in range(NF):
                        slot = i1 % N1
                        cast = first_cast and bq == 0
                        srcw = dr["w1" if cast else "w1b"]
                        srcap = srcw.rearrange("(c p) o -> p c o", p=128)[
                            :, :, ei * FF + fi * 128: ei * FF + (fi + 1) * 128]

                        def f(e, srcap=srcap, slot=slot):
                            return e.dma_start(out=c3(w1v[slot]), in_=srcap)
                        w1l = P.add("gpsimd" if cast else "sync", f, kind="dma",
                                    name=f"{st}w1L{bq}_{ei}_{fi}",
                                    reads=() if cast else (R_w1b,),
                                    writes=(R_w1r[slot],))
                        if cast:
                            dstap = dr["w1b"].rearrange("(c p) o -> p c o", p=128)[
                                :, :, ei * FF + fi * 128: ei * FF + (fi + 1) * 128]

                            def f(e, dstap=dstap, slot=slot):
                                return e.dma_start(out=dstap, in_=c3(w1v[slot]))
                            P.add("sync", f, deps=[w1l], kind="dma",
                                  name=f"{st}w1B{ei}_{fi}",
                                  reads=(R_w1r[slot],), writes=(R_w1b,))

                        pH, oH, RH = pslot(fi % 2)

                        def f(e, slot=slot, pH=pH, oH=oH):
                            last = None
                            for c in range(PC):
                                last = e.matmul(
                                    pH[:, oH:oH + 512],
                                    w1v[slot][:, c * 128:(c + 1) * 128],
                                    z2t[:, c * 512:(c + 1) * 512],
                                    start=(c == 0), stop=(c == PC - 1))
                            return last
                        P.add("tensor", f, deps=[w1l, zld],
                              name=f"{st}hM{bq}_{ei}_{fi}",
                              reads=(R_w1r[slot], R_z2t), writes=(RH,))

                        def f(e, fi=fi, pH=pH, oH=oH):
                            return e.activation(hsv[fi][:, :],
                                                pH[:, oH:oH + 512], AF.Gelu)
                        P.add("scalar", f, name=f"{st}g{bq}_{ei}_{fi}",
                              reads=(RH,), writes=(R_hs[fi],))

                        def f(e, fi=fi):
                            sl = hsv[fi][:, :]
                            return e.tensor_tensor(sl, sl, wbs[:, :], ALU.mult)
                        h_ops.append(P.add("vector", f, deps=[wbsop],
                                           name=f"{st}hs{bq}_{ei}_{fi}",
                                           reads=(R_hs[fi], R_wbs),
                                           writes=(R_hs[fi],)))
                        i1 += 1

                    for co in range(PC):
                        slot = i2 % N2
                        cast = first_cast and bq == 0
                        srcw = dr["w2" if cast else "w2b"]
                        srcap = srcw.rearrange("(f p) o -> p f o", p=128)[
                            :, ei * NF:(ei + 1) * NF, co * 128:(co + 1) * 128]

                        def f(e, srcap=srcap, slot=slot):
                            return e.dma_start(out=c3(w2v[slot]), in_=srcap)
                        w2l = P.add("gpsimd" if cast else "sync", f, kind="dma",
                                    name=f"{st}w2L{bq}_{ei}_{co}",
                                    reads=() if cast else (R_w2b,),
                                    writes=(R_w2r[slot],))
                        if cast:
                            dstap = dr["w2b"].rearrange("(f p) o -> p f o", p=128)[
                                :, ei * NF:(ei + 1) * NF, co * 128:(co + 1) * 128]

                            def f(e, dstap=dstap, slot=slot):
                                return e.dma_start(out=dstap, in_=c3(w2v[slot]))
                            P.add("sync", f, deps=[w2l], kind="dma",
                                  name=f"{st}w2B{ei}_{co}",
                                  reads=(R_w2r[slot],), writes=(R_w2b,))

                        pY, oY, RY = pslot(2 + co % 2)

                        def f(e, slot=slot, pY=pY, oY=oY):
                            last = None
                            for fi in range(NF):
                                last = e.matmul(
                                    pY[:, oY:oY + 512],
                                    w2v[slot][:, fi * 128:(fi + 1) * 128],
                                    hsv[fi][:, :],
                                    start=(fi == 0), stop=(fi == NF - 1))
                            return last
                        ymm = P.add("tensor", f, deps=[w2l] + h_ops,
                                    name=f"{st}yM{bq}_{ei}_{co}",
                                    reads=(R_w2r[slot],) + tuple(R_hs),
                                    writes=(RY,))

                        def f(e, co=co, bq=bq, pY=pY, oY=oY):
                            sl = slice(co * T + bq * TOKB, co * T + (bq + 1) * TOKB)
                            return e.tensor_tensor(x[:, sl], x[:, sl],
                                                   pY[:, oY:oY + 512], ALU.add)
                        P.add("vector", f, name=f"{st}yA{bq}_{ei}_{co}",
                              reads=(RY, R_x[co]), writes=(R_x[co],))
                        i2 += 1

        def emit_ffn_stage(st, first):
            arena.reset()
            z_views, R_z = arena.al_multi(PC, T, "z")
            z_t = arn_t[:, 0:PC * T]
            wrv, R_wring = arena.al_multi(2, 1024, "wr")
            wvw_t, R_wvw = arena.al2(PC * 512, "wvw")

            emit_ln(x, R_x, None, accb, R_acc, None, None, st + "L0")
            if st == "A":
                emit_tap("yA", accb, R_acc)
            emit_ln(accb, R_acc, l1w, None, None, z_t, R_z, st + "L1")

            if st == "A" and "z1" in taps:
                def f(e):
                    return [e.dma_start(out=dr["tap_z1"][c * 128:(c + 1) * 128, :],
                                        in_=z_t[:, c * T:(c + 1) * T])
                            for c in range(PC)]
                P.add("gpsimd", f, kind="dma", name="tap_z1",
                      reads=tuple(R_z), n_incs=PC, semkey="tap")
            cc_k, cc_v, q_ops = emit_qkv(st, z_t, R_z, wrv, R_wring,
                                         wvw_t, R_wvw)

            arena.reset()
            kvkv, R_kvkl = arena.al_multi(2, 2 * 2048, "kvk")
            vt, R_vt = arena.al2(2 * 16 * 130, "vt")
            qpv, R_qpl = arena.al_multi(2, 2048, "qp")
            prbv, R_prb = arena.al_multi(2, 1024, "pr")
            mskb, R_mskb = arena.al2(8 * 512, "mskb")
            ad_ops = emit_mha(st, cc_k, cc_v, q_ops, kvkv, R_kvkl, vt, R_vt,
                              qpv, R_qpl, prbv, R_prb, mskb, R_mskb)
            if st == "A" and "Bq" in taps:
                def f(e):
                    insts = [e.dma_start(out=dr["tap_Bq"][0:1024, :],
                                         in_=dr["qs"][:, :])]
                    if "Badot" in taps:
                        insts.append(e.dma_start(out=dr["tap_Badot"][0:1024, :],
                                                 in_=dr["Ak_in"][:, :]))
                    return insts
                P.add("gpsimd", f, deps=q_ops, kind="dma", name="tap_q",
                      reads=(R_qs,), n_incs=2 if "Badot" in taps else 1,
                      semkey="tap")
            if st == "A" and "attn_heads" in taps:
                def f(e):
                    return e.dma_start(out=dr["tap_attn_heads"][:, :],
                                       in_=dr["attns"][:, :])
                P.add("gpsimd", f, deps=ad_ops, kind="dma", name="tap_ah",
                      reads=(R_attns,), semkey="tap")

            arena.reset()
            attst, R_attst = arena.al2(PC * 512, "attst")
            pjw, R_pjw = arena.al2(PC * 1024, "pjw")
            emit_proj(st, attst, R_attst, pjw, R_pjw)
            if st == "A":
                emit_tap("a", accb, R_acc)

            emit_fold(x, R_x, accb, R_acc, 0.5, st + "fold")
            arena.reset()
            z2_views, R_z2 = arena.al_multi(PC, T, "z2")
            z2_t = arn_t[:, 0:PC * T]
            rww, R_rww = arena.al2(PC * 8, "rww")
            hilo, R_hilo = arena.al2(PC * 256, "hilo")
            rwwf, R_rwwf = arena.al2(PC * 8 * 2, "rwwf", F32)
            emit_ln(accb, R_acc, l2w, accb, R_acc, z2_t, R_z2, st + "L2")

            def f(e):
                return [e.dma_start(out=dr["z2s"][c * 128:(c + 1) * 128, :],
                                    in_=z2_t[:, c * T:(c + 1) * T])
                        for c in range(PC)]
            P.add("sync", f, kind="dma", name=f"{st}z2d",
                  reads=tuple(R_z2), writes=(R_z2s,), n_incs=PC)

            emit_router(st, accb, R_acc, rww, R_rww, rwwf, hilo, R_hilo)

            arena.reset()
            z2t, R_z2t = arena.al2(PC * 512, "z2t")
            w1v, R_w1r = arena.al_multi(4, 1024, "w1r")
            w2v, R_w2r = arena.al_multi(2, 2048, "w2r")
            hsv, R_hs = arena.al_multi(NF, 512, "hs")
            oh8t, R_oh8 = arena.al2(8 * 128, "oh8")
            emit_moe(st, z2t, R_z2t, w1v, R_w1r, w2v, R_w2r, hsv, R_hs,
                     oh8t, R_oh8, first)

        def emit_stageB():
            arena.reset()
            z_views, R_z = arena.al_multi(PC, T, "zB")
            z_t = arn_t[:, 0:PC * T]
            wrv, R_wring = arena.al_multi(2, 1024, "wrB")
            wvw_t, R_wvw = arena.al2(PC * 512, "wvwB")

            emit_ln(x, R_x, None, None, None, z_t, R_z, "BL0")
            cc_k, cc_v, q_ops = emit_qkv("B", z_t, R_z, wrv, R_wring,
                                         wvw_t, R_wvw)

            arena.reset()
            memsb, R_memsb = arena.al2(16 * 128, "Bmem")
            sigv, R_sig = arena.al_multi(4, 512, "sig")
            qtlv, R_qtl = arena.al_multi(2, 512, "qt")

            def f(e):
                return e.dma_start(
                    out=c3(memsb[:, :], 512),
                    in_=dr["mem"].rearrange("(ci p) (co q) -> p ci (co q)",
                                            p=128, q=128))
            memop = P.add("gpsimd", f, kind="dma", name="BmemL",
                          writes=(R_memsb,))

            for gq in range(2):
                for bq in range(NB):
                    elu_ops = []
                    for c4 in range(4):
                        cc_ = 4 * gq + c4
                        qslot = c4 % 2

                        def f(e, cc_=cc_, bq=bq, qslot=qslot):
                            return e.dma_start(
                                out=qtlv[qslot][:, :],
                                in_=dr["qs"][cc_ * 128:(cc_ + 1) * 128,
                                             bq * TOKB:(bq + 1) * TOKB])
                        qldo = P.add("sync", f, deps=q_ops, kind="dma",
                                     name=f"BqL{gq}_{bq}_{c4}", reads=(R_qs,),
                                     writes=(R_qtl[qslot],))

                        def f(e, qslot=qslot):
                            qv = qtlv[qslot][:, :]
                            return e.tensor_scalar_min(tmpv[:, :], qv, 0.0)
                        v1 = P.add("vector", f, deps=[qldo],
                                   name=f"Bel1{gq}_{bq}_{c4}",
                                   reads=(R_qtl[qslot],), writes=(R_tmp,))

                        def f(e):
                            return e.activation(tmpv2[:, :], tmpv[:, :], AF.Exp)
                        s1 = P.add("scalar", f, deps=[v1],
                                   name=f"Bel2{gq}_{bq}_{c4}",
                                   reads=(R_tmp,), writes=(R_tmp2,))

                        def f(e, qslot=qslot, c4=c4):
                            qv = qtlv[qslot][:, :]
                            e.tensor_scalar_max(tmpv[:, :], qv, 0.0)
                            e.drain()
                            return e.scalar_tensor_tensor(
                                sigv[c4][:, :], tmpv2[:, :],
                                -1.0, tmpv[:, :], ALU.add, ALU.add)
                        elu_ops.append(P.add(
                            "vector", f, deps=[s1],
                            name=f"Bel3{gq}_{bq}_{c4}",
                            reads=(R_qtl[qslot], R_tmp2),
                            writes=(R_sig[c4], R_tmp)))

                    pD, oD, RD = pslot(6)

                    def f(e, pD=pD, oD=oD):
                        last = None
                        for c4 in range(4):
                            last = e.matmul(pD[0:1, oD:oD + 512],
                                            mnormb[:, c4:c4 + 1],
                                            sigv[c4][:, :],
                                            start=(c4 == 0), stop=(c4 == 3))
                        return last
                    dmm = P.add("tensor", f, deps=elu_ops + [op_consts],
                                name=f"Bden{gq}_{bq}",
                                reads=tuple(R_sig), writes=(RD,))

                    def f(e, pD=pD, oD=oD):
                        e.tensor_copy(dnst[0:1, 0:512], pD[0:1, oD:oD + 512])
                        e.drain()
                        e.reciprocal(dnst[0:1, 0:512], dnst[0:1, 0:512])
                        e.drain()
                        e.tensor_copy(dnsb[0:1, 0:512], dnst[0:1, 0:512])
                        e.drain()
                        return e.tensor_tensor(dnsb[0:1, 512:1024],
                                               dnst[0:1, 0:512], dnsb[0:1, 0:512],
                                               ALU.subtract)
                    dnop = P.add("vector", f, deps=[dmm], name=f"Bdn{gq}_{bq}",
                                 reads=(RD,), writes=(R_dns,))

                    pR, oR, RR = pslot(7)

                    def f(e, pR=pR, oR=oR):
                        e.matmul(pR[:, oR:oR + 512], onw[:, :],
                                 dnsb[0:1, 0:512], start=True, stop=False)
                        return e.matmul(pR[:, oR:oR + 512], onw[:, :],
                                        dnsb[0:1, 512:1024], start=False,
                                        stop=True)
                    rdop = P.add("tensor", f, deps=[dnop, op_selw],
                                 name=f"Brd{gq}_{bq}", reads=(R_dns,),
                                 writes=(RR,))

                    def f(e, pR=pR, oR=oR):
                        return e.activation(rdbs[:, :], pR[:, oR:oR + 512],
                                            AF.Copy)
                    rcop = P.add("scalar", f, name=f"Brc{gq}_{bq}",
                                 reads=(RR,), writes=(R_rdbs,))

                    for kvo in range(4):
                        pN, oN, RN = pslot(4 + kvo % 2)

                        def f(e, gq=gq, kvo=kvo, pN=pN, oN=oN):
                            last = None
                            for c4 in range(4):
                                mcol = (c4 * 4 + kvo) * 128
                                last = e.matmul(
                                    pN[:, oN:oN + 512],
                                    memsb[:, mcol:mcol + 128],
                                    sigv[c4][:, :],
                                    start=(c4 == 0), stop=(c4 == 3))
                            return last
                        nmm = P.add("tensor", f, deps=[memop] + elu_ops,
                                    name=f"BnM{gq}_{bq}_{kvo}",
                                    reads=(R_memsb,) + tuple(R_sig),
                                    writes=(RN,))

                        def f(e, gq=gq, bq=bq, kvo=kvo, pN=pN, oN=oN):
                            co = 4 * gq + kvo
                            sl = slice(co * T + bq * TOKB,
                                       co * T + (bq + 1) * TOKB)
                            e.tensor_tensor(tmpv[:, :], pN[:, oN:oN + 512],
                                            rdbs[:, :], ALU.mult)
                            e.drain()
                            return e.tensor_tensor(x[:, sl], x[:, sl],
                                                   tmpv[:, :], ALU.add)
                        P.add("vector", f, deps=[rcop, nmm],
                              name=f"BnA{gq}_{bq}_{kvo}",
                              reads=(RN, R_rdbs, R_x[4 * gq + kvo]),
                              writes=(R_x[4 * gq + kvo], R_tmp))

            arena.reset()
            kdv, R_kdl = arena.al_multi(2, 2 * 2048, "kd")
            vt, R_vt = arena.al2(2 * 16 * 65, "vtB")
            qpv, R_qpl = arena.al_multi(2, 2048, "qpB")
            prbv, R_prb = arena.al_multi(2, 1024, "prB")

            def f(e):
                return e.memset(vt[:, :], 1.0)
            vtones = P.add("vector", f, name="Bvt1", writes=(R_vt,))

            Rav = (R_ps[4], R_ps[5])
            for p in range(NP):
                psl = p % 2
                kd = kdv[psl]
                R_kd = R_kdl[psl]
                qp = qpv[psl]
                R_qp = R_qpl[psl]

                def f(e, p=p, kd=kd):
                    insts = []
                    for s in range(2):
                        src = dr["Bk_out"][s * 512 + (p // 2) * 128 + (p % 2) * 64:
                                           s * 512 + (p // 2) * 128 + (p % 2) * 64 + 64, :]
                        insts.append(e.dma_start(
                            out=kd[0:64, s * 2048:(s + 1) * 2048], in_=src))
                        insts.append(e.dma_start(
                            out=kd[64:128, s * 2048:(s + 1) * 2048], in_=src))
                    return insts
                kld = P.add("sync", f, deps=[cc_k], kind="dma",
                            name=f"Bkld{p}", writes=(R_kd,), n_incs=4)

                def f(e, p=p):
                    insts = []
                    for s in range(2):
                        dst = vt[:, s * 16 * 65:(s + 1) * 16 * 65].rearrange(
                            "k (g w) -> k g w", w=65)[:, :, 0:64]
                        src = dr["Bv_out"][s * NP * T + p * T: s * NP * T + (p + 1) * T, :
                                           ].rearrange("(g k) d -> k g d", k=128)
                        insts.append(e.dma_start(out=dst, in_=src))
                    return insts
                vld = P.add("sync", f, deps=[cc_v, vtones], kind="dma",
                            name=f"Bvld{p}", writes=(R_vt,), n_incs=2)

                def f(e, p=p, qp=qp):
                    return e.dma_start(out=qp[:, :],
                                       in_=dr["qs"][p * 128:(p + 1) * 128, :])
                qld = P.add("sync", f, deps=q_ops, kind="dma",
                            name=f"Bqld{p}", reads=(R_qs,), writes=(R_qp,))

                for j in range(NB):
                    for ci in range(32):
                        g = ci
                        pp = PPs[ci % 2]
                        Rpa, Rpb = R_ps[(ci % 2) * 2], R_ps[(ci % 2) * 2 + 1]

                        def f(e, g=g, j=j, pp=pp, kd=kd, qp=qp):
                            e.matmul(pp[:, 0:512], kd[0:64, g * 128:(g + 1) * 128],
                                     qp[0:64, j * TOKB:(j + 1) * TOKB],
                                     start=True, stop=True, tile_position=(0, 0))
                            return e.matmul(
                                pp[:, 512:1024], kd[64:128, g * 128:(g + 1) * 128],
                                qp[64:128, j * TOKB:(j + 1) * TOKB],
                                start=True, stop=True, tile_position=(64, 0))
                        scop = P.add("tensor", f, deps=[kld, qld],
                                     name=f"Bsc{p}_{j}_{ci}",
                                     reads=(R_kd, R_qp), writes=(Rpa, Rpb))

                        prs = ci % 2
                        R_pr = R_prb[prs]

                        def f(e, pp=pp, prs=prs):
                            return e.activation(
                                prbv[prs][:, :],
                                pp[:, 0:1024], AF.Exp, bias=negC[:, 0:1],
                                scale=0.125)
                        exop = P.add("scalar", f, deps=[scop],
                                     name=f"Bex{p}_{j}_{ci}",
                                     reads=(Rpa, Rpb, R_rconst), writes=(R_pr,))

                        gb = g * 65

                        def f(e, gb=gb, prs=prs, first=(ci == 0),
                              last=(ci == 31)):
                            pr = prbv[prs][:, :]
                            e.matmul(PP2[0:65, 0:512], vt[:, gb:gb + 65],
                                     pr[:, 0:512], start=first, stop=last)
                            return e.matmul(PP2[0:65, 512:1024],
                                            vt[:, gb:gb + 65],
                                            pr[:, 512:1024], start=first,
                                            stop=last)
                        avop = P.add("tensor", f, deps=[vld],
                                     name=f"Bav{p}_{j}_{ci}",
                                     reads=(R_pr, R_vt),
                                     writes=Rav if ci == 0 else ())
                        if ci > 0:
                            for rr in Rav:
                                rr.writers.append(avop)
                        last_av = avop

                    def f(e):
                        e.tensor_copy(dnst[0:1, 0:512], PP2[64:65, 0:512])
                        e.tensor_copy(dnst[0:1, 512:1024], PP2[64:65, 512:1024])
                        e.drain()
                        e.reciprocal(dnst[:, :], dnst[:, :])
                        e.drain()
                        e.tensor_copy(dnsb[:, 0:1024], dnst[:, :])
                        e.drain()
                        return e.tensor_tensor(dnsb[:, 1024:2048], dnst[:, :],
                                               dnsb[:, 0:1024], ALU.subtract)
                    dnop = P.add("vector", f, deps=[last_av],
                                 name=f"Bdn2{p}_{j}", reads=Rav,
                                 writes=(R_dns,))

                    pR, oR, RR = pslot(6)

                    def f(e, pR=pR, oR=oR):
                        e.matmul(pR[:, oR:oR + 512], selw[0:1, 0:128],
                                 dnsb[:, 0:512], start=True, stop=False)
                        e.matmul(pR[:, oR:oR + 512], selw[0:1, 128:256],
                                 dnsb[:, 512:1024], start=False, stop=False)
                        e.matmul(pR[:, oR:oR + 512], selw[0:1, 0:128],
                                 dnsb[:, 1024:1536], start=False, stop=False)
                        return e.matmul(pR[:, oR:oR + 512], selw[0:1, 128:256],
                                        dnsb[:, 1536:2048], start=False,
                                        stop=True)
                    rdop = P.add("tensor", f, deps=[dnop, op_selw],
                                 name=f"Brd2{p}_{j}", reads=(R_dns,),
                                 writes=(RR,))

                    def f(e, pR=pR, oR=oR):
                        return e.activation(rdbs[:, :], pR[:, oR:oR + 512],
                                            AF.Copy)
                    rcop = P.add("scalar", f, name=f"Brc2{p}_{j}",
                                 reads=(RR,), writes=(R_rdbs,))

                    def f(e, p=p, j=j):
                        sl = slice(p * T + j * TOKB, p * T + (j + 1) * TOKB)
                        e.tensor_tensor(tmpv[0:64, :], PP2[0:64, 0:512],
                                        rdbs[0:64, :], ALU.mult)
                        e.tensor_tensor(tmpv[64:128, :], PP2[0:64, 512:1024],
                                        rdbs[64:128, :], ALU.mult)
                        e.drain()
                        e.tensor_tensor(x[0:64, sl], x[0:64, sl],
                                        tmpv[0:64, :], ALU.add)
                        return e.tensor_tensor(x[64:128, sl], x[64:128, sl],
                                               tmpv[64:128, :], ALU.add)
                    P.add("vector", f, deps=[rcop], name=f"Bao{p}_{j}",
                          reads=Rav + (R_rdbs, R_x[p]),
                          writes=(R_x[p], R_tmp))

        if stages == "L":
            emit_ln(x, R_x, None, accb, R_acc, None, None, "L0")
            emit_tap("yA", accb, R_acc)
        if stages == "L3":
            emit_ln(x, R_x, None, accb, R_acc, None, None, "L0")
            def f(e):
                e.tensor_copy(accb[:, 0:512], aux_m[:, :])
                e.tensor_copy(accb[:, 512:1024], aux_v[:, :])
                e.tensor_copy(accb[:, 1024:1536], aux_r[:, :])
                e.tensor_copy(accb[:, 1536:2048].bitcast(BF16)[:, 0:512], xsqb[:, 0:512])
                e.tensor_copy(accb[:, 2048:2560].bitcast(BF16)[:, 0:512], xcpb[:, 0:512])
                return e.engine_nop()
            P.add("vector", f, deps=bar_deps(), name="dbg_aux",
                  writes=tuple(R_acc), reads=(R_aux,))
            emit_tap("yA", accb, R_acc)
        if stages == "L2":
            def f(e):
                for c in range(PC):
                    e.memset(accb[:, c * T:(c + 1) * T], 3.14)
                return e.engine_nop()
            mop_ = P.add("vector", f, name="dbg_memset", writes=tuple(R_acc))
            emit_tap("yA", accb, R_acc)
        if "A" in stages:
            emit_ffn_stage("A", first=True)
            emit_tap("x1", x, R_x)
        if "B" in stages:
            emit_stageB()
            emit_tap("x2", x, R_x)
        if "C" in stages:
            emit_ffn_stage("C", first=("A" not in stages))
            emit_tap("x3", x, R_x)

        def f(e):
            return [e.dma_start(out=dr["out_x"][c * 128:(c + 1) * 128, :],
                                in_=x[:, c * T:(c + 1) * T])
                    for c in range(PC)]
        outop = P.add("sync", f, kind="dma", name="xout",
                      reads=tuple(R_x), n_incs=PC, semkey="out")

        def f(e):
            return e.nop()
        P.add("sync", f, deps=[outop], name="terminal")

        P.emit(nc, block, es)
        rem = nc.sbuf_bytes_remaining
        nops = len(P.ops)
    return nc, {"sbuf_remaining": rem, "n_ops": nops}


def kernel(**inputs):
    nc, meta = build({"stages": "ABC", "taps": []})
    in_maps = shard_inputs(inputs)
    res = run_bass_kernel_spmd(nc, in_maps, core_ids=list(range(8)))
    return gather_output(res.results)


# version 03 (final)
# speedup vs baseline: 1.0797x; 1.0447x over previous
import bisect

ENGINES = ("tensor", "scalar", "vector", "gpsimd", "sync")


class Op:
    __slots__ = ("eng", "fn", "deps", "kind", "need_tok", "val", "name",
                 "n_incs", "pos", "semkey", "idx")

    def __init__(self, eng, fn, deps, kind, name, n_incs=1, semkey=None):
        self.eng = eng
        self.fn = fn
        self.deps = list(deps)
        self.kind = kind
        self.need_tok = False
        self.val = None
        self.name = name
        self.n_incs = n_incs
        self.pos = None
        self.semkey = semkey
        self.idx = None


class Res:
    __slots__ = ("writers", "readers", "name")

    def __init__(self, name=""):
        self.writers = []
        self.readers = []
        self.name = name


class Plan:
    def __init__(self):
        self.ops = []

    def add(self, eng, fn, deps=(), kind="compute", name="",
            reads=(), writes=(), n_incs=1, semkey=None):
        deps = list(deps)
        for r in reads:
            deps += r.writers
        for r in writes:
            deps += r.readers
        seen = set()
        dd = []
        for d in deps:
            if d is None or id(d) in seen:
                continue
            seen.add(id(d))
            dd.append(d)
        if semkey is None and kind in ("dma", "cc"):
            semkey = writes[0].name if writes else name
        op = Op(eng, fn, dd, kind, name, n_incs, semkey)
        op.idx = len(self.ops)
        self.ops.append(op)
        for r in writes:
            r.readers = []
            r.writers.append(op)
        for r in reads:
            r.readers.append(op)
        for d in dd:
            if d.eng != eng or d.kind != "compute":
                d.need_tok = True
        return op

    def resolve(self):
        cnt = {}
        pos = {}
        self.sem_hist = {}
        for op in self.ops:
            pos[op.eng] = pos.get(op.eng, 0) + 1
            op.pos = pos[op.eng]
            if op.kind in ("dma", "cc"):
                op.need_tok = True
            if op.need_tok:
                key = (op.eng, op.kind, op.semkey)
                step = 16 if op.kind == "dma" else 1
                cnt[key] = cnt.get(key, 0) + step * op.n_incs
                op.val = cnt[key]
                h = self.sem_hist.setdefault(key, ([], []))
                h[0].append(op.idx)
                h[1].append(cnt[key])
        self.totals = cnt

    def wait_val(self, dep, consumer_idx):
        if dep.kind == "compute":
            return dep.val
        key = (dep.eng, dep.kind, dep.semkey)
        idxs, vals = self.sem_hist[key]
        i = bisect.bisect_left(idxs, consumer_idx) - 1
        if i < 0:
            return dep.val
        return max(dep.val, vals[i])

    def emit(self, nc, block, es):
        self.resolve()
        sems = {}
        for n, key in enumerate(self.totals):
            sems[key] = es.enter_context(nc.semaphore(f"s{n}_{key[2] or key[0]}"))
        by_eng = {e: [op for op in self.ops if op.eng == e] for e in ENGINES}

        def mk_body(eng_name, ops):
            def body(e):
                hi = {}
                last_drain = 0
                for op in ops:
                    need_drain = False
                    for d in op.deps:
                        if d.eng == eng_name and d.kind == "compute":
                            if d.pos > last_drain:
                                need_drain = True
                            continue
                        assert d.val is not None, (op.name, d.name)
                        key = (d.eng, d.kind, d.semkey)
                        v = self.wait_val(d, op.idx)
                        if hi.get(key, -1) >= v:
                            continue
                        hi[key] = v
                        e.wait_ge(sems[key], v)
                    if need_drain:
                        e.drain()
                        last_drain = op.pos - 1
                    last = op.fn(e)
                    if op.need_tok:
                        assert last is not None, f"{op.name}: token needed"
                        insts = last if isinstance(last, (list, tuple)) else [last]
                        assert len(insts) == op.n_incs, (op.name, len(insts),
                                                         op.n_incs)
                        step = 16 if op.kind == "dma" else 1
                        for ins_ in insts:
                            ins_.then_inc(sems[(op.eng, op.kind, op.semkey)],
                                          step)
            return body

        for e in ENGINES:
            if by_eng[e]:
                getattr(block, e)(mk_body(e, by_eng[e]))
        return sems


import contextlib
import numpy as np
import ml_dtypes
import concourse.bass as bass
import concourse.mybir as mybir
from concourse.bass_utils import run_bass_kernel_spmd

F32 = mybir.dt.float32
F32R = mybir.dt.float32r
BF16 = mybir.dt.bfloat16
AF = mybir.ActivationFunctionType
ALU = mybir.AluOpType
AXX = mybir.AxisListType.X

C = 1024
PC = 8
T = 2048
NT = 16
TOKB = 512
NB = 4
NP = 8
NE = 8
FF = 2048
NF = 16
EPS = 1e-5
EXPB = -12.0

CONFIG = {"stages": "ABC", "taps": []}
BF = ml_dtypes.bfloat16
RG2 = [[0, 1], [2, 3], [4, 5], [6, 7]]


def shard_inputs(ins):
    x = np.ascontiguousarray(np.asarray(ins["L0_attn_x"], np.float32))
    tril = np.tril(np.ones((128, 128), np.float32))
    masks_by_role = []
    for r in (0, 1):
        mm = np.zeros((8, 128, 512), np.float32)
        for d in range(8):
            for a in range(4):
                own = 2 * a + r
                if own > d:
                    mm[d, :, a * 128:(a + 1) * 128] = 1.0
                elif own == d:
                    mm[d, :, a * 128:(a + 1) * 128] = tril.T
        masks_by_role.append(np.ascontiguousarray(mm.astype(BF)))
    jmean = np.full((128, 128), 1.0 / C, np.float32)
    jone = np.ones((128, 128), np.float32)
    onesb = np.ones((1, 128), np.float32).astype(BF)
    ident = np.eye(128, dtype=np.float32)
    sel2 = np.zeros((1, 256), np.float32)
    sel2[0, 0:64] = 1.0
    sel2[0, 192:256] = 1.0
    oh8 = np.zeros((8, 8 * 128), np.float32)
    for e_ in range(8):
        oh8[e_, e_ * 128:(e_ + 1) * 128] = 1.0
    oh8 = oh8.astype(BF)
    ln1w = np.ascontiguousarray(np.asarray(ins["L0_ffn_ln1_w"], np.float32).reshape(PC, 128).T)
    ln2w = np.ascontiguousarray(np.asarray(ins["L0_ffn_ln2_w"], np.float32).reshape(PC, 128).T)
    mnorm = np.ascontiguousarray(np.asarray(ins["L0_attn_memory_norm"], np.float32).reshape(4, 128).T)
    maps = []
    for core in range(8):
        b, r = core // 2, core % 2
        xb = x[b].reshape(32, 128, C)[r::2].reshape(T, C)
        maps.append({
            "x_in": np.ascontiguousarray(xb.T),
            "wattn": np.asarray(ins["L0_ffn_c_attn_w"], np.float32),
            "wproj": np.asarray(ins["L0_ffn_c_proj_w"], np.float32),
            "wrout": np.asarray(ins["L0_ffn_router_w"], np.float32),
            "w1": np.asarray(ins["L0_ffn_w1"], np.float32),
            "w2": np.asarray(ins["L0_ffn_w2"], np.float32),
            "wq": np.asarray(ins["L0_attn_Wq"], np.float32),
            "wk": np.asarray(ins["L0_attn_Wk"], np.float32),
            "wv": np.asarray(ins["L0_attn_Wv"], np.float32),
            "mem": np.asarray(ins["L0_attn_memory"], np.float32),
            "mnorm": mnorm,
            "memw": np.asarray(ins["L0_attn_memory_weight"], np.float32).reshape(1, 1),
            "ln1w": ln1w, "ln2w": ln2w,
            "masks": masks_by_role[r],
            "jmean": jmean, "jone": jone, "onesb": onesb, "ident": ident,
            "sel2": sel2, "oh8": oh8,
        })
    return maps


def gather_output(results):
    out = np.empty((4, 4096, C), np.float32)
    for core in range(8):
        b, r = core // 2, core % 2
        tok = np.ascontiguousarray(np.asarray(results[core]["out_x"]).T)
        out[b].reshape(32, 128, C)[r::2] = tok.reshape(NT, 128, C)
    return out


class Arena:

    def __init__(self, tens, cols):
        self.t = tens
        self.cols = cols
        self.off = 0
        self.spans = []

    def al2(self, ncols_bf16, name, dt=BF16):
        ncols_bf16 = (ncols_bf16 + 1) // 2 * 2
        assert self.off + ncols_bf16 <= self.cols, (self.off, ncols_bf16)
        s, e = self.off, self.off + ncols_bf16
        v = self.t[:, s:e]
        self.off = e
        r = Res(name)
        dead = []
        for (s0, e0, r0) in self.spans:
            if s0 < e and s < e0:
                r.readers.extend(r0.readers)
                r.readers.extend(r0.writers)
        self.spans = [sp for sp in self.spans if not (sp[0] < e and s < sp[1])]
        self.spans.append((s, e, r))
        return (v.bitcast(F32) if dt == F32 else v), r

    def al_multi(self, nslots, slot_cols, name, dt=BF16):
        views, ress = [], []
        for i in range(nslots):
            v, r = self.al2(slot_cols, f"{name}{i}", dt)
            views.append(v)
            ress.append(r)
        return views, ress

    def reset(self):
        self.off = 0


def build(cfg=None):
    cfg = dict(CONFIG, **(cfg or {}))
    stages = cfg["stages"]
    taps = set(cfg["taps"])

    nc = bass.Bass("TRN2", target_bir_lowering=False)
    dr = {}

    def din(name, shape, dt=F32):
        dr[name] = nc.dram_tensor(name, shape, dt, kind="ExternalInput").ap()

    def dout(name, shape, dt=F32):
        dr[name] = nc.dram_tensor(name, shape, dt, kind="ExternalOutput").ap()

    def dint(name, shape, dt=BF16):
        dr[name] = nc.dram_tensor(name, shape, dt).ap()

    din("x_in", [C, T])
    din("wattn", [C, 3 * C]); din("wproj", [C, C]); din("wrout", [C, 8])
    din("w1", [C, NE * FF]); din("w2", [NE * FF, C])
    din("wq", [C, C]); din("wk", [C, 512]); din("wv", [C, 512])
    din("mem", [512, 512]); din("mnorm", [128, 4]); din("memw", [1, 1])
    din("ln1w", [128, PC]); din("ln2w", [128, PC])
    din("masks", [8, 128, 512], BF16)
    din("jmean", [128, 128]); din("jone", [128, 128])
    din("onesb", [1, 128], BF16); din("ident", [128, 128])
    din("sel2", [1, 256]); din("oh8", [8, 8 * 128], BF16)
    dout("out_x", [C, T], F32)

    dint("w1b", [C, NE * FF]); dint("w2b", [NE * FF, C])
    dint("qs", [C, T]); dint("attns", [C, T]); dint("z2s", [C, T])
    for st in [s for s in stages if s in "AC"]:
        dint(f"{st}k_in", [C, T]); dint(f"{st}k_out", [2 * C, T])
        dint(f"{st}v_in", [NP * T, 128]); dint(f"{st}v_out", [2 * NP * T, 128])
    if "B" in stages:
        dint("Bk_in", [512, T]); dint("Bk_out", [1024, T])
        dint("Bv_in", [NP * T, 64]); dint("Bv_out", [2 * NP * T, 64])

    for tn in taps:
        dout("tap_" + tn, [C, T], F32)

    es = contextlib.ExitStack()

    def sb(name, shape, dt):
        return es.enter_context(nc.sbuf_tensor(name, shape, dt))

    with es:
        x = sb("x", [128, PC * T], F32)
        accb = sb("accb", [128, PC * T], F32)
        NSTG = 3
        stgt = sb("stgt", [128, NSTG * 512], BF16)
        tmpv = sb("tmpv", [128, TOKB], F32)
        tmpv2 = sb("tmpv2", [128, TOKB], F32)
        rdbs = sb("rdbs", [128, TOKB], F32)
        dnst = sb("dnst", [1, 1024], F32)
        aux_m = sb("aux_m", [128, TOKB], F32)
        aux_v = sb("aux_v", [128, TOKB], F32)
        aux_r = sb("aux_r", [128, TOKB], F32)
        xsqb = sb("xsqb", [128, 2 * TOKB], BF16)
        xcpb = sb("xcpb", [128, 2 * TOKB], BF16)
        jm = sb("jm", [128, 128], BF16)
        jo = sb("jo", [128, 128], BF16)
        onb = sb("onb", [1, 128], BF16)
        idn = sb("idn", [128, 128], F32)
        sel = sb("sel", [1, 256], BF16)
        selw = sb("selw", [1, 256], BF16)
        onw = sb("onw", [1, 128], BF16)
        oneb = sb("oneb", [128, 1], F32)
        dnsb = sb("dnsb", [1, 4 * 512], BF16)
        l1w = sb("l1w", [128, PC], F32)
        l2w = sb("l2w", [128, PC], F32)
        mnormt = sb("mnormt", [128, 4], F32)
        mnormb = sb("mnormb", [128, 4], BF16)
        epsb = sb("epsb", [128, 1], F32)
        negC = sb("negC", [128, 1], F32)
        wblend = sb("wblend", [1, 1], F32)
        scal = sb("scal", [2, 8], F32)
        r8 = sb("r8", [128, 8], F32)
        r8b = sb("r8b", [128, 8], F32)
        r1a = sb("r1a", [128, 1], F32)
        r1b = sb("r1b", [128, 1], F32)
        rwt = sb("rwt", [128, 8], F32)
        wbs = sb("wbs", [128, TOKB], BF16)
        rwT = sb("rwT", [8, T], BF16)

        ARN = (nc.sbuf_bytes_remaining - 64) // 2 // 2 * 2
        arn_t = sb("arn", [128, ARN], BF16)
        arena = Arena(arn_t, ARN)

        PPs = [es.enter_context(nc.psum_tensor(f"PP{i}", [128, 1024], F32))
               for i in range(4)]
        PP0, PP1, PP2, PP3 = PPs

        block = es.enter_context(nc.Block())
        P = Plan()

        R_x = [Res(f"x{c}") for c in range(PC)]
        R_acc = [Res(f"acc{c}") for c in range(PC)]
        R_ps = [Res(f"ps{i}") for i in range(8)]
        R_stg = [Res(f"stg{i}") for i in range(NSTG)]
        R_aux = Res("aux")
        R_xsq = [Res("xsq0"), Res("xsq1")]
        R_dns = Res("dns")
        R_dnt = Res("dnt")
        R_rdbs = Res("rdbs")
        R_tmp = Res("tmp")
        R_tmp2 = Res("tmp2")
        R_qs = Res("qs")
        R_z2s = Res("z2s")
        R_attns = Res("attns")
        R_w1b = Res("w1b")
        R_w2b = Res("w2b")
        R_wbs = Res("wbs")
        R_rconst = Res("consts")
        R_scal = Res("scal")
        R_r8 = Res("r8")
        R_rwT = Res("rwT")
        R_r1 = Res("r1")
        R_rwt = Res("rwt")

        def pslot(i):
            return PPs[i // 2], (i % 2) * 512, R_ps[i]

        def bar_deps():
            last = {}
            for op in P.ops:
                last[op.eng] = op
            return list(last.values())

        def new_res(name, bar):
            r = Res(name)
            r.readers = list(bar)
            return r

        def gran_w(w_ap, o128, cchunks=PC):
            return w_ap.rearrange("(c p) o -> p c o", p=128)[
                :, :, o128 * 128:(o128 + 1) * 128]

        def c3(sb_ap, o=128):
            return sb_ap.rearrange("p (c o) -> p c o", o=o)

        def f(e):
            return [
                e.dma_start(out=jm[:, :], in_=dr["jmean"][:, :]),
                e.dma_start(out=jo[:, :], in_=dr["jone"][:, :]),
                e.dma_start(out=onb[:, :], in_=dr["onesb"][:, :]),
                e.dma_start(out=idn[:, :], in_=dr["ident"][:, :]),
                e.dma_start(out=sel[:, :], in_=dr["sel2"][:, :]),
                e.dma_start(out=l1w[:, :], in_=dr["ln1w"][:, :]),
                e.dma_start(out=l2w[:, :], in_=dr["ln2w"][:, :]),
                e.dma_start(out=mnormt[:, :], in_=dr["mnorm"][:, :]),
                e.dma_start(out=wblend[:, :], in_=dr["memw"][:, :]),
            ]
        op_consts = P.add("gpsimd", f, kind="dma", name="consts",
                          writes=(R_rconst,), n_incs=9)

        def f(e):
            insts = []
            for c in range(PC):
                insts.append(e.dma_start(
                    out=x[:, c * T:(c + 1) * T],
                    in_=dr["x_in"][c * 128:(c + 1) * 128, :]))
            return insts
        P.add("sync", f, kind="dma", name="xload", writes=tuple(R_x), n_incs=PC)

        def f(e):
            e.memset(epsb[:, :], EPS)
            e.memset(oneb[:, :], 1.0)
            e.memset(negC[:, :], EXPB)
            e.tensor_copy(mnormb[:, :], mnormt[:, :])
            return e.tensor_scalar_mul(scal[0:1, 0:1], wblend[0:1, 0:1], -1.0)
        op_i1 = P.add("vector", f, deps=[op_consts], name="init_v1",
                      writes=(R_scal,))
        R_rconst.writers.append(op_i1)

        def f(e):
            return e.activation(scal[0:1, 1:2], scal[0:1, 0:1], AF.Exp)
        op_i2 = P.add("scalar", f, deps=[op_i1], name="init_s1",
                      reads=(R_scal,))
        R_scal.writers.append(op_i2)

        def f(e):
            e.tensor_scalar_add(scal[0:1, 1:2], scal[0:1, 1:2], 1.0)
            e.drain()
            e.reciprocal(scal[0:1, 3:4], scal[0:1, 1:2])
            e.drain()
            e.tensor_scalar_mul(scal[0:1, 2:3], scal[0:1, 3:4], -1.0)
            e.drain()
            return e.tensor_scalar_add(scal[0:1, 2:3], scal[0:1, 2:3], 1.0)
        op_i3 = P.add("vector", f, deps=[op_i2], name="init_v2",
                      reads=(R_scal,), writes=(R_scal,))

        def f(e):
            return e.dma_start(out=scal[1:2, 2:4], in_=scal[0:1, 2:4])
        op_i4 = P.add("gpsimd", f, deps=[op_i3], kind="dma", name="init_rep",
                      reads=(R_scal,))
        R_scal.writers.append(op_i4)

        def f(e):
            e.tensor_scalar(selw[:, :], sel[:, :], scal[0:1, 2:3], None, ALU.mult)
            return e.tensor_scalar(onw[:, :], jo[0:1, :], scal[0:1, 3:4],
                                   None, ALU.mult)
        op_selw = P.add("vector", f, deps=[op_i4], name="init_v3",
                        reads=(R_scal, R_rconst), writes=(R_scal,))

        def emit_ln(src_t, R_src, gain_col, out32_t, R_o32, outbf_t, R_obf, tag):
            for bq in range(NB):
                ppm, offm, Rm = pslot(0)
                _, offs, Rs = pslot(1)
                ps_m = ppm[:, offm:offm + 512]
                ps_ss = ppm[:, 512:1024]
                last_stat = None
                for c in range(PC):
                    sl = slice(c * T + bq * TOKB, c * T + (bq + 1) * TOKB)
                    xq = xsqb[:, (c % 2) * 512:(c % 2) * 512 + 512]
                    xc = xcpb[:, (c % 2) * 512:(c % 2) * 512 + 512]

                    def f(e, sl=sl, xq=xq, xc=xc):
                        e.activation(xc, src_t[:, sl], AF.Copy)
                        return e.activation(xq, src_t[:, sl], AF.Square)
                    sqop = P.add("scalar", f, name=f"{tag}sq{bq}_{c}",
                                 reads=(R_src[c],), writes=(R_xsq[c % 2],))

                    def f(e, xq=xq, xc=xc, c=c):
                        e.matmul(ps_m, jm[:, :], xc,
                                 start=(c == 0), stop=(c == PC - 1))
                        return e.matmul(ps_ss, jo[:, :], xq,
                                        start=(c == 0), stop=(c == PC - 1))
                    stat = P.add("tensor", f, deps=[sqop, op_consts],
                                 name=f"{tag}st{bq}_{c}",
                                 reads=(R_xsq[c % 2], R_src[c]),
                                 writes=(Rm, Rs) if c == 0 else ())
                    if c > 0:
                        Rm.writers.append(stat)
                        Rs.writers.append(stat)
                    last_stat = stat

                def f(e):
                    e.tensor_copy(aux_m[:, :], ps_m)
                    e.tensor_scalar_mul(aux_v[:, :], ps_ss, 1.0 / C)
                    e.drain()
                    e.tensor_tensor(tmpv[:, :], aux_m[:, :], aux_m[:, :], ALU.mult)
                    e.drain()
                    return e.tensor_tensor(aux_v[:, :], aux_v[:, :], tmpv[:, :],
                                           ALU.subtract)
                vop = P.add("vector", f, deps=[last_stat], name=f"{tag}var{bq}",
                            reads=(Rm, Rs), writes=(R_aux, R_tmp))

                def f(e):
                    e.activation(aux_v[:, :], aux_v[:, :], AF.Ln, bias=epsb[:, 0:1])
                    e.drain()
                    return e.activation(aux_r[:, :], aux_v[:, :], AF.Exp, scale=-0.5)
                rop = P.add("scalar", f, deps=[vop], name=f"{tag}r{bq}",
                            reads=(R_aux, R_rconst))
                R_aux.writers.append(rop)

                def f(e, bq=bq):
                    last = None
                    for c0 in range(0, PC, 2):
                        for c in (c0, c0 + 1):
                            sl = slice(c * T + bq * TOKB, c * T + (bq + 1) * TOKB)
                            tv = tmpv if c == c0 else tmpv2
                            e.tensor_tensor(tv[:, :], src_t[:, sl], aux_m[:, :],
                                            ALU.subtract)
                        e.drain()
                        for c in (c0, c0 + 1):
                            sl = slice(c * T + bq * TOKB, c * T + (bq + 1) * TOKB)
                            tv = tmpv if c == c0 else tmpv2
                            if out32_t is not None:
                                last = e.tensor_tensor(out32_t[:, sl], tv[:, :],
                                                       aux_r[:, :], ALU.mult)
                            if outbf_t is not None:
                                if gain_col is not None:
                                    last = e.scalar_tensor_tensor(
                                        outbf_t[:, sl], tv[:, :],
                                        gain_col[:, c:c + 1], aux_r[:, :],
                                        ALU.mult, ALU.mult)
                                else:
                                    last = e.tensor_tensor(outbf_t[:, sl],
                                                           tv[:, :],
                                                           aux_r[:, :], ALU.mult)
                        if c0 + 2 < PC:
                            e.drain()
                    return last
                wr = tuple(R_o32[c] for c in range(PC)) if R_o32 else ()
                wr += tuple(R_obf[c] for c in range(PC)) if R_obf else ()
                P.add("vector", f, deps=[rop], name=f"{tag}nrm{bq}",
                      reads=tuple(R_src) + (R_aux, R_rconst),
                      writes=wr + (R_tmp, R_tmp2))

        def emit_tap(name, src_t, R_list):
            if name not in taps:
                return

            def f(e):
                insts = []
                for c in range(PC):
                    insts.append(e.dma_start(
                        out=dr["tap_" + name][c * 128:(c + 1) * 128, :],
                        in_=src_t[:, c * T:(c + 1) * T]))
                return insts
            P.add("sync", f, kind="dma", name="tap_" + name,
                  reads=tuple(R_list), n_incs=PC, semkey="tap")

        def emit_qkv(st, z_t, R_z, wrv, R_wring, wvw_t, R_wvw):
            kin = dr[f"{st}k_in"]
            vin = dr[f"{st}v_in"]
            NSL = len(R_wring)
            k_dma, v_dma, q_dma = [], [], []
            if st in "AC":
                grans = [("wattn", o, o) for o in list(range(8, 16)) + list(range(8))]
                vsrc = ("wattn", 2048)
                nvh = 2
            else:
                grans = [("wk", o, 8 + o) for o in range(4)] + \
                        [("wq", o, o) for o in range(8)]
                vsrc = ("wv", 0)
                nvh = 1
            ip = 0
            for gi, (wn, o, dsto) in enumerate(grans):
                slot = gi % NSL

                def f(e, wn=wn, o=o, slot=slot):
                    return e.dma_start(out=c3(wrv[slot]), in_=gran_w(dr[wn], o))
                gop = P.add("gpsimd", f, kind="dma", name=f"{st}qW{gi}",
                            writes=(R_wring[slot],))
                for bq in range(NB):
                    pp, off, Rp = pslot(ip % 8)

                    def f(e, slot=slot, bq=bq, pp=pp, off=off):
                        last = None
                        for c in range(PC):
                            last = e.matmul(
                                pp[:, off:off + 512],
                                wrv[slot][:, c * 128:(c + 1) * 128],
                                z_t[:, c * T + bq * TOKB: c * T + (bq + 1) * TOKB],
                                start=(c == 0), stop=(c == PC - 1))
                        return last
                    P.add("tensor", f, deps=[gop], name=f"{st}qM{gi}_{bq}",
                          reads=(R_wring[slot],) + tuple(R_z), writes=(Rp,))

                    def f(e, ip=ip, pp=pp, off=off):
                        return e.activation(stgt[:, (ip % NSTG) * 512:(ip % NSTG) * 512 + 512],
                                            pp[:, off:off + 512], AF.Copy)
                    P.add("scalar", f, name=f"{st}qE{gi}_{bq}",
                          reads=(Rp,), writes=(R_stg[ip % NSTG],))

                    if dsto >= 8:
                        dst = kin[(dsto - 8) * 128:(dsto - 7) * 128,
                                  bq * TOKB:(bq + 1) * TOKB]
                        wres = ()
                    else:
                        dst = dr["qs"][dsto * 128:(dsto + 1) * 128,
                                       bq * TOKB:(bq + 1) * TOKB]
                        wres = (R_qs,)

                    def f(e, dst=dst, ip=ip):
                        return e.dma_start(out=dst,
                                           in_=stgt[:, (ip % NSTG) * 512:(ip % NSTG) * 512 + 512])
                    dmop = P.add("sync", f, kind="dma", name=f"{st}qD{gi}_{bq}",
                                 reads=(R_stg[ip % NSTG],), writes=wres,
                                 semkey=("qs" if wres else "kin"))
                    (k_dma if dsto >= 8 else q_dma).append(dmop)
                    ip += 1

            def f(e):
                return e.collective_compute(
                    "AllGather", ALU.bypass, replica_groups=RG2,
                    ins=[kin.opt()], outs=[dr[f"{st}k_out"].opt()])
            cc_k = P.add("gpsimd", f, deps=k_dma, kind="cc", name=f"{st}cck", semkey="cck")

            for vh in range(nvh):
                def f(e, vh=vh):
                    wn, base = vsrc
                    src = dr[wn].rearrange("(c p) o -> p c o", p=128)[
                        :, :, base + vh * 512: base + (vh + 1) * 512]
                    return e.dma_start(out=c3(wvw_t[:, :], 512), in_=src)
                wvop = P.add("gpsimd", f, kind="dma", name=f"{st}wv{vh}",
                             writes=(R_wvw,))
                for tc in range(NT):
                    pp, off, Rp = pslot(ip % 8)

                    def f(e, tc=tc, pp=pp, off=off):
                        last = None
                        for c in range(PC):
                            last = e.matmul(
                                pp[:, off:off + 512],
                                z_t[:, c * T + tc * 128: c * T + (tc + 1) * 128],
                                wvw_t[:, c * 512:(c + 1) * 512],
                                start=(c == 0), stop=(c == PC - 1))
                        return last
                    P.add("tensor", f, deps=[wvop], name=f"{st}vM{vh}_{tc}",
                          reads=(R_wvw,) + tuple(R_z), writes=(Rp,))

                    def f(e, ip=ip, pp=pp, off=off):
                        return e.activation(stgt[:, (ip % NSTG) * 512:(ip % NSTG) * 512 + 512],
                                            pp[:, off:off + 512], AF.Copy)
                    P.add("scalar", f, name=f"{st}vE{vh}_{tc}",
                          reads=(Rp,), writes=(R_stg[ip % NSTG],))

                    if st in "AC":
                        dst = vin.rearrange("(p t) d -> t p d", p=NP)[
                            tc * 128:(tc + 1) * 128, 4 * vh:4 * vh + 4, :]
                        npd = 4
                    else:
                        dst = vin.rearrange("(p t) d -> t p d", p=NP)[
                            tc * 128:(tc + 1) * 128, :, :]
                        npd = 8

                    def f(e, dst=dst, ip=ip, npd=npd):
                        src = stgt[:, (ip % NSTG) * 512:(ip % NSTG) * 512 + 512
                                   ].rearrange("t (p d) -> t p d", p=npd)
                        return e.dma_start(out=dst, in_=src)
                    v_dma.append(P.add("sync", f, kind="dma",
                                       name=f"{st}vD{vh}_{tc}",
                                       reads=(R_stg[ip % NSTG],),
                                       semkey="vin"))
                    ip += 1

            def f(e):
                return e.collective_compute(
                    "AllGather", ALU.bypass, replica_groups=RG2,
                    ins=[vin.opt()], outs=[dr[f"{st}v_out"].opt()])
            cc_v = P.add("gpsimd", f, deps=v_dma, kind="cc", name=f"{st}ccv", semkey="ccv")
            return cc_k, cc_v, q_dma

        def chunks_for(j, causal):
            out = []
            for g in range(32):
                s, tt = divmod(g, 16)
                seq = 2 * tt + s
                if seq < 8 * j:
                    out.append((g, None))
                elif seq < 8 * j + 8:
                    out.append((g, seq - 8 * j))
                elif not causal:
                    out.append((g, None))
            return out

        def emit_mha(st, cc_k, cc_v, q_ops, kvkv, R_kvkl, vt, R_vt, qpv,
                     R_qpl, prbv, R_prb, mskb, R_mskb):
            kout = dr[f"{st}k_out"]
            vout = dr[f"{st}v_out"]
            attn_dma = []
            stg2 = [(tmpv, R_tmp), (tmpv2, R_tmp2)]
            pend = None

            def f(e):
                return e.dma_start(out=c3(mskb[:, :], 512),
                                   in_=dr["masks"].rearrange("d k q -> k d q"))
            mop = P.add("sync", f, kind="dma", name=f"{st}mld", writes=(R_mskb,))

            def f(e):
                return e.memset(vt[:, :], 1.0)
            vtones = P.add("vector", f, name=f"{st}vt1", writes=(R_vt,))

            Rav = (R_ps[4], R_ps[5])
            for p in range(NP):
                psl = p % 2
                kvk = kvkv[psl]
                R_kvk = R_kvkl[psl]
                qp = qpv[psl]
                R_qp = R_qpl[psl]

                def f(e, p=p, kvk=kvk):
                    return [e.dma_start(
                        out=kvk[:, s * 2048:(s + 1) * 2048],
                        in_=kout[s * C + p * 128: s * C + (p + 1) * 128, :])
                        for s in range(2)]
                kld = P.add("sync", f, deps=[cc_k], kind="dma",
                            name=f"{st}kld{p}", writes=(R_kvk,), n_incs=2)

                def f(e, p=p):
                    insts = []
                    for s in range(2):
                        for h in range(2):
                            base = s * 16 * 130 + h * 65
                            dst = vt[:, base:base + 15 * 130 + 65].rearrange(
                                "k (g w) -> k g w", w=65)[:, 0::2, :][:, :, 0:64]
                            src = vout[s * NP * T + p * T: s * NP * T + (p + 1) * T,
                                       h * 64:(h + 1) * 64
                                       ].rearrange("(g k) d -> k g d", k=128)
                            insts.append(e.dma_start(out=dst, in_=src))
                    return insts
                vld = P.add("sync", f, deps=[cc_v, vtones], kind="dma",
                            name=f"{st}vld{p}", writes=(R_vt,), n_incs=4)

                def f(e, p=p, qp=qp):
                    return e.dma_start(out=qp[:, :],
                                       in_=dr["qs"][p * 128:(p + 1) * 128, :])
                qld = P.add("sync", f, deps=q_ops, kind="dma",
                            name=f"{st}qld{p}", reads=(R_qs,), writes=(R_qp,))

                for j in range(NB):
                    sched = chunks_for(j, True)
                    nch = len(sched)
                    for ci, (g, d) in enumerate(sched):
                        pp = PPs[ci % 2]
                        Rpa, Rpb = R_ps[(ci % 2) * 2], R_ps[(ci % 2) * 2 + 1]

                        def f(e, g=g, j=j, pp=pp, kvk=kvk, qp=qp):
                            e.matmul(pp[:, 0:512], kvk[0:64, g * 128:(g + 1) * 128],
                                     qp[0:64, j * TOKB:(j + 1) * TOKB],
                                     start=True, stop=True, tile_position=(0, 0))
                            return e.matmul(
                                pp[:, 512:1024], kvk[64:128, g * 128:(g + 1) * 128],
                                qp[64:128, j * TOKB:(j + 1) * TOKB],
                                start=True, stop=True, tile_position=(64, 0))
                        scop = P.add("tensor", f, deps=[kld, qld],
                                     name=f"{st}sc{p}_{j}_{ci}",
                                     reads=(R_kvk, R_qp), writes=(Rpa, Rpb))

                        prs = ci % 2
                        R_pr = R_prb[prs]

                        def f(e, pp=pp, prs=prs):
                            return e.activation(
                                prbv[prs][:, :],
                                pp[:, 0:1024], AF.Exp, bias=negC[:, 0:1],
                                scale=0.125)
                        exop = P.add("scalar", f, deps=[scop],
                                     name=f"{st}ex{p}_{j}_{ci}",
                                     reads=(Rpa, Rpb, R_rconst), writes=(R_pr,))

                        if d is not None:
                            def f(e, prs=prs, d=d):
                                pr = prbv[prs][:, :]
                                mk = mskb[:, d * 512:(d + 1) * 512]
                                e.tensor_tensor(pr[:, 0:512], pr[:, 0:512], mk,
                                                ALU.mult)
                                return e.tensor_tensor(pr[:, 512:1024],
                                                       pr[:, 512:1024], mk,
                                                       ALU.mult)
                            P.add("vector", f, deps=[exop, mop],
                                  name=f"{st}mk{p}_{j}_{ci}",
                                  reads=(R_pr, R_mskb), writes=(R_pr,))

                        gb = (g // 16) * 16 * 130 + (g % 16) * 130

                        def f(e, gb=gb, prs=prs, first=(ci == 0),
                              last=(ci == nch - 1)):
                            pr = prbv[prs][:, :]
                            e.matmul(PP2[0:65, 0:512], vt[:, gb:gb + 65],
                                     pr[:, 0:512], start=first, stop=last)
                            return e.matmul(PP2[0:65, 512:1024],
                                            vt[:, gb + 65:gb + 130],
                                            pr[:, 512:1024], start=first,
                                            stop=last)
                        avop = P.add("tensor", f, deps=[vld],
                                     name=f"{st}av{p}_{j}_{ci}",
                                     reads=(R_pr, R_vt),
                                     writes=Rav if ci == 0 else ())
                        if ci > 0:
                            for rr in Rav:
                                rr.writers.append(avop)
                        last_av = avop

                    jj = p * NB + j
                    sg, Rsg = stg2[jj % 2]
                    if pend is not None:
                        pend()

                    def f(e, sg=sg):
                        return e.activation(sg[0:64, :], PP2[0:64, 0:512],
                                            AF.Copy)
                    cpA = P.add("scalar", f, deps=[last_av],
                                name=f"{st}cpA{p}_{j}", reads=Rav,
                                writes=(Rsg,))

                    def f(e, sg=sg):
                        return e.activation(sg[64:128, :], PP2[0:64, 512:1024],
                                            AF.Copy)
                    cpB = P.add("scalar", f, deps=[last_av],
                                name=f"{st}cpB{p}_{j}", reads=Rav,
                                writes=(Rsg,))

                    def f(e):
                        return e.activation(dnst[0:1, 0:512],
                                            PP2[64:65, 0:512], AF.Copy)
                    cpDA = P.add("scalar", f, deps=[last_av],
                                 name=f"{st}cpDa{p}_{j}", reads=Rav,
                                 writes=(R_dnt,))

                    def f(e):
                        return e.activation(dnst[0:1, 512:1024],
                                            PP2[64:65, 512:1024], AF.Copy)
                    cpDB = P.add("scalar", f, deps=[last_av],
                                 name=f"{st}cpDb{p}_{j}", reads=Rav,
                                 writes=(R_dnt,))

                    def f(e):
                        e.activation(dnst[:, :], dnst[:, :], AF.Ln)
                        e.drain()
                        return e.activation(dnst[:, :], dnst[:, :], AF.Exp,
                                            scale=-1.0)
                    dnop = P.add("scalar", f, deps=[cpDA, cpDB],
                                 name=f"{st}dn{p}_{j}", reads=(R_dnt,),
                                 writes=(R_dnt,))

                    def f(e):
                        e.tensor_copy(dnsb[:, 0:1024], dnst[:, :])
                        e.drain()
                        return e.tensor_tensor(dnsb[:, 1024:2048], dnst[:, :],
                                               dnsb[:, 0:1024], ALU.subtract)
                    hilo = P.add("vector", f, deps=[dnop],
                                 name=f"{st}hl{p}_{j}", reads=(R_dnt,),
                                 writes=(R_dns,))

                    def mk_fin(p=p, j=j, sg=sg, Rsg=Rsg, hilo=hilo):
                        def fin():
                            pR, oR, RR = pslot(6)

                            def f(e, pR=pR, oR=oR):
                                e.matmul(pR[:, oR:oR + 512], sel[0:1, 0:128],
                                         dnsb[:, 0:512], start=True, stop=False)
                                e.matmul(pR[:, oR:oR + 512], sel[0:1, 128:256],
                                         dnsb[:, 512:1024], start=False,
                                         stop=False)
                                e.matmul(pR[:, oR:oR + 512], sel[0:1, 0:128],
                                         dnsb[:, 1024:1536], start=False,
                                         stop=False)
                                return e.matmul(pR[:, oR:oR + 512],
                                                sel[0:1, 128:256],
                                                dnsb[:, 1536:2048], start=False,
                                                stop=True)
                            rdop = P.add("tensor", f, deps=[hilo, op_consts],
                                         name=f"{st}rd{p}_{j}", reads=(R_dns,),
                                         writes=(RR,))

                            def f(e, pR=pR, oR=oR):
                                return e.tensor_copy(rdbs[:, :],
                                                     pR[:, oR:oR + 512])
                            rcop = P.add("vector", f, name=f"{st}rc{p}_{j}",
                                         reads=(RR,), writes=(R_rdbs,))

                            sslot = (p * NB + j) % NSTG

                            def f(e, sslot=sslot, sg=sg):
                                sv = stgt[:, sslot * 512:(sslot + 1) * 512]
                                return e.tensor_tensor(sv[:, :], sg[:, :],
                                                       rdbs[:, :], ALU.mult)
                            oop = P.add("vector", f, deps=[rcop],
                                        name=f"{st}ao{p}_{j}",
                                        reads=(Rsg, R_rdbs),
                                        writes=(R_stg[sslot],))

                            def f(e, p=p, j=j, sslot=sslot):
                                return e.dma_start(
                                    out=dr["attns"][p * 128:(p + 1) * 128,
                                                    j * TOKB:(j + 1) * TOKB],
                                    in_=stgt[:, sslot * 512:(sslot + 1) * 512])
                            attn_dma.append(P.add("sync", f, kind="dma",
                                                  name=f"{st}aod{p}_{j}",
                                                  reads=(R_stg[sslot],),
                                                  writes=(R_attns,)))
                        return fin
                    pend = mk_fin()
            if pend is not None:
                pend()
            return attn_dma

        def emit_proj(st, attst, R_attst, pjw, R_pjw):
            def f(e):
                return [e.dma_start(out=c3(pjw[:, o * 1024:(o + 1) * 1024]),
                                    in_=gran_w(dr["wproj"], o))
                        for o in range(PC)]
            wop = P.add("gpsimd", f, kind="dma", name=f"{st}pjw",
                        writes=(R_pjw,), n_incs=PC)
            for bq in range(NB):
                def f(e, bq=bq):
                    return [e.dma_start(
                        out=attst[:, c * 512:(c + 1) * 512],
                        in_=dr["attns"][c * 128:(c + 1) * 128,
                                        bq * TOKB:(bq + 1) * TOKB])
                        for c in range(PC)]
                lop = P.add("sync", f, kind="dma", name=f"{st}pjl{bq}",
                            reads=(R_attns,), writes=(R_attst,), n_incs=PC)
                for o in range(PC):
                    pp, off, Rp = pslot((bq * PC + o) % 8)

                    def f(e, o=o, pp=pp, off=off):
                        last = None
                        for c in range(PC):
                            last = e.matmul(
                                pp[:, off:off + 512],
                                pjw[:, o * 1024 + c * 128: o * 1024 + (c + 1) * 128],
                                attst[:, c * 512:(c + 1) * 512],
                                start=(c == 0), stop=(c == PC - 1))
                        return last
                    mmop = P.add("tensor", f, deps=[wop, lop],
                                 name=f"{st}pjM{bq}_{o}",
                                 reads=(R_pjw, R_attst), writes=(Rp,))

                    def f(e, o=o, bq=bq, pp=pp, off=off):
                        sl = slice(o * T + bq * TOKB, o * T + (bq + 1) * TOKB)
                        return e.tensor_tensor(accb[:, sl], accb[:, sl],
                                               pp[:, off:off + 512], ALU.add)
                    P.add("vector", f, name=f"{st}pjA{bq}_{o}",
                          reads=(Rp, R_acc[o]), writes=(R_acc[o],))

        def emit_fold(dst_t, R_dst, src_t, R_src, scale, tag):
            for c in range(PC):
                def f(e, c=c):
                    last = None
                    for bq in range(NB):
                        sl = slice(c * T + bq * TOKB, c * T + (bq + 1) * TOKB)
                        last = e.scalar_tensor_tensor(dst_t[:, sl], src_t[:, sl],
                                                      scale, dst_t[:, sl],
                                                      ALU.mult, ALU.add)
                    return last
                P.add("vector", f, name=f"{tag}{c}",
                      reads=(R_src[c], R_dst[c]), writes=(R_dst[c],))

        def emit_router(st, z_t, R_z, rww, R_rww, rwwf, hilo, R_hilo):
            def f(e):
                return e.dma_start(
                    out=c3(rwwf[:, :], 8),
                    in_=dr["wrout"].rearrange("(c p) o -> p c o", p=128))
            wop = P.add("gpsimd", f, kind="dma", name=f"{st}rww",
                        writes=(R_rww,))

            def f(e):
                last = None
                for c in range(PC):
                    last = e.tensor_scalar(rww[:, c * 8:(c + 1) * 8],
                                           rwwf[:, c * 8:(c + 1) * 8],
                                           l2w[:, c:c + 1], None, ALU.mult)
                return last
            wop = P.add("vector", f, deps=[wop, op_consts], name=f"{st}rwf",
                        reads=(R_rww, R_rconst), writes=(R_rww,))
            for tc in range(NT):
                pp, off, Rp = pslot(tc % 4)

                def f(e, tc=tc):
                    last = None
                    for c in range(PC):
                        sl = slice(c * T + tc * 128, c * T + (tc + 1) * 128)
                        e.activation(xcpb[:, 0:128], z_t[:, sl], AF.Copy)
                        last = e.tensor_tensor(xcpb[:, 128:256], z_t[:, sl],
                                               xcpb[:, 0:128], ALU.subtract)
                    return last
                def fmm(e, tc=tc, pp=pp, off=off):
                    last = None
                    for c in range(PC):
                        last = e.matmul(
                            pp[:, off:off + 8],
                            hilo[:, c * 256:c * 256 + 128],
                            rww[:, c * 8:(c + 1) * 8],
                            start=(c == 0), stop=False)
                        last = e.matmul(
                            pp[:, off:off + 8],
                            hilo[:, c * 256 + 128:c * 256 + 256],
                            rww[:, c * 8:(c + 1) * 8],
                            start=False, stop=(c == PC - 1))
                    return last

                def fhl(e, tc=tc):
                    last = None
                    for c in range(PC):
                        sl = slice(c * T + tc * 128, c * T + (tc + 1) * 128)
                        e.activation(hilo[:, c * 256:c * 256 + 128], z_t[:, sl],
                                     AF.Copy)
                        last = e.activation(hilo[:, c * 256 + 128:c * 256 + 256],
                                            z_t[:, sl], AF.Copy, scale=-1.0,
                                            bias=None) if False else last
                    return last
                def fhi(e, tc=tc):
                    last = None
                    for c in range(PC):
                        sl = slice(c * T + tc * 128, c * T + (tc + 1) * 128)
                        last = e.activation(hilo[:, c * 256:c * 256 + 128],
                                            z_t[:, sl], AF.Copy)
                    return last
                hiop = P.add("scalar", fhi, name=f"{st}rHi{tc}",
                             reads=tuple(R_z), writes=(R_hilo,))

                def flo(e, tc=tc):
                    last = None
                    for c in range(PC):
                        sl = slice(c * T + tc * 128, c * T + (tc + 1) * 128)
                        last = e.tensor_tensor(
                            hilo[:, c * 256 + 128:c * 256 + 256], z_t[:, sl],
                            hilo[:, c * 256:c * 256 + 128], ALU.subtract)
                    return last
                loop_ = P.add("vector", flo, deps=[hiop], name=f"{st}rLo{tc}",
                              reads=tuple(R_z) + (R_hilo,), writes=(R_hilo,))

                mmop = P.add("tensor", fmm, deps=[wop, loop_],
                             name=f"{st}rM{tc}",
                             reads=(R_rww, R_hilo), writes=(Rp,))

                def f(e, pp=pp, off=off):
                    e.tensor_copy(r8[:, :], pp[:, off:off + 8])
                    e.drain()
                    return e.reduce_max(r1a[:, :], r8[:, :], AXX, negate=True)
                v1 = P.add("vector", f, deps=[mmop], name=f"{st}rV1{tc}",
                           reads=(Rp,), writes=(R_r8, R_r1))

                def f(e):
                    return e.activation(r8[:, :], r8[:, :], AF.Exp,
                                        bias=r1a[:, 0:1])
                s1 = P.add("scalar", f, deps=[v1], name=f"{st}rS{tc}",
                           reads=(R_r8, R_r1))
                R_r8.writers.append(s1)

                def f(e):
                    e.reduce_max(r1a[:, :], r8[:, :], AXX)
                    e.drain()
                    e.tensor_scalar(r8b[:, :], r8[:, :], r1a[:, 0:1], None,
                                    ALU.is_lt)
                    e.drain()
                    e.tensor_tensor(r8b[:, :], r8b[:, :], r8[:, :], ALU.mult)
                    e.drain()
                    e.reduce_max(r1b[:, :], r8b[:, :], AXX)
                    e.drain()
                    e.tensor_scalar(r8b[:, :], r8[:, :], r1b[:, 0:1], None,
                                    ALU.is_ge)
                    e.drain()
                    e.tensor_tensor(r8b[:, :], r8b[:, :], r8[:, :], ALU.mult)
                    e.drain()
                    e.reduce_sum(r1b[:, :], r8b[:, :], AXX)
                    e.drain()
                    e.reciprocal(r1b[:, :], r1b[:, :])
                    e.drain()
                    return e.tensor_scalar(rwt[:, :], r8b[:, :], r1b[:, 0:1],
                                           0.5, ALU.mult, ALU.mult)
                v2 = P.add("vector", f, deps=[s1], name=f"{st}rV2{tc}",
                           reads=(R_r8,), writes=(R_r8, R_rwt))

                pT, oT, RT = pslot(5)

                def f(e, pT=pT, oT=oT):
                    return e.transpose(pT[0:8, oT:oT + 128], rwt[:, :], idn[:, :])
                top = P.add("tensor", f, deps=[v2, op_consts],
                            name=f"{st}rT{tc}", reads=(R_rconst, R_rwt),
                            writes=(RT,))

                def f(e, tc=tc, pT=pT, oT=oT):
                    return e.tensor_copy(rwT[0:8, tc * 128:(tc + 1) * 128],
                                         pT[0:8, oT:oT + 128])
                P.add("vector", f, deps=[top], name=f"{st}rC{tc}",
                      reads=(RT,), writes=(R_rwT,))

        def emit_moe(st, z2t, R_z2t, w1v, R_w1r, w2v, R_w2r, hsv, R_hs,
                     oh8, R_oh8, first_cast):
            def f(e):
                return e.dma_start(out=oh8[0:8, :], in_=dr["oh8"][:, :])
            ohop = P.add("sync", f, kind="dma", name=f"{st}oh8",
                         writes=(R_oh8,))
            N1 = len(R_w1r)
            N2 = len(R_w2r)
            i1 = 0
            i2 = 0
            for bq in range(NB):
                def f(e, bq=bq):
                    return [e.dma_start(
                        out=z2t[:, c * 512:(c + 1) * 512],
                        in_=dr["z2s"][c * 128:(c + 1) * 128,
                                      bq * TOKB:(bq + 1) * TOKB])
                        for c in range(PC)]
                zld = P.add("sync", f, kind="dma", name=f"{st}mZ{bq}",
                            reads=(R_z2s,), writes=(R_z2t,), n_incs=PC)
                for ei in range(NE):
                    pW, oW, RW = pslot(6)

                    def f(e, ei=ei, bq=bq, pW=pW, oW=oW):
                        return e.matmul(pW[:, oW:oW + 512],
                                        oh8[0:8, ei * 128:(ei + 1) * 128],
                                        rwT[0:8, bq * TOKB:(bq + 1) * TOKB],
                                        start=True, stop=True)
                    wbop = P.add("tensor", f, deps=[ohop],
                                 name=f"{st}wb{bq}_{ei}",
                                 reads=(R_rwT, R_oh8), writes=(RW,))

                    def f(e, pW=pW, oW=oW):
                        return e.activation(wbs[:, :], pW[:, oW:oW + 512],
                                            AF.Copy)
                    wbsop = P.add("scalar", f, name=f"{st}wbs{bq}_{ei}",
                                  reads=(RW,), writes=(R_wbs,))

                    h_ops = []
                    for fi in range(NF):
                        slot = i1 % N1
                        cast = first_cast and bq == 0
                        srcw = dr["w1" if cast else "w1b"]
                        srcap = srcw.rearrange("(c p) o -> p c o", p=128)[
                            :, :, ei * FF + fi * 128: ei * FF + (fi + 1) * 128]

                        def f(e, srcap=srcap, slot=slot):
                            return e.dma_start(out=c3(w1v[slot]), in_=srcap)
                        w1l = P.add("gpsimd" if cast else "sync", f, kind="dma",
                                    name=f"{st}w1L{bq}_{ei}_{fi}",
                                    reads=() if cast else (R_w1b,),
                                    writes=(R_w1r[slot],))
                        if cast:
                            dstap = dr["w1b"].rearrange("(c p) o -> p c o", p=128)[
                                :, :, ei * FF + fi * 128: ei * FF + (fi + 1) * 128]

                            def f(e, dstap=dstap, slot=slot):
                                return e.dma_start(out=dstap, in_=c3(w1v[slot]))
                            P.add("sync", f, deps=[w1l], kind="dma",
                                  name=f"{st}w1B{ei}_{fi}",
                                  reads=(R_w1r[slot],), writes=(R_w1b,))

                        pH, oH, RH = pslot(fi % 2)

                        def f(e, slot=slot, pH=pH, oH=oH):
                            last = None
                            for c in range(PC):
                                last = e.matmul(
                                    pH[:, oH:oH + 512],
                                    w1v[slot][:, c * 128:(c + 1) * 128],
                                    z2t[:, c * 512:(c + 1) * 512],
                                    start=(c == 0), stop=(c == PC - 1))
                            return last
                        P.add("tensor", f, deps=[w1l, zld],
                              name=f"{st}hM{bq}_{ei}_{fi}",
                              reads=(R_w1r[slot], R_z2t), writes=(RH,))

                        def f(e, fi=fi, pH=pH, oH=oH):
                            return e.activation(hsv[fi][:, :],
                                                pH[:, oH:oH + 512], AF.Gelu)
                        P.add("scalar", f, name=f"{st}g{bq}_{ei}_{fi}",
                              reads=(RH,), writes=(R_hs[fi],))

                        def f(e, fi=fi):
                            sl = hsv[fi][:, :]
                            return e.tensor_tensor(sl, sl, wbs[:, :], ALU.mult)
                        h_ops.append(P.add("vector", f, deps=[wbsop],
                                           name=f"{st}hs{bq}_{ei}_{fi}",
                                           reads=(R_hs[fi], R_wbs),
                                           writes=(R_hs[fi],)))
                        i1 += 1

                    for co in range(PC):
                        slot = i2 % N2
                        cast = first_cast and bq == 0
                        srcw = dr["w2" if cast else "w2b"]
                        srcap = srcw.rearrange("(f p) o -> p f o", p=128)[
                            :, ei * NF:(ei + 1) * NF, co * 128:(co + 1) * 128]

                        def f(e, srcap=srcap, slot=slot):
                            return e.dma_start(out=c3(w2v[slot]), in_=srcap)
                        w2l = P.add("gpsimd" if cast else "sync", f, kind="dma",
                                    name=f"{st}w2L{bq}_{ei}_{co}",
                                    reads=() if cast else (R_w2b,),
                                    writes=(R_w2r[slot],))
                        if cast:
                            dstap = dr["w2b"].rearrange("(f p) o -> p f o", p=128)[
                                :, ei * NF:(ei + 1) * NF, co * 128:(co + 1) * 128]

                            def f(e, dstap=dstap, slot=slot):
                                return e.dma_start(out=dstap, in_=c3(w2v[slot]))
                            P.add("sync", f, deps=[w2l], kind="dma",
                                  name=f"{st}w2B{ei}_{co}",
                                  reads=(R_w2r[slot],), writes=(R_w2b,))

                        pY, oY, RY = pslot(2 + co % 2)

                        def f(e, slot=slot, pY=pY, oY=oY):
                            last = None
                            for fi in range(NF):
                                last = e.matmul(
                                    pY[:, oY:oY + 512],
                                    w2v[slot][:, fi * 128:(fi + 1) * 128],
                                    hsv[fi][:, :],
                                    start=(fi == 0), stop=(fi == NF - 1))
                            return last
                        ymm = P.add("tensor", f, deps=[w2l] + h_ops,
                                    name=f"{st}yM{bq}_{ei}_{co}",
                                    reads=(R_w2r[slot],) + tuple(R_hs),
                                    writes=(RY,))

                        def f(e, co=co, bq=bq, pY=pY, oY=oY):
                            sl = slice(co * T + bq * TOKB, co * T + (bq + 1) * TOKB)
                            return e.tensor_tensor(x[:, sl], x[:, sl],
                                                   pY[:, oY:oY + 512], ALU.add)
                        P.add("vector", f, name=f"{st}yA{bq}_{ei}_{co}",
                              reads=(RY, R_x[co]), writes=(R_x[co],))
                        i2 += 1

        def emit_ffn_stage(st, first):
            arena.reset()
            z_views, R_z = arena.al_multi(PC, T, "z")
            z_t = arn_t[:, 0:PC * T]
            wrv, R_wring = arena.al_multi(2, 1024, "wr")
            wvw_t, R_wvw = arena.al2(PC * 512, "wvw")

            emit_ln(x, R_x, None, accb, R_acc, None, None, st + "L0")
            if st == "A":
                emit_tap("yA", accb, R_acc)
            emit_ln(accb, R_acc, l1w, None, None, z_t, R_z, st + "L1")

            if st == "A" and "z1" in taps:
                def f(e):
                    return [e.dma_start(out=dr["tap_z1"][c * 128:(c + 1) * 128, :],
                                        in_=z_t[:, c * T:(c + 1) * T])
                            for c in range(PC)]
                P.add("gpsimd", f, kind="dma", name="tap_z1",
                      reads=tuple(R_z), n_incs=PC, semkey="tap")
            cc_k, cc_v, q_ops = emit_qkv(st, z_t, R_z, wrv, R_wring,
                                         wvw_t, R_wvw)

            arena.reset()
            kvkv, R_kvkl = arena.al_multi(2, 2 * 2048, "kvk")
            vt, R_vt = arena.al2(2 * 16 * 130, "vt")
            qpv, R_qpl = arena.al_multi(2, 2048, "qp")
            prbv, R_prb = arena.al_multi(2, 1024, "pr")
            mskb, R_mskb = arena.al2(8 * 512, "mskb")
            ad_ops = emit_mha(st, cc_k, cc_v, q_ops, kvkv, R_kvkl, vt, R_vt,
                              qpv, R_qpl, prbv, R_prb, mskb, R_mskb)
            if st == "A" and "Bq" in taps:
                def f(e):
                    insts = [e.dma_start(out=dr["tap_Bq"][0:1024, :],
                                         in_=dr["qs"][:, :])]
                    if "Badot" in taps:
                        insts.append(e.dma_start(out=dr["tap_Badot"][0:1024, :],
                                                 in_=dr["Ak_in"][:, :]))
                    return insts
                P.add("gpsimd", f, deps=q_ops, kind="dma", name="tap_q",
                      reads=(R_qs,), n_incs=2 if "Badot" in taps else 1,
                      semkey="tap")
            if st == "A" and "attn_heads" in taps:
                def f(e):
                    return e.dma_start(out=dr["tap_attn_heads"][:, :],
                                       in_=dr["attns"][:, :])
                P.add("gpsimd", f, deps=ad_ops, kind="dma", name="tap_ah",
                      reads=(R_attns,), semkey="tap")

            arena.reset()
            attst, R_attst = arena.al2(PC * 512, "attst")
            pjw, R_pjw = arena.al2(PC * 1024, "pjw")
            emit_proj(st, attst, R_attst, pjw, R_pjw)
            if st == "A":
                emit_tap("a", accb, R_acc)

            emit_fold(x, R_x, accb, R_acc, 0.5, st + "fold")
            arena.reset()
            z2_views, R_z2 = arena.al_multi(PC, T, "z2")
            z2_t = arn_t[:, 0:PC * T]
            rww, R_rww = arena.al2(PC * 8, "rww")
            hilo, R_hilo = arena.al2(PC * 256, "hilo")
            rwwf, R_rwwf = arena.al2(PC * 8 * 2, "rwwf", F32)
            emit_ln(accb, R_acc, l2w, accb, R_acc, z2_t, R_z2, st + "L2")

            def f(e):
                return [e.dma_start(out=dr["z2s"][c * 128:(c + 1) * 128, :],
                                    in_=z2_t[:, c * T:(c + 1) * T])
                        for c in range(PC)]
            P.add("sync", f, kind="dma", name=f"{st}z2d",
                  reads=tuple(R_z2), writes=(R_z2s,), n_incs=PC)

            emit_router(st, accb, R_acc, rww, R_rww, rwwf, hilo, R_hilo)

            arena.reset()
            z2t, R_z2t = arena.al2(PC * 512, "z2t")
            w1v, R_w1r = arena.al_multi(4, 1024, "w1r")
            w2v, R_w2r = arena.al_multi(2, 2048, "w2r")
            hsv, R_hs = arena.al_multi(NF, 512, "hs")
            oh8t, R_oh8 = arena.al2(8 * 128, "oh8")
            emit_moe(st, z2t, R_z2t, w1v, R_w1r, w2v, R_w2r, hsv, R_hs,
                     oh8t, R_oh8, first)

        def emit_stageB():
            arena.reset()
            z_views, R_z = arena.al_multi(PC, T, "zB")
            z_t = arn_t[:, 0:PC * T]
            wrv, R_wring = arena.al_multi(2, 1024, "wrB")
            wvw_t, R_wvw = arena.al2(PC * 512, "wvwB")

            emit_ln(x, R_x, None, None, None, z_t, R_z, "BL0")
            cc_k, cc_v, q_ops = emit_qkv("B", z_t, R_z, wrv, R_wring,
                                         wvw_t, R_wvw)

            arena.reset()
            memsb, R_memsb = arena.al2(16 * 128, "Bmem")
            sigv, R_sig = arena.al_multi(4, 512, "sig")
            qtlv, R_qtl = arena.al_multi(2, 512, "qt")

            def f(e):
                return e.dma_start(
                    out=c3(memsb[:, :], 512),
                    in_=dr["mem"].rearrange("(ci p) (co q) -> p ci (co q)",
                                            p=128, q=128))
            memop = P.add("gpsimd", f, kind="dma", name="BmemL",
                          writes=(R_memsb,))

            for gq in range(2):
                for bq in range(NB):
                    elu_ops = []
                    for c4 in range(4):
                        cc_ = 4 * gq + c4
                        qslot = c4 % 2

                        def f(e, cc_=cc_, bq=bq, qslot=qslot):
                            return e.dma_start(
                                out=qtlv[qslot][:, :],
                                in_=dr["qs"][cc_ * 128:(cc_ + 1) * 128,
                                             bq * TOKB:(bq + 1) * TOKB])
                        qldo = P.add("sync", f, deps=q_ops, kind="dma",
                                     name=f"BqL{gq}_{bq}_{c4}", reads=(R_qs,),
                                     writes=(R_qtl[qslot],))

                        def f(e, qslot=qslot):
                            qv = qtlv[qslot][:, :]
                            return e.tensor_scalar_min(tmpv[:, :], qv, 0.0)
                        v1 = P.add("vector", f, deps=[qldo],
                                   name=f"Bel1{gq}_{bq}_{c4}",
                                   reads=(R_qtl[qslot],), writes=(R_tmp,))

                        def f(e):
                            return e.activation(tmpv2[:, :], tmpv[:, :], AF.Exp)
                        s1 = P.add("scalar", f, deps=[v1],
                                   name=f"Bel2{gq}_{bq}_{c4}",
                                   reads=(R_tmp,), writes=(R_tmp2,))

                        def f(e, qslot=qslot, c4=c4):
                            qv = qtlv[qslot][:, :]
                            e.tensor_scalar_max(tmpv[:, :], qv, 0.0)
                            e.drain()
                            return e.scalar_tensor_tensor(
                                sigv[c4][:, :], tmpv2[:, :],
                                -1.0, tmpv[:, :], ALU.add, ALU.add)
                        elu_ops.append(P.add(
                            "vector", f, deps=[s1],
                            name=f"Bel3{gq}_{bq}_{c4}",
                            reads=(R_qtl[qslot], R_tmp2),
                            writes=(R_sig[c4], R_tmp)))

                    pD, oD, RD = pslot(6)

                    def f(e, pD=pD, oD=oD):
                        last = None
                        for c4 in range(4):
                            last = e.matmul(pD[0:1, oD:oD + 512],
                                            mnormb[:, c4:c4 + 1],
                                            sigv[c4][:, :],
                                            start=(c4 == 0), stop=(c4 == 3))
                        return last
                    dmm = P.add("tensor", f, deps=elu_ops + [op_consts],
                                name=f"Bden{gq}_{bq}",
                                reads=tuple(R_sig), writes=(RD,))

                    def f(e, pD=pD, oD=oD):
                        e.tensor_copy(dnst[0:1, 0:512], pD[0:1, oD:oD + 512])
                        e.drain()
                        e.reciprocal(dnst[0:1, 0:512], dnst[0:1, 0:512])
                        e.drain()
                        e.tensor_copy(dnsb[0:1, 0:512], dnst[0:1, 0:512])
                        e.drain()
                        return e.tensor_tensor(dnsb[0:1, 512:1024],
                                               dnst[0:1, 0:512], dnsb[0:1, 0:512],
                                               ALU.subtract)
                    dnop = P.add("vector", f, deps=[dmm], name=f"Bdn{gq}_{bq}",
                                 reads=(RD,), writes=(R_dns,))

                    pR, oR, RR = pslot(7)

                    def f(e, pR=pR, oR=oR):
                        e.matmul(pR[:, oR:oR + 512], onw[:, :],
                                 dnsb[0:1, 0:512], start=True, stop=False)
                        return e.matmul(pR[:, oR:oR + 512], onw[:, :],
                                        dnsb[0:1, 512:1024], start=False,
                                        stop=True)
                    rdop = P.add("tensor", f, deps=[dnop, op_selw],
                                 name=f"Brd{gq}_{bq}", reads=(R_dns,),
                                 writes=(RR,))

                    def f(e, pR=pR, oR=oR):
                        return e.activation(rdbs[:, :], pR[:, oR:oR + 512],
                                            AF.Copy)
                    rcop = P.add("scalar", f, name=f"Brc{gq}_{bq}",
                                 reads=(RR,), writes=(R_rdbs,))

                    for kvo in range(4):
                        pN, oN, RN = pslot(4 + kvo % 2)

                        def f(e, gq=gq, kvo=kvo, pN=pN, oN=oN):
                            last = None
                            for c4 in range(4):
                                mcol = (c4 * 4 + kvo) * 128
                                last = e.matmul(
                                    pN[:, oN:oN + 512],
                                    memsb[:, mcol:mcol + 128],
                                    sigv[c4][:, :],
                                    start=(c4 == 0), stop=(c4 == 3))
                            return last
                        nmm = P.add("tensor", f, deps=[memop] + elu_ops,
                                    name=f"BnM{gq}_{bq}_{kvo}",
                                    reads=(R_memsb,) + tuple(R_sig),
                                    writes=(RN,))

                        def f(e, gq=gq, bq=bq, kvo=kvo, pN=pN, oN=oN):
                            co = 4 * gq + kvo
                            sl = slice(co * T + bq * TOKB,
                                       co * T + (bq + 1) * TOKB)
                            e.tensor_tensor(tmpv[:, :], pN[:, oN:oN + 512],
                                            rdbs[:, :], ALU.mult)
                            e.drain()
                            return e.tensor_tensor(x[:, sl], x[:, sl],
                                                   tmpv[:, :], ALU.add)
                        P.add("vector", f, deps=[rcop, nmm],
                              name=f"BnA{gq}_{bq}_{kvo}",
                              reads=(RN, R_rdbs, R_x[4 * gq + kvo]),
                              writes=(R_x[4 * gq + kvo], R_tmp))

            arena.reset()
            kdv, R_kdl = arena.al_multi(2, 2 * 2048, "kd")
            vt, R_vt = arena.al2(2 * 16 * 65, "vtB")
            qpv, R_qpl = arena.al_multi(2, 2048, "qpB")
            prbv, R_prb = arena.al_multi(2, 1024, "prB")
            stg2 = [(tmpv, R_tmp), (tmpv2, R_tmp2)]
            pend = None

            def f(e):
                return e.memset(vt[:, :], 1.0)
            vtones = P.add("vector", f, name="Bvt1", writes=(R_vt,))

            Rav = (R_ps[4], R_ps[5])
            for p in range(NP):
                psl = p % 2
                kd = kdv[psl]
                R_kd = R_kdl[psl]
                qp = qpv[psl]
                R_qp = R_qpl[psl]

                def f(e, p=p, kd=kd):
                    insts = []
                    for s in range(2):
                        src = dr["Bk_out"][s * 512 + (p // 2) * 128 + (p % 2) * 64:
                                           s * 512 + (p // 2) * 128 + (p % 2) * 64 + 64, :]
                        insts.append(e.dma_start(
                            out=kd[0:64, s * 2048:(s + 1) * 2048], in_=src))
                        insts.append(e.dma_start(
                            out=kd[64:128, s * 2048:(s + 1) * 2048], in_=src))
                    return insts
                kld = P.add("sync", f, deps=[cc_k], kind="dma",
                            name=f"Bkld{p}", writes=(R_kd,), n_incs=4)

                def f(e, p=p):
                    insts = []
                    for s in range(2):
                        dst = vt[:, s * 16 * 65:(s + 1) * 16 * 65].rearrange(
                            "k (g w) -> k g w", w=65)[:, :, 0:64]
                        src = dr["Bv_out"][s * NP * T + p * T: s * NP * T + (p + 1) * T, :
                                           ].rearrange("(g k) d -> k g d", k=128)
                        insts.append(e.dma_start(out=dst, in_=src))
                    return insts
                vld = P.add("sync", f, deps=[cc_v, vtones], kind="dma",
                            name=f"Bvld{p}", writes=(R_vt,), n_incs=2)

                def f(e, p=p, qp=qp):
                    return e.dma_start(out=qp[:, :],
                                       in_=dr["qs"][p * 128:(p + 1) * 128, :])
                qld = P.add("sync", f, deps=q_ops, kind="dma",
                            name=f"Bqld{p}", reads=(R_qs,), writes=(R_qp,))

                for j in range(NB):
                    for ci in range(32):
                        g = ci
                        pp = PPs[ci % 2]
                        Rpa, Rpb = R_ps[(ci % 2) * 2], R_ps[(ci % 2) * 2 + 1]

                        def f(e, g=g, j=j, pp=pp, kd=kd, qp=qp):
                            e.matmul(pp[:, 0:512], kd[0:64, g * 128:(g + 1) * 128],
                                     qp[0:64, j * TOKB:(j + 1) * TOKB],
                                     start=True, stop=True, tile_position=(0, 0))
                            return e.matmul(
                                pp[:, 512:1024], kd[64:128, g * 128:(g + 1) * 128],
                                qp[64:128, j * TOKB:(j + 1) * TOKB],
                                start=True, stop=True, tile_position=(64, 0))
                        scop = P.add("tensor", f, deps=[kld, qld],
                                     name=f"Bsc{p}_{j}_{ci}",
                                     reads=(R_kd, R_qp), writes=(Rpa, Rpb))

                        prs = ci % 2
                        R_pr = R_prb[prs]

                        def f(e, pp=pp, prs=prs):
                            return e.activation(
                                prbv[prs][:, :],
                                pp[:, 0:1024], AF.Exp, bias=negC[:, 0:1],
                                scale=0.125)
                        exop = P.add("scalar", f, deps=[scop],
                                     name=f"Bex{p}_{j}_{ci}",
                                     reads=(Rpa, Rpb, R_rconst), writes=(R_pr,))

                        gb = g * 65

                        def f(e, gb=gb, prs=prs, first=(ci == 0),
                              last=(ci == 31)):
                            pr = prbv[prs][:, :]
                            e.matmul(PP2[0:65, 0:512], vt[:, gb:gb + 65],
                                     pr[:, 0:512], start=first, stop=last)
                            return e.matmul(PP2[0:65, 512:1024],
                                            vt[:, gb:gb + 65],
                                            pr[:, 512:1024], start=first,
                                            stop=last)
                        avop = P.add("tensor", f, deps=[vld],
                                     name=f"Bav{p}_{j}_{ci}",
                                     reads=(R_pr, R_vt),
                                     writes=Rav if ci == 0 else ())
                        if ci > 0:
                            for rr in Rav:
                                rr.writers.append(avop)
                        last_av = avop

                    jj = p * NB + j
                    sg, Rsg = stg2[jj % 2]
                    if pend is not None:
                        pend()

                    def f(e, sg=sg):
                        return e.activation(sg[0:64, :], PP2[0:64, 0:512],
                                            AF.Copy)
                    cpA = P.add("scalar", f, deps=[last_av],
                                name=f"BcpA{p}_{j}", reads=Rav, writes=(Rsg,))

                    def f(e, sg=sg):
                        return e.activation(sg[64:128, :], PP2[0:64, 512:1024],
                                            AF.Copy)
                    cpB = P.add("scalar", f, deps=[last_av],
                                name=f"BcpB{p}_{j}", reads=Rav, writes=(Rsg,))

                    def f(e):
                        return e.activation(dnst[0:1, 0:512],
                                            PP2[64:65, 0:512], AF.Copy)
                    cpDA = P.add("scalar", f, deps=[last_av],
                                 name=f"BcpDa{p}_{j}", reads=Rav,
                                 writes=(R_dnt,))

                    def f(e):
                        return e.activation(dnst[0:1, 512:1024],
                                            PP2[64:65, 512:1024], AF.Copy)
                    cpDB = P.add("scalar", f, deps=[last_av],
                                 name=f"BcpDb{p}_{j}", reads=Rav,
                                 writes=(R_dnt,))

                    def f(e):
                        e.activation(dnst[:, :], dnst[:, :], AF.Ln)
                        e.drain()
                        return e.activation(dnst[:, :], dnst[:, :], AF.Exp,
                                            scale=-1.0)
                    dnop = P.add("scalar", f, deps=[cpDA, cpDB],
                                 name=f"Bdn2{p}_{j}", reads=(R_dnt,),
                                 writes=(R_dnt,))

                    def f(e):
                        e.tensor_copy(dnsb[:, 0:1024], dnst[:, :])
                        e.drain()
                        return e.tensor_tensor(dnsb[:, 1024:2048], dnst[:, :],
                                               dnsb[:, 0:1024], ALU.subtract)
                    hilo = P.add("vector", f, deps=[dnop],
                                 name=f"Bhl{p}_{j}", reads=(R_dnt,),
                                 writes=(R_dns,))

                    def mk_fin(p=p, j=j, sg=sg, Rsg=Rsg, hilo=hilo):
                        def fin():
                            pR, oR, RR = pslot(6)

                            def f(e, pR=pR, oR=oR):
                                e.matmul(pR[:, oR:oR + 512], selw[0:1, 0:128],
                                         dnsb[:, 0:512], start=True, stop=False)
                                e.matmul(pR[:, oR:oR + 512], selw[0:1, 128:256],
                                         dnsb[:, 512:1024], start=False,
                                         stop=False)
                                e.matmul(pR[:, oR:oR + 512], selw[0:1, 0:128],
                                         dnsb[:, 1024:1536], start=False,
                                         stop=False)
                                return e.matmul(pR[:, oR:oR + 512],
                                                selw[0:1, 128:256],
                                                dnsb[:, 1536:2048], start=False,
                                                stop=True)
                            rdop = P.add("tensor", f, deps=[hilo, op_selw],
                                         name=f"Brd2{p}_{j}", reads=(R_dns,),
                                         writes=(RR,))

                            def f(e, pR=pR, oR=oR):
                                return e.tensor_copy(rdbs[:, :],
                                                     pR[:, oR:oR + 512])
                            rcop = P.add("vector", f, name=f"Brc2{p}_{j}",
                                         reads=(RR,), writes=(R_rdbs,))

                            def f(e, p=p, j=j, sg=sg):
                                sl = slice(p * T + j * TOKB,
                                           p * T + (j + 1) * TOKB)
                                e.tensor_tensor(sg[:, :], sg[:, :],
                                                rdbs[:, :], ALU.mult)
                                e.drain()
                                return e.tensor_tensor(x[:, sl], x[:, sl],
                                                       sg[:, :], ALU.add)
                            P.add("vector", f, deps=[rcop],
                                  name=f"Bao{p}_{j}",
                                  reads=(Rsg, R_rdbs, R_x[p]),
                                  writes=(R_x[p], Rsg))
                        return fin
                    pend = mk_fin()
            if pend is not None:
                pend()

        if stages == "L":
            emit_ln(x, R_x, None, accb, R_acc, None, None, "L0")
            emit_tap("yA", accb, R_acc)
        if stages == "L3":
            emit_ln(x, R_x, None, accb, R_acc, None, None, "L0")
            def f(e):
                e.tensor_copy(accb[:, 0:512], aux_m[:, :])
                e.tensor_copy(accb[:, 512:1024], aux_v[:, :])
                e.tensor_copy(accb[:, 1024:1536], aux_r[:, :])
                e.tensor_copy(accb[:, 1536:2048].bitcast(BF16)[:, 0:512], xsqb[:, 0:512])
                e.tensor_copy(accb[:, 2048:2560].bitcast(BF16)[:, 0:512], xcpb[:, 0:512])
                return e.engine_nop()
            P.add("vector", f, deps=bar_deps(), name="dbg_aux",
                  writes=tuple(R_acc), reads=(R_aux,))
            emit_tap("yA", accb, R_acc)
        if stages == "L2":
            def f(e):
                for c in range(PC):
                    e.memset(accb[:, c * T:(c + 1) * T], 3.14)
                return e.engine_nop()
            mop_ = P.add("vector", f, name="dbg_memset", writes=tuple(R_acc))
            emit_tap("yA", accb, R_acc)
        if "A" in stages:
            emit_ffn_stage("A", first=True)
            emit_tap("x1", x, R_x)
        if "B" in stages:
            emit_stageB()
            emit_tap("x2", x, R_x)
        if "C" in stages:
            emit_ffn_stage("C", first=("A" not in stages))
            emit_tap("x3", x, R_x)

        def f(e):
            return [e.dma_start(out=dr["out_x"][c * 128:(c + 1) * 128, :],
                                in_=x[:, c * T:(c + 1) * T])
                    for c in range(PC)]
        outop = P.add("sync", f, kind="dma", name="xout",
                      reads=tuple(R_x), n_incs=PC, semkey="out")

        def f(e):
            return e.nop()
        P.add("sync", f, deps=[outop], name="terminal")

        P.emit(nc, block, es)
        rem = nc.sbuf_bytes_remaining
        nops = len(P.ops)
    return nc, {"sbuf_remaining": rem, "n_ops": nops}


def kernel(**inputs):
    nc, meta = build({"stages": "ABC", "taps": []})
    in_maps = shard_inputs(inputs)
    res = run_bass_kernel_spmd(nc, in_maps, core_ids=list(range(8)))
    return gather_output(res.results)
